# version 06 (final)
# baseline (speedup 1.0000x reference)
import numpy as np
import ml_dtypes
from contextlib import ExitStack

import concourse.bass as bass
import concourse.mybir as mybir
from concourse.bass_utils import run_bass_kernel_spmd
from concourse.bass import IndirectOffsetOnAxis

F32 = mybir.dt.float32
BF16 = mybir.dt.bfloat16
U32 = mybir.dt.uint32
AF = mybir.ActivationFunctionType
ALU = mybir.AluOpType
AX = mybir.AxisListType

NCORE = 8
S = 4096
D = 1024
NH = 16
DH = 128
DEXP = 2048
E = 64
EPC = 8
KTOP = 8
CAP = 1024
FF = 512
MOE_SCALE = 2.5
LN_EPS = 1e-5
NT = S // 128
NFT = D // 128
ZROW = E * CAP
RG = [list(range(NCORE))]
RSQ = float(1.0 / np.sqrt(128.0))


class Sem:
    def __init__(self, ctx, nc, name):
        self.h = ctx.enter_context(nc.semaphore(name))
        self.n = 0

    def inc(self, instr, k=1):
        instr.then_inc(self.h, k)
        self.n += k
        return instr


def _ceil(a, b):
    return (a + b - 1) // b


def build_nc(cfg):
    do_attn = cfg.get("attn", True)
    do_moe = cfg.get("moe", True)
    skip_ln = cfg.get("skip_ln", 0)
    nc = bass.Bass("TRN2", target_bir_lowering=False,
                   detect_race_conditions=not cfg.get("no_race", False))
    dtn = nc.dram_tensor

    x_in = dtn("x", [S, D], F32, kind="ExternalInput").ap()
    wqkv = dtn("wqkv", [NH, D, 384], F32, kind="ExternalInput").ap()
    wp_in = dtn("wp", [NH, D, DH], F32, kind="ExternalInput").ap()
    wy_in = dtn("wy", [D, NH], F32, kind="ExternalInput").ap()
    bqkv_in = dtn("bqkv", [128, 48], F32, kind="ExternalInput").ap()
    bp_in = dtn("bp", [128, NH], F32, kind="ExternalInput").ap()
    by_in = dtn("by", [NH, 1], F32, kind="ExternalInput").ap()
    wout_in = dtn("wout", [DEXP, D], F32, kind="ExternalInput").ap()
    lnw8_in = dtn("lnw8", [128, 8], F32, kind="ExternalInput").ap()
    lnb8_in = dtn("lnb8", [128, 8], F32, kind="ExternalInput").ap()
    olnw_in = dtn("olnw", [1, D], F32, kind="ExternalInput").ap()
    olnb_in = dtn("olnb", [1, D], F32, kind="ExternalInput").ap()
    smear_in = dtn("smear", [1, NH], F32, kind="ExternalInput").ap()
    lsc_in = dtn("lsc", [1, NH], F32, kind="ExternalInput").ap()
    ebias_in = dtn("ebias", [1, E], F32, kind="ExternalInput").ap()
    gwt_in = dtn("gwt", [D, E], F32, kind="ExternalInput").ap()
    sgt_in = dtn("sgt", [D, D], F32, kind="ExternalInput").ap()
    sut_in = dtn("sut", [D, D], F32, kind="ExternalInput").ap()
    sdt_in = dtn("sdt", [D, D], F32, kind="ExternalInput").ap()
    w1t_in = dtn("w1t", [EPC, D, FF], F32, kind="ExternalInput").ap()
    w3t_in = dtn("w3t", [EPC, D, FF], F32, kind="ExternalInput").ap()
    w2t_in = dtn("w2t", [EPC, FF, D], F32, kind="ExternalInput").ap()
    rmask_in = dtn("rmask", [8, 1], F32, kind="ExternalInput").ap()
    cbase_in = dtn("cbase", [1, 2], F32, kind="ExternalInput").ap()
    out = dtn("out", [S, D], F32, kind="ExternalOutput").ap()

    xn_rows = dtn("xn_rows", [S, D], BF16).ap()
    dbg_stat = dtn("dbg_stat", [NT * 128, 16], F32).ap()
    xht_sp = dtn("xht_sp", [128, NFT * S], BF16).ap()
    pre_d = dtn("pre_d", [S, D], F32).ap()
    gots = dtn("gots", [NT, 128, NH * DH], BF16).ap()
    dbg_att = dtn("dbg_att", [NT * 128, 1024], BF16).ap()
    dbg_f32 = dtn("dbg_f32", [NT * 128, 1024], F32).ap()
    if do_moe:
        xn_all = dtn("xn_all", [NCORE * S, D], BF16).ap()
        counts_in_d = dtn("counts_in_d", [1, E], F32).ap()
        counts_all = dtn("counts_all", [NCORE, E], F32).ap()
        tbl = dtn("tbl", [ZROW + 128, 2], F32).ap()
        tbl_mine = dtn("tbl_mine", [EPC * CAP, 2], F32).ap()
        ye_mine = dtn("ye_mine", [EPC * CAP, D], BF16).ap()
        ye_all = dtn("ye_all", [ZROW + 128, D], BF16).ap()

    with ExitStack() as top:
        et = top.enter_context

        s_ld = Sem(top, nc, "s_ld")
        s_sd = Sem(top, nc, "s_sd")
        s_gd = Sem(top, nc, "s_gd")
        s_st = Sem(top, nc, "s_st")
        s_pe = Sem(top, nc, "s_pe")
        s_v = Sem(top, nc, "s_v")
        s_a = Sem(top, nc, "s_a")
        s_g = Sem(top, nc, "s_g")
        s_cc = Sem(top, nc, "s_cc")
        s_sp = Sem(top, nc, "s_sp")
        s_se = Sem(top, nc, "s_se")
        s_eb = Sem(top, nc, "s_eb")
        s_ez = Sem(top, nc, "s_ez")
        s_tp = Sem(top, nc, "s_tp")
        s_te = Sem(top, nc, "s_te")
        s_pv = Sem(top, nc, "s_pv")
        s_ov = Sem(top, nc, "s_ov")
        s_go = Sem(top, nc, "s_go")
        s_wb = Sem(top, nc, "s_wb")
        s_c = Sem(top, nc, "s_c")
        s_x = [Sem(top, nc, f"s_x{i}") for i in range(2)]
        s_xnr = Sem(top, nc, "s_xnr")
        s_xsp = Sem(top, nc, "s_xsp")
        s_gw = Sem(top, nc, "s_gw")
        s_wy = Sem(top, nc, "s_wy")
        s_xr = [Sem(top, nc, f"s_xr{i}") for i in range(2)]
        s_cnt = Sem(top, nc, "s_cnt")
        s_call = Sem(top, nc, "s_call")
        s_fill = Sem(top, nc, "s_fill")
        s_scat = Sem(top, nc, "s_scat")
        s_sg = Sem(top, nc, "s_sg")
        s_sd2 = Sem(top, nc, "s_sd2")
        s_xnb = Sem(top, nc, "s_xnb")
        s_xre = [Sem(top, nc, f"s_xre{i}") for i in range(2)]
        s_pre = Sem(top, nc, "s_pre")
        s_tbl = Sem(top, nc, "s_tbl")
        s_w2 = [Sem(top, nc, f"s_w2{i}") for i in range(2)]
        s_xeg = Sem(top, nc, "s_xeg")
        s_yew = [Sem(top, nc, f"s_yew{i}") for i in range(2)]
        s_prem = [Sem(top, nc, f"s_prem{i}") for i in range(2)]
        s_xhb = Sem(top, nc, "s_xhb")
        s_wq = [Sem(top, nc, f"s_wq{i}") for i in range(2)]
        s_wp = Sem(top, nc, "s_wp")
        s_ph3 = Sem(top, nc, "s_ph3")
        s_got = Sem(top, nc, "s_got")
        s_p4l = [Sem(top, nc, f"s_p4l{i}") for i in range(2)]
        s_wo = Sem(top, nc, "s_wo")
        ALLS = [s_ld, s_sd, s_gd, s_st, s_pe, s_v, s_a, s_g, s_cc,
                s_c, s_x[0], s_x[1], s_xnr, s_xsp, s_gw, s_wy, s_xr[0], s_xr[1],
                s_cnt, s_call, s_fill, s_scat, s_sg, s_sd2, s_xnb,
                s_xre[0], s_xre[1], s_pre, s_tbl, s_w2[0], s_w2[1], s_xeg,
                s_yew[0], s_yew[1], s_prem[0], s_prem[1], s_xhb,
                s_wq[0], s_wq[1], s_wp, s_ph3, s_got, s_p4l[0], s_p4l[1], s_wo]

        route_stack = ExitStack()
        p0_stack = ExitStack()
        y_stack = ExitStack()
        p1_stack = ExitStack()
        p12_stack = ExitStack()
        p2_stack = ExitStack()
        p3_stack = ExitStack()
        p4_stack = ExitStack()
        er = route_stack.enter_context
        e0 = p0_stack.enter_context
        ey = y_stack.enter_context
        e1 = p1_stack.enter_context
        e12 = p12_stack.enter_context
        e2 = p2_stack.enter_context
        e3 = p3_stack.enter_context
        e4 = p4_stack.enter_context
        top.callback(route_stack.close)
        top.callback(p0_stack.close)
        top.callback(y_stack.close)
        top.callback(p1_stack.close)
        top.callback(p12_stack.close)
        top.callback(p2_stack.close)
        top.callback(p3_stack.close)
        top.callback(p4_stack.close)

        PB = [et(nc.psum_tensor(f"PB{i}", [128, 512], F32)) for i in range(4)]
        PBT = et(nc.psum_tensor("PBT", [128, 1024], BF16))
        PBT2 = et(nc.psum_tensor("PBT2", [128, 1024], BF16))
        PBO = et(nc.psum_tensor("PBO", [128, 512], F32))
        PBM = et(nc.psum_tensor("PBM", [128, 512], F32))

        IDENT = et(nc.sbuf_tensor("IDENT", [128, 128], BF16))
        IDENTF = et(nc.sbuf_tensor("IDENTF", [128, 128], F32))
        ONESROW = et(nc.sbuf_tensor("ONESROW", [1, 128], F32))
        ONESCOLB = et(nc.sbuf_tensor("ONESCOLB", [128, 1], BF16))
        ONES3B = et(nc.sbuf_tensor("ONES3B", [3, 128], BF16))
        IOTAP = et(nc.sbuf_tensor("IOTAP", [128, 1], F32))
        IOTA64 = et(nc.sbuf_tensor("IOTA64", [128, 64], F32))
        EXPOFF = et(nc.sbuf_tensor("EXPOFF", [128, 64], F32))
        DUMPC = et(nc.sbuf_tensor("DUMPC", [128, 1], F32))
        UT128 = et(nc.sbuf_tensor("UT128", [128, 128], BF16))
        UTNEG = et(nc.sbuf_tensor("UTNEG", [128, 128], BF16))
        HC = et(nc.sbuf_tensor("HC", [128, 64], F32))
        EBREP = et(nc.sbuf_tensor("EBREP", [128, 64], F32))
        OLWR = et(nc.sbuf_tensor("OLWR", [128, D], F32))
        OLBR = et(nc.sbuf_tensor("OLBR", [128, D], F32))
        LNW = et(nc.sbuf_tensor("LNW", [128, 8], F32))
        LNB = et(nc.sbuf_tensor("LNB", [128, 8], F32))
        LNBB = et(nc.sbuf_tensor("LNBB", [128, 8], BF16))
        BQKV = et(nc.sbuf_tensor("BQKV", [128, 48], F32))
        BPS = et(nc.sbuf_tensor("BPS", [128, NH], F32))
        BYS = et(nc.sbuf_tensor("BYS", [NH, 1], F32))
        SMS = et(nc.sbuf_tensor("SMS", [1, 64], F32))
        RMS = et(nc.sbuf_tensor("RMS", [8, 1], F32))
        CBS = et(nc.sbuf_tensor("CBS", [1, 2], F32))
        POS3B = et(nc.sbuf_tensor("POS3B", [80, S], BF16))
        SLOTS = et(nc.sbuf_tensor("SLOTS", [128, NT * KTOP], F32))
        SLU32 = et(nc.sbuf_tensor("SLU32", [128, NT * KTOP], U32))
        BASER = et(nc.sbuf_tensor("BASER", [128, E], F32))
        ZTILE = et(nc.sbuf_tensor("ZTILE", [128, 1024], F32))
        FILLT = et(nc.sbuf_tensor("FILLT", [128, 128], F32))
        EPSC = et(nc.sbuf_tensor("EPSC", [128, 1], F32))
        GWT = er(nc.sbuf_tensor("GWT", [128, NFT * E], F32))
        LOGIT = er(nc.sbuf_tensor("LOGIT", [128, NT * E], F32))
        WNALL = er(nc.sbuf_tensor("WNALL", [128, NT * KTOP], F32))
        E8ALL = er(nc.sbuf_tensor("E8ALL", [128, NT * KTOP], F32))
        PREFB = er(nc.sbuf_tensor("PREFB", [128, NT * E], F32))
        SFC = er(nc.sbuf_tensor("SFC", [128, 64], F32))
        G8S = er(nc.sbuf_tensor("G8S", [128, 64], F32))
        GSUM = er(nc.sbuf_tensor("GSUM", [128, 8], F32))
        T4B = er(nc.sbuf_tensor("T4B", [128, 8], F32))
        GIDX = er(nc.sbuf_tensor("GIDX", [128, 8], U32))
        GIDF = er(nc.sbuf_tensor("GIDF", [128, 8], F32))
        GMB = er(nc.sbuf_tensor("GMB", [128, 8], F32))
        TMPR = er(nc.sbuf_tensor("TMPR", [128, 64], F32))
        V8B = er(nc.sbuf_tensor("V8B", [128, 8], F32))
        E8U = er(nc.sbuf_tensor("E8U", [128, 8], U32))
        OHK = er(nc.sbuf_tensor("OHK", [128, 64], F32))
        W8B = er(nc.sbuf_tensor("W8B", [128, 8], F32))
        S01 = er(nc.sbuf_tensor("S01", [128, 64], F32))
        S01B = er(nc.sbuf_tensor("S01B", [128, 64], BF16))
        RC8 = er(nc.sbuf_tensor("RC8", [128, 1], F32))
        CARRY = er(nc.sbuf_tensor("CARRY", [1, 64], F32))
        CARRB = er(nc.sbuf_tensor("CARRB", [1, 64], BF16))
        CNTROW = er(nc.sbuf_tensor("CNTROW", [1, 64], F32))
        CALL = er(nc.sbuf_tensor("CALL", [8, 64], F32))
        CALLB = er(nc.sbuf_tensor("CALLB", [8, 64], BF16))
        RMSB = er(nc.sbuf_tensor("RMSB", [8, 1], BF16))
        SCV = er(nc.sbuf_tensor("SCV", [128, NT * 16], F32))
        FLAT = er(nc.sbuf_tensor("FLAT", [128, 64], F32))
        VBM = er(nc.sbuf_tensor("VBM", [128, 64], F32))
        TOKC = er(nc.sbuf_tensor("TOKC", [128, 1], F32))

        blk = et(nc.Block())

        def T(f):
            blk.tensor(f)

        def V(f):
            blk.vector(f)

        def SC(f):
            blk.scalar(f)

        def SY(f):
            blk.sync(f)

        def GP(f):
            blk.gpsimd(f)

        def wait_all(eng, sems=None):
            for s in (sems or ALLS):
                if s.n:
                    eng.wait_ge(s.h, s.n)

        def barrier():
            for em in (T, V, SC, SY, GP):
                def _f(eng):
                    wait_all(eng)
                em(_f)

        def barrier_except(skip):
            sems = [s for s in ALLS if s not in skip]
            for em in (T, V, SC, SY, GP):
                def _f(eng, sems=sems):
                    wait_all(eng, sems)
                em(_f)

        def _sy(sy):
            for dst, src in [(LNW[:], lnw8_in), (LNB[:], lnb8_in),
                             (BQKV[:], bqkv_in), (BPS[:], bp_in), (BYS[:], by_in),
                             (SMS[0:1, 0:NH], smear_in),
                             (SMS[0:1, 16:32], lsc_in),
                             (SMS[0:1, 32:48], smear_in),
                             (EBREP[0:1, 0:E], ebias_in),
                             (RMS[:], rmask_in), (CBS[:], cbase_in),
                             (OLWR[0:1, :], olnw_in), (OLBR[0:1, :], olnb_in)]:
                s_c.inc(sy.dma_start(out=dst, in_=src), 16)
        SY(_sy)
        ld_const = s_c.n

        REGS = {}

        def _gp(gp):
            REGS["btbl"] = gp.alloc_register("btbl")
            gp.reg_mov(REGS["btbl"], ZROW + 127)
            REGS["bxn"] = gp.alloc_register("bxn")
            gp.reg_mov(REGS["bxn"], NCORE * S - 1)
            nc.gpsimd.memset(IDENT[:], 1.0)
            nc.gpsimd.affine_select(out=IDENT[:], in_=IDENT[:], compare_op=ALU.is_equal,
                                    fill=0.0, base=0, pattern=[[-1, 128]],
                                    channel_multiplier=1)
            nc.gpsimd.memset(IDENTF[:], 1.0)
            nc.gpsimd.affine_select(out=IDENTF[:], in_=IDENTF[:], compare_op=ALU.is_equal,
                                    fill=0.0, base=0, pattern=[[-1, 128]],
                                    channel_multiplier=1)
            nc.gpsimd.memset(ONESROW[:], 1.0)
            nc.gpsimd.memset(ONESCOLB[:], 1.0)
            nc.gpsimd.memset(ONES3B[:], 1.0)
            nc.gpsimd.iota(IOTAP[:], pattern=[[0, 1]], base=0, channel_multiplier=1, allow_small_or_imprecise_dtypes=True)
            nc.gpsimd.iota(IOTA64[:], pattern=[[1, 64]], base=0, channel_multiplier=0, allow_small_or_imprecise_dtypes=True)
            nc.gpsimd.iota(EXPOFF[:], pattern=[[1024, 64]], base=0, channel_multiplier=0, allow_small_or_imprecise_dtypes=True)
            nc.gpsimd.iota(DUMPC[:], pattern=[[0, 1]], base=ZROW, channel_multiplier=1, allow_small_or_imprecise_dtypes=True)
            nc.gpsimd.memset(UT128[:], 1.0)
            nc.gpsimd.affine_select(
                out=UT128[:], in_=UT128[:], compare_op=ALU.is_gt, fill=0.0,
                base=0, pattern=[[1, 128]], channel_multiplier=-1)
            nc.gpsimd.memset(UTNEG[:], -1.0e10)
            nc.gpsimd.affine_select(
                out=UTNEG[:], in_=UTNEG[:], compare_op=ALU.is_gt, fill=0.0,
                base=0, pattern=[[1, 128]], channel_multiplier=-1)
            nc.gpsimd.memset(ZTILE[:], 0.0)
            nc.gpsimd.memset(SMS[0:1, 48:64], 0.0)
            nc.gpsimd.memset(EPSC[:], LN_EPS)
            s_g.inc(nc.gpsimd.memset(FILLT[:], 1.0e9))
        GP(_gp)

        def _sc(sc):
            sc.wait_ge(s_c.h, ld_const)
            nc.scalar.activation(SMS[0:1, 32:48], SMS[0:1, 32:48], AF.Sigmoid,
                                 bias=0.0, scale=-1.0)
            nc.scalar.activation(SMS[0:1, 0:16], SMS[0:1, 0:16], AF.Sigmoid)
            nc.scalar.activation(SMS[0:1, 16:32], SMS[0:1, 16:32], AF.Exp,
                                 bias=0.0, scale=-2.0)
            s_a.inc(sc.drain())
        SC(_sc)

        def _v(v):
            v.wait_ge(s_a.h, s_a.n)
            s_v.inc(nc.vector.tensor_scalar_mul(SMS[0:1, 16:32], SMS[0:1, 16:32], RSQ))
        V(_v)

        def _t2(t):
            wait_all(t, [s_v, s_g, s_c])
            _li = nc.tensor.matmul(PBM[:, 0:64], ONESROW[:], SMS[0:1, 0:64],
                             start=True, stop=True)
            _li = nc.tensor.matmul(PBM[:, 64:128], ONESROW[:], EBREP[0:1, 0:E],
                             start=True, stop=True)
            _li = nc.tensor.matmul(PBM[:, 128:130], ONESROW[:], CBS[0:1, :],
                             start=True, stop=True)
            _li = nc.tensor.matmul(PB[0][:, 0:512], ONESROW[:], OLWR[0:1, 0:512],
                             start=True, stop=True)
            _li = nc.tensor.matmul(PB[1][:, 0:512], ONESROW[:], OLWR[0:1, 512:1024],
                             start=True, stop=True)
            _li = nc.tensor.matmul(PB[2][:, 0:512], ONESROW[:], OLBR[0:1, 0:512],
                             start=True, stop=True)
            _li = nc.tensor.matmul(PB[3][:, 0:512], ONESROW[:], OLBR[0:1, 512:1024],
                             start=True, stop=True)
            s_pe.inc(_li)
        T(_t2)
        pe_const = s_pe.n

        def _v2(v):
            v.wait_ge(s_pe.h, pe_const)
            nc.vector.tensor_copy(HC[:, 0:50], PBM[:, 0:50])
            nc.vector.tensor_copy(HC[:, 48:50], PBM[:, 128:130])
            nc.vector.tensor_copy(LNBB[:], LNB[:])
            nc.vector.tensor_copy(EBREP[:], PBM[:, 64:128])
            nc.vector.tensor_copy(OLWR[:, 0:512], PB[0][:, 0:512])
            nc.vector.tensor_copy(OLWR[:, 512:1024], PB[1][:, 0:512])
            nc.vector.tensor_copy(OLBR[:, 0:512], PB[2][:, 0:512])
            s_v.inc(nc.vector.tensor_copy(OLBR[:, 512:1024], PB[3][:, 0:512]))
        V(_v2)

        barrier()

        WYB = ey(nc.sbuf_tensor("WYB", [128, NFT * NH], BF16))
        YT = ey(nc.sbuf_tensor("YT", [NH, S], F32))
        MIDF = ey(nc.sbuf_tensor("MIDF", [NH, S], F32))
        ONES16 = ey(nc.sbuf_tensor("ONES16", [NH, S], F32))

        XT = [e0(nc.sbuf_tensor(f"XT{i}", [128, D], F32)) for i in range(2)]
        XN = e0(nc.sbuf_tensor("XN", [128, D], F32))
        XH = e0(nc.sbuf_tensor("XH", [128, D], F32))
        XNB = e0(nc.sbuf_tensor("XNB", [128, D], BF16))
        XHTB = e0(nc.sbuf_tensor("XHTB", [128, NFT * 128], BF16))
        XNTF = e0(nc.sbuf_tensor("XNTF", [128, NFT * 128], F32))
        STAT = e0(nc.sbuf_tensor("STAT", [128, 16], F32))

        def _syg(sy):
            s_gw.inc(sy.dma_start(
                out=GWT[:].rearrange("p (a e) -> p a e", a=NFT), in_=gwt_in.rearrange("(a p) e -> p a e", p=128)), 16)
        SY(_syg)

        def _gpy0(gp):
            gp.wait_ge(s_c.h, ld_const)
            s_wy.inc(gp.dma_start(out=WYB[:].rearrange("p (a h) -> p a h", a=NFT),
                                  in_=wy_in.rearrange("(a p) h -> p a h", p=128)), 16)
        GP(_gpy0)
        gd_wy = s_wy.n

        g = {"v_done": {}, "pe_tr": {}, "ev": {}, "sd_sp": {}, "ya": {}, "sg": {}}
        for ti in range(NT):
            xt = XT[ti % 2]

            def _sy(sy, ti=ti, xt=xt):
                if ti >= 2:
                    sy.wait_ge(s_v.h, g["v_done"][ti - 2])
                s_x[ti % 2].inc(sy.dma_start(out=xt[:], in_=x_in[ti * 128:(ti + 1) * 128, :]), 16)
            SY(_sy)
            ld_here = s_x[ti % 2].n

            def _v(v, ti=ti, xt=xt, ld_here=ld_here):
                v.wait_ge(s_x[ti % 2].h, ld_here)
                if ti >= 1:
                    v.wait_ge(s_pe.h, g["pe_tr"][ti - 1])
                nc.vector.tensor_tensor(XN[:], xt[:], xt[:], ALU.mult)
                nc.vector.reduce_sum(STAT[:, 0:1], xt[:], axis=AX.X)
                v.drain()
                nc.vector.reduce_sum(STAT[:, 1:2], XN[:], axis=AX.X)
                nc.vector.tensor_scalar_mul(STAT[:, 0:1], STAT[:, 0:1], 1.0 / D)
                v.drain()
                nc.vector.tensor_scalar_mul(STAT[:, 1:2], STAT[:, 1:2], 1.0 / D)
                nc.vector.tensor_tensor(STAT[:, 2:3], STAT[:, 0:1], STAT[:, 0:1], ALU.mult)
                v.drain()
                s_v.inc(nc.vector.tensor_tensor(STAT[:, 1:2], STAT[:, 1:2],
                                                STAT[:, 2:3], ALU.subtract))
            V(_v)

            def _sc(sc, ti=ti):
                sc.wait_ge(s_v.h, s_v.n)
                nc.scalar.activation(STAT[:, 2:3], STAT[:, 1:2], AF.Ln, bias=EPSC[:, 0:1])
                sc.drain()
                _li = nc.scalar.activation(STAT[:, 2:3], STAT[:, 2:3], AF.Exp,
                                           bias=0.0, scale=-0.5)
                s_a.inc(_li)
            SC(_sc)

            def _v2(v, ti=ti):
                v.wait_ge(s_a.h, s_a.n)
                nc.vector.tensor_tensor(STAT[:, 3:4], STAT[:, 0:1], STAT[:, 2:3], ALU.mult)
                v.drain()
                s_v.inc(nc.vector.tensor_scalar_mul(STAT[:, 3:4], STAT[:, 3:4], -1.0))
            V(_v2)

            def _sc2(sc, ti=ti, xt=xt):
                sc.wait_ge(s_v.h, s_v.n)
                if skip_ln == 1:
                    _li = nc.scalar.copy(XN[:], xt[:])
                else:
                    _li = nc.scalar.activation(XN[:], xt[:], AF.Identity,
                                               bias=STAT[:, 3:4], scale=STAT[:, 2:3])
                s_a.inc(_li)
            SC(_sc2)

            def _v3(v, ti=ti):
                v.wait_ge(s_a.h, s_a.n)
                nc.vector.tensor_tensor(XH[:], XN[:], XN[:], ALU.mult)
                nc.vector.reduce_sum(STAT[:, 4:5], XN[:], axis=AX.X)
                v.drain()
                nc.vector.reduce_sum(STAT[:, 5:6], XH[:], axis=AX.X)
                nc.vector.tensor_scalar_mul(STAT[:, 4:5], STAT[:, 4:5], 1.0 / D)
                v.drain()
                nc.vector.tensor_scalar_mul(STAT[:, 5:6], STAT[:, 5:6], 1.0 / D)
                nc.vector.tensor_tensor(STAT[:, 6:7], STAT[:, 4:5], STAT[:, 4:5], ALU.mult)
                v.drain()
                s_v.inc(nc.vector.tensor_tensor(STAT[:, 5:6], STAT[:, 5:6],
                                                STAT[:, 6:7], ALU.subtract))
            V(_v3)

            def _sc3(sc, ti=ti):
                sc.wait_ge(s_v.h, s_v.n)
                nc.scalar.activation(STAT[:, 6:7], STAT[:, 5:6], AF.Ln, bias=EPSC[:, 0:1])
                sc.drain()
                _li = nc.scalar.activation(STAT[:, 6:7], STAT[:, 6:7], AF.Exp,
                                           bias=0.0, scale=-0.5)
                s_a.inc(_li)
            SC(_sc3)

            def _v4(v, ti=ti):
                v.wait_ge(s_a.h, s_a.n)
                nc.vector.tensor_tensor(STAT[:, 7:8], STAT[:, 4:5], STAT[:, 6:7], ALU.mult)
                v.drain()
                s_v.inc(nc.vector.tensor_scalar_mul(STAT[:, 7:8], STAT[:, 7:8], -1.0))
            V(_v4)

            def _sc4(sc, ti=ti):
                sc.wait_ge(s_v.h, s_v.n)
                _li = nc.scalar.activation(XH[:], XN[:], AF.Identity,
                                           bias=STAT[:, 7:8], scale=STAT[:, 6:7])
                s_a.inc(_li)
            SC(_sc4)
            a_ln = s_a.n

            def _v5(v, ti=ti, a_ln=a_ln):
                v.wait_ge(s_a.h, a_ln)
                s_v.inc(nc.vector.tensor_copy(XNB[:], XN[:]))
            V(_v5)
            v_xnb = s_v.n

            if do_moe:
                def _sy2(sy, ti=ti, v_xnb=v_xnb):
                    sy.wait_ge(s_v.h, v_xnb)
                    s_xnr.inc(sy.dma_start(out=xn_rows[ti * 128:(ti + 1) * 128, :],
                                           in_=XNB[:]), 16)
                    if cfg.get("dbg_stat"):
                        s_xnr.inc(sy.dma_start(
                            out=dbg_stat[ti * 128:(ti + 1) * 128, :],
                            in_=STAT[:]), 16)
                SY(_sy2)

            for half in range(2):
                def _t(t, ti=ti, half=half, a_ln=a_ln):
                    t.wait_ge(s_a.h, a_ln)
                    key = (ti, half - 1) if half else (ti - 1, 1)
                    if key in g["ev"]:
                        t.wait_ge(s_v.h, g["ev"][key])
                    for j in range(4):
                        jj = half * 4 + j
                        _li = nc.tensor.transpose(PB[2][:, j * 128:(j + 1) * 128],
                                            XN[:, jj * 128:(jj + 1) * 128], IDENTF[:])
                        _li = nc.tensor.transpose(PB[3][:, j * 128:(j + 1) * 128],
                                            XH[:, jj * 128:(jj + 1) * 128], IDENTF[:])
                    s_pe.inc(_li)
                T(_t)
                pe_here = s_pe.n

                def _v6(v, ti=ti, half=half, pe_here=pe_here):
                    v.wait_ge(s_pe.h, pe_here)
                    if (ti - 1, 1) in g["sd_sp"]:
                        v.wait_ge(s_xsp.h, g["sd_sp"][(ti - 1, 1)])
                    nc.vector.tensor_copy(XNTF[:, half * 512:(half + 1) * 512],
                                          PB[2][:, 0:512])
                    s_v.inc(nc.vector.tensor_copy(XHTB[:, half * 512:(half + 1) * 512],
                                                  PB[3][:, 0:512]))
                V(_v6)
                g["ev"][(ti, half)] = s_v.n
            g["pe_tr"][ti] = s_pe.n

            def _sy3(sy, ti=ti):
                sy.wait_ge(s_v.h, g["ev"][(ti, 1)])
                for j in range(NFT):
                    s_xsp.inc(sy.dma_start(
                        out=xht_sp[:, j * S + ti * 128: j * S + (ti + 1) * 128],
                        in_=XHTB[:, j * 128:(j + 1) * 128]), 16)
            SY(_sy3)
            g["sd_sp"][(ti, 1)] = s_xsp.n

            def _t3(t, ti=ti):
                t.wait_ge(s_v.h, g["ev"][(ti, 1)])
                if ti == 0:
                    t.wait_ge(s_gw.h, s_gw.n)
                if do_moe:
                    if ti >= 1:
                        t.wait_ge(s_a.h, g["sg"][ti - 1])
                    for j in range(NFT):
                        _li = nc.tensor.matmul(PBO[:, 0:64],
                                         XNTF[:, j * 128:(j + 1) * 128],
                                         GWT[:, j * E:(j + 1) * E],
                                         start=(j == 0), stop=(j == NFT - 1))
                    s_pe.inc(_li)
                else:
                    s_pe.inc(t.nop())
            T(_t3)
            pe_lg = s_pe.n

            if do_moe:
                def _scl(sc, ti=ti, pe_lg=pe_lg):
                    sc.wait_ge(s_pe.h, pe_lg)
                    _li = nc.scalar.activation(LOGIT[:, ti * E:(ti + 1) * E],
                                               PBO[:, 0:64], AF.Sigmoid)
                    s_a.inc(_li)
                SC(_scl)
                g["sg"][ti] = s_a.n
                a_sg = s_a.n

                if ti == 0:
                    def _vr0(v):
                        nc.vector.memset(CARRY[:], 0.0)
                        v.drain()
                        s_v.inc(nc.vector.tensor_copy(CARRB[:], CARRY[:]))
                    V(_vr0)

                def _vr(v, ti=ti, a_sg=a_sg):
                    v.wait_ge(s_a.h, a_sg)
                    sc0 = LOGIT[:, ti * E:(ti + 1) * E]
                    nc.vector.tensor_tensor(SFC[:], sc0, EBREP[:], ALU.add)
                    v.drain()
                    for gg in range(8):
                        nc.vector.max(out=G8S[:, gg * 8:(gg + 1) * 8],
                                      in_=SFC[:, gg * 8:(gg + 1) * 8])
                    v.drain()
                    a0 = G8S[:].rearrange("p (g e) -> p g e", g=8)
                    nc.vector.tensor_tensor(GSUM[:], a0[:, :, 0], a0[:, :, 1], ALU.add)
                    v.drain()
                    s_v.inc(nc.vector.max(out=T4B[:], in_=GSUM[:]))
                    v.wait_ge(s_v.h, s_v.n)
                    nc.vector.max_index(out=GIDX[:], in_max=T4B[:], in_values=GSUM[:])
                    v.drain()
                    nc.vector.tensor_copy(GIDF[:], GIDX[:])
                    nc.vector.memset(GMB[:], 0.0)
                    v.drain()
                    for kk in range(4):
                        nc.vector.tensor_scalar(OHK[:, 0:8], IOTA64[:, 0:8],
                                                GIDF[:, kk:kk + 1], None, ALU.is_equal)
                        v.drain()
                        nc.vector.tensor_tensor(GMB[:], GMB[:], OHK[:, 0:8], ALU.add)
                        v.drain()
                    for gg in range(8):
                        nc.vector.tensor_scalar(TMPR[:, gg * 8:(gg + 1) * 8],
                                                SFC[:, gg * 8:(gg + 1) * 8],
                                                GMB[:, gg:gg + 1], None, ALU.mult)
                    v.drain()
                    s_v.inc(nc.vector.max(out=V8B[:], in_=TMPR[:]))
                    v.wait_ge(s_v.h, s_v.n)
                    nc.vector.max_index(out=E8U[:], in_max=V8B[:], in_values=TMPR[:])
                    v.drain()
                    nc.vector.tensor_copy(E8ALL[:, ti * 8:(ti + 1) * 8], E8U[:])
                    nc.vector.memset(S01[:], 0.0)
                    v.drain()
                    for kk in range(KTOP):
                        nc.vector.tensor_scalar(OHK[:], IOTA64[:],
                                                E8ALL[:, ti * 8 + kk: ti * 8 + kk + 1],
                                                None, ALU.is_equal)
                        v.drain()
                        nc.vector.tensor_tensor(S01[:], S01[:], OHK[:], ALU.add)
                        nc.vector.tensor_tensor(TMPR[:], sc0, OHK[:], ALU.mult)
                        v.drain()
                        nc.vector.reduce_sum(W8B[:, kk:kk + 1], TMPR[:], axis=AX.X)
                        v.drain()
                    nc.vector.reduce_sum(RC8[:], W8B[:], axis=AX.X)
                    v.drain()
                    nc.vector.tensor_scalar(RC8[:], RC8[:], 1e-20, None, ALU.add)
                    v.drain()
                    nc.vector.reciprocal(RC8[:], RC8[:])
                    v.drain()
                    nc.vector.tensor_scalar(WNALL[:, ti * 8:(ti + 1) * 8], W8B[:],
                                            RC8[:, 0:1], MOE_SCALE, ALU.mult, ALU.mult)
                    s_v.inc(nc.vector.tensor_copy(S01B[:], S01[:]))
                V(_vr)
                v_s01 = s_v.n

                def _tr(t, ti=ti, v_s01=v_s01):
                    t.wait_ge(s_v.h, v_s01)
                    _li = nc.tensor.matmul(PB[0][:, 0:64], UT128[:], S01B[:],
                                     start=True, stop=False)
                    _li = nc.tensor.matmul(PB[0][:, 0:64], ONES3B[0:1, :], CARRB[0:1, :],
                                     start=False, stop=True)
                    _li = nc.tensor.matmul(PB[1][0:1, 0:64], ONESCOLB[:], S01B[:],
                                     start=True, stop=True)
                    s_pe.inc(_li)
                T(_tr)
                pe_pref = s_pe.n

                def _vpr(v, ti=ti, pe_pref=pe_pref):
                    v.wait_ge(s_pe.h, pe_pref)
                    nc.vector.tensor_copy(PREFB[:, ti * E:(ti + 1) * E],
                                          PB[0][:, 0:64])
                    nc.vector.tensor_copy(CNTROW[:], PB[1][0:1, 0:64])
                    v.drain()
                    nc.vector.tensor_tensor(CARRY[:], CARRY[:], CNTROW[:], ALU.add)
                    v.drain()
                    s_v.inc(nc.vector.tensor_copy(CARRB[:], CARRY[:]))
                V(_vpr)
                g["v_done"][ti] = s_v.n
            else:
                def _v7(v, ti=ti, pe_lg=pe_lg):
                    v.wait_ge(s_pe.h, pe_lg)
                    s_v.inc(nc.vector.engine_nop())
                V(_v7)
                g["v_done"][ti] = s_v.n

            def _t3y(t, ti=ti):
                if ti == 0:
                    t.wait_ge(s_wy.h, gd_wy)
                else:
                    t.wait_ge(s_a.h, g["ya"][ti - 1])
                for j in range(NFT):
                    _li = nc.tensor.matmul(PBM[0:NH, 0:128],
                                     WYB[:, j * NH:(j + 1) * NH],
                                     XHTB[:, j * 128:(j + 1) * 128],
                                     start=(j == 0), stop=(j == NFT - 1))
                s_pe.inc(_li)
            T(_t3y)
            pe_y0 = s_pe.n

            def _scy2(sc, ti=ti, pe_y0=pe_y0):
                sc.wait_ge(s_pe.h, pe_y0)
                _li = nc.scalar.activation(YT[0:NH, ti * 128:(ti + 1) * 128],
                                           PBM[0:NH, 0:128], AF.Sigmoid,
                                           bias=BYS[0:NH, 0:1])
                s_a.inc(_li)
            SC(_scy2)
            g["ya"][ti] = s_a.n

        barrier()

        if do_moe:
            def _gp2(gp):
                gp.wait_ge(s_xnr.h, s_xnr.n)
                s_cc.inc(nc.gpsimd.collective_compute(
                    "AllGather", ALU.bypass, replica_groups=RG,
                    ins=[xn_rows.opt()], outs=[xn_all.opt()]))
            GP(_gp2)
        cc_xn = s_cc.n

        def _gpy(gp):
            nc.gpsimd.memset(ONES16[:], 1.0)
            s_g.inc(nc.gpsimd.memset(POS3B[:], 0.0))
        GP(_gpy)
        g_ones = s_g.n

        a_sig = g["ya"][NT - 1]

        def _vp(v):
            v.wait_ge(s_a.h, a_sig)
            v.wait_ge(s_g.h, g_ones)
            nc.vector.tensor_tensor_scan(out=YT[:], data0=ONES16[:], data1=YT[:],
                                         initial=0.0, op0=ALU.mult, op1=ALU.add)
            v.drain()
            nc.vector.tensor_copy(POS3B[0:16, :], YT[:])
            v.drain()
            nc.vector.tensor_copy(MIDF[:], POS3B[0:16, :])
            v.drain()
            nc.vector.tensor_tensor(MIDF[:], YT[:], MIDF[:], ALU.subtract)
            v.drain()
            nc.vector.tensor_copy(POS3B[32:48, :], MIDF[:])
            v.drain()
            nc.vector.tensor_copy(ONES16[:], POS3B[32:48, :])
            v.drain()
            nc.vector.tensor_tensor(MIDF[:], MIDF[:], ONES16[:], ALU.subtract)
            v.drain()
            s_v.inc(nc.vector.tensor_copy(POS3B[64:80, :], MIDF[:]))
        V(_vp)
        v_pos3 = s_v.n

        barrier_except({s_cc})
        p0_stack.close()
        y_stack.close()

        if do_moe:
            v_route = g["v_done"][NT - 1]

            def _syc(sy):
                sy.wait_ge(s_v.h, v_route)
                s_cnt.inc(sy.dma_start(out=counts_in_d, in_=CARRY[:]), 16)
            SY(_syc)
            ld_cnt = s_cnt.n

            def _gpc(gp):
                gp.wait_ge(s_cnt.h, ld_cnt)
                s_cc.inc(nc.gpsimd.collective_compute(
                    "AllGather", ALU.bypass, replica_groups=RG,
                    ins=[counts_in_d.opt()], outs=[counts_all.opt()]))
                for i in range(8):
                    s_fill.inc(gp.dma_start(
                        out=tbl[i * 8192:(i + 1) * 8192, :]
                            .rearrange("(p a) two -> p (a two)", p=128),
                        in_=FILLT[0:128, 0:128]), 16)
            GP(_gpc)
            cc_cnt = s_cc.n
            gd_fill = s_fill.n

            def _syc2(sy):
                sy.wait_ge(s_cc.h, cc_cnt)
                s_call.inc(sy.dma_start(out=CALL[:], in_=counts_all), 16)
            SY(_syc2)
            ld_call = s_call.n

            def _vb(v):
                v.wait_ge(s_call.h, ld_call)
                nc.vector.tensor_copy(CALLB[:], CALL[:])
                s_v.inc(nc.vector.tensor_copy(RMSB[:], RMS[:]))
            V(_vb)
            v_call = s_v.n

            def _tb(t):
                t.wait_ge(s_v.h, v_call)
                _li = nc.tensor.matmul(PBM[0:1, 64:128], RMSB[0:8, 0:1], CALLB[:],
                                 start=True, stop=True)
                s_pe.inc(_li)
            T(_tb)
            pe_base = s_pe.n

            def _vb2(v):
                v.wait_ge(s_pe.h, pe_base)
                s_v.inc(nc.vector.tensor_copy(CNTROW[:], PBM[0:1, 64:128]))
            V(_vb2)
            v_b2 = s_v.n

            def _tb2(t):
                t.wait_ge(s_v.h, v_b2)
                _li = nc.tensor.matmul(PBM[:, 128:192], ONESROW[:], CNTROW[0:1, :],
                                 start=True, stop=True)
                s_pe.inc(_li)
            T(_tb2)
            pe_base2 = s_pe.n

            def _vb3(v):
                v.wait_ge(s_pe.h, pe_base2)
                s_v.inc(nc.vector.tensor_copy(BASER[:], PBM[:, 128:192]))
            V(_vb3)

            for ti in range(NT):
                def _vt(v, ti=ti):
                    pr = PREFB[:, ti * E:(ti + 1) * E]
                    nc.vector.tensor_tensor(FLAT[:], pr, BASER[:], ALU.add)
                    v.drain()
                    nc.vector.tensor_scalar(VBM[:], FLAT[:], float(CAP), None, ALU.is_ge)
                    nc.vector.tensor_tensor(FLAT[:], FLAT[:], EXPOFF[:], ALU.add)
                    v.drain()
                    nc.vector.tensor_scalar(TMPR[:], VBM[:], -1.0, 1.0, ALU.mult, ALU.add)
                    v.drain()
                    nc.vector.tensor_tensor(FLAT[:], FLAT[:], TMPR[:], ALU.mult)
                    v.drain()
                    nc.vector.tensor_scalar(TMPR[:], VBM[:], DUMPC[:, 0:1], None, ALU.mult)
                    v.drain()
                    nc.vector.tensor_tensor(FLAT[:], FLAT[:], TMPR[:], ALU.add)
                    v.drain()
                    for kk in range(KTOP):
                        nc.vector.tensor_scalar(OHK[:], IOTA64[:],
                                                E8ALL[:, ti * 8 + kk: ti * 8 + kk + 1],
                                                None, ALU.is_equal)
                        v.drain()
                        nc.vector.tensor_tensor(TMPR[:], FLAT[:], OHK[:], ALU.mult)
                        v.drain()
                        nc.vector.reduce_sum(SLOTS[:, ti * 8 + kk: ti * 8 + kk + 1],
                                             TMPR[:], axis=AX.X)
                        v.drain()
                    nc.vector.tensor_scalar(TOKC[:], IOTAP[:], HC[:, 48:49],
                                            float(ti * 128), ALU.add, ALU.add)
                    v.drain()
                    scv = SCV[:].rearrange("p (t k two) -> p t k two", t=NT, two=2)
                    nc.vector.tensor_copy(scv[:, ti, :, 0],
                                          TOKC[:, 0:1].to_broadcast([128, 8]))
                    s_v.inc(nc.vector.tensor_copy(scv[:, ti, :, 1],
                                                  WNALL[:, ti * 8:(ti + 1) * 8]))
                V(_vt)
            v_tbl = s_v.n

            def _vu(v):
                v.wait_ge(s_v.h, v_tbl)
                s_v.inc(nc.vector.tensor_copy(SLU32[:], SLOTS[:]))
            V(_vu)
            v_slu = s_v.n

            def _gps(gp):
                gp.wait_ge(s_v.h, v_tbl)
                gp.wait_ge(s_fill.h, gd_fill)
                for ti in range(NT):
                    for kk in range(KTOP):
                        col = ti * 8 + kk
                        s_scat.inc(nc.gpsimd.indirect_dma_start(
                            out=tbl,
                            out_offset=IndirectOffsetOnAxis(
                                SLU32[:, col:col + 1], 0),
                            in_=SCV[:, col * 2:col * 2 + 2], in_offset=None,
                            bounds_check=REGS["btbl"], oob_is_err=False), 16)
            GP(_gps)
            gd_scat = s_scat.n

            def _gpr(gp):
                gp.wait_ge(s_scat.h, gd_scat)
                s_cc.inc(nc.gpsimd.collective_compute(
                    "ReduceScatter", ALU.min, replica_groups=RG,
                    ins=[tbl[0:ZROW, :].opt()], outs=[tbl_mine.opt()]))
                s_fill.inc(gp.dma_start(out=ye_all[ZROW:ZROW + 128, :],
                                         in_=ZTILE[:]), 16)
            GP(_gpr)
        cc_tbl = s_cc.n

        barrier_except({s_cc, s_fill})
        route_stack.close()

        SE1 = e12(nc.sbuf_tensor("SE1", [128, 512], F32))
        SE2 = e12(nc.sbuf_tensor("SE2", [128, 512], F32))
        SEU = e12(nc.sbuf_tensor("SEU", [128, 512], BF16))
        if do_moe:
            XNBL = e1(nc.sbuf_tensor("XNBL", [128, D], BF16))
            SGB = e1(nc.sbuf_tensor("SGB", [128, 8 * D], BF16))
            SUBW = e1(nc.sbuf_tensor("SUBW", [128, 8 * D], BF16))
            GSB = e1(nc.sbuf_tensor("GSB", [128, 8 * S], BF16))
            XC512 = [e1(nc.sbuf_tensor(f"XC512_{i}", [128, NFT * 512], BF16))
                     for i in range(2)]

            def _gpw(gp):
                wait_all(gp, [s_v, s_a, s_pe])
                s_sg.inc(gp.dma_start(
                    out=SGB[:].rearrange("p (a c) -> p a c", a=NFT), in_=sgt_in.rearrange("(a p) c -> p a c", p=128)), 16)
                s_sg.inc(gp.dma_start(
                    out=SUBW[:].rearrange("p (a c) -> p a c", a=NFT), in_=sut_in.rearrange("(a p) c -> p a c", p=128)), 16)
            GP(_gpw)
            gd_sg = s_sg.n

            p1 = {"ev": {}, "vg": {}, "pe": {}}
            for tc in range(8):
                buf = XC512[tc % 2]
                for sub in range(4):
                    ti = tc * 4 + sub

                    def _syl(sy, ti=ti, tc=tc, sub=sub):
                        if tc >= 2 and sub == 0:
                            sy.wait_ge(s_pe.h, p1["pe"][tc - 2])
                        if ti >= 1:
                            sy.wait_ge(s_v.h, p1["ev"][ti - 1])
                        s_xnb.inc(sy.dma_start(
                            out=XNBL[:], in_=xn_rows[ti * 128:(ti + 1) * 128, :]), 16)
                    SY(_syl)
                    ld_p1 = s_xnb.n

                    def _tt(t, ti=ti, ld_p1=ld_p1):
                        t.wait_ge(s_xnb.h, ld_p1)
                        if ti >= 1:
                            t.wait_ge(s_v.h, p1["ev"][ti - 1])
                        for j in range(NFT):
                            _li = nc.tensor.transpose(PBT[:, j * 128:(j + 1) * 128],
                                                XNBL[:, j * 128:(j + 1) * 128],
                                                IDENT[:])
                        s_pe.inc(_li)
                    T(_tt)
                    pe_tt = s_pe.n

                    def _vtt(v, ti=ti, sub=sub, buf=buf, pe_tt=pe_tt):
                        v.wait_ge(s_pe.h, pe_tt)
                        for j in range(NFT):
                            _li = nc.vector.tensor_copy(
                                buf[:, j * 512 + sub * 128: j * 512 + sub * 128 + 128],
                                PBT[:, j * 128:(j + 1) * 128])
                        s_v.inc(_li)
                    V(_vtt)
                    p1["ev"][ti] = s_v.n
                v_chunk = s_v.n

                for i in range(8):
                    def _tg(t, tc=tc, i=i, buf=buf, v_chunk=v_chunk):
                        t.wait_ge(s_v.h, v_chunk)
                        if tc == 0 and i == 0:
                            t.wait_ge(s_sg.h, gd_sg)
                        if (tc, i - 1) in p1["vg"]:
                            t.wait_ge(s_v.h, p1["vg"][(tc, i - 1)])
                        elif tc >= 1:
                            t.wait_ge(s_v.h, p1["vg"][(tc - 1, 7)])
                        for j in range(NFT):
                            _li = nc.tensor.matmul(PB[0][:, 0:512],
                                             SGB[:, j * D + i * 128: j * D + (i + 1) * 128],
                                             buf[:, j * 512:(j + 1) * 512],
                                             start=(j == 0), stop=(j == NFT - 1))
                        for j in range(NFT):
                            _li = nc.tensor.matmul(PB[1][:, 0:512],
                                             SUBW[:, j * D + i * 128: j * D + (i + 1) * 128],
                                             buf[:, j * 512:(j + 1) * 512],
                                             start=(j == 0), stop=(j == NFT - 1))
                        s_pe.inc(_li)
                    T(_tg)
                    pe_g = s_pe.n

                    def _scg(sc, pe_g=pe_g):
                        sc.wait_ge(s_pe.h, pe_g)
                        nc.scalar.activation(SE1[:], PB[0][:, 0:512], AF.Exp,
                                             bias=0.0, scale=-1.0)
                        _li = nc.scalar.copy(SEU[:], PB[1][:, 0:512])
                        s_a.inc(_li)
                    SC(_scg)
                    a_g = s_a.n

                    def _vg(v, tc=tc, i=i, a_g=a_g):
                        v.wait_ge(s_a.h, a_g)
                        nc.vector.tensor_scalar(SE1[:], SE1[:], 1.0, None, ALU.add)
                        v.drain()
                        nc.vector.reciprocal(SE1[:], SE1[:])
                        v.drain()
                        nc.vector.scalar_tensor_tensor(SE2[:], PB[0][:, 0:512], 1.0,
                                                       SE1[:], ALU.mult, ALU.mult)
                        v.drain()
                        s_v.inc(nc.vector.tensor_tensor(
                            GSB[:, i * S + tc * 512: i * S + (tc + 1) * 512],
                            SE2[:], SEU[:], ALU.mult))
                    V(_vg)
                    p1["vg"][(tc, i)] = s_v.n
                p1["pe"][tc] = s_pe.n

            barrier_except({s_cc, s_fill, s_scat})

            SDB = e1(nc.sbuf_tensor("SDB", [128, 8 * D], BF16))
            PRE1 = e1(nc.sbuf_tensor("PRE1", [128, D], F32))
            XRE = [e1(nc.sbuf_tensor(f"XRE{i}", [128, D], F32)) for i in range(2)]

            def _gpd(gp):
                s_sd2.inc(gp.dma_start(
                    out=SDB[:].rearrange("p (a h) -> p a h", a=NFT), in_=sdt_in.rearrange("(a p) h -> p a h", p=128)), 16)
            GP(_gpd)
            gd_sd2 = s_sd2.n

            p1b = {"v": {}, "sd": {}}
            for ti in range(NT):
                def _syd(sy, ti=ti):
                    if ti >= 2:
                        sy.wait_ge(s_v.h, p1b["v"][ti - 2])
                    s_xre[ti % 2].inc(sy.dma_start(out=XRE[ti % 2][:],
                                       in_=x_in[ti * 128:(ti + 1) * 128, :]), 16)
                SY(_syd)
                ld_x2 = s_xre[ti % 2].n

                def _td(t, ti=ti, ld_x2=ld_x2):
                    if ti == 0:
                        t.wait_ge(s_sd2.h, gd_sd2)
                    if ti >= 1:
                        t.wait_ge(s_v.h, p1b["v"][ti - 1])
                    for hc in range(2):
                        for i in range(8):
                            _li = nc.tensor.matmul(
                                PB[2 + hc][:, 0:512],
                                GSB[:, i * S + ti * 128: i * S + (ti + 1) * 128],
                                SDB[:, i * D + hc * 512: i * D + (hc + 1) * 512],
                                start=(i == 0), stop=(i == 7))
                    s_pe.inc(_li)
                T(_td)
                pe_d2 = s_pe.n

                def _vd(v, ti=ti, pe_d2=pe_d2, ld_x2=ld_x2):
                    v.wait_ge(s_pe.h, pe_d2)
                    v.wait_ge(s_xre[ti % 2].h, ld_x2)
                    if ti >= 1:
                        v.wait_ge(s_pre.h, p1b["sd"][ti - 1])
                    nc.vector.tensor_tensor(PRE1[:, 0:512], PB[2][:, 0:512],
                                            XRE[ti % 2][:, 0:512], ALU.add)
                    s_v.inc(nc.vector.tensor_tensor(PRE1[:, 512:1024], PB[3][:, 0:512],
                                                    XRE[ti % 2][:, 512:1024], ALU.add))
                V(_vd)
                p1b["v"][ti] = s_v.n

                def _syd2(sy, ti=ti):
                    sy.wait_ge(s_v.h, p1b["v"][ti])
                    s_pre.inc(sy.dma_start(out=pre_d[ti * 128:(ti + 1) * 128, :],
                                           in_=PRE1[:]), 16)
                SY(_syd2)
                p1b["sd"][ti] = s_pre.n
        else:
            PRE1 = e1(nc.sbuf_tensor("PRE1", [128, D], F32))
            prev_sd = {}
            for ti in range(NT):
                def _syd(sy, ti=ti):
                    if ti >= 1:
                        sy.wait_ge(s_pre.h, prev_sd[ti - 1])
                    s_x[0].inc(sy.dma_start(out=PRE1[:],
                                            in_=x_in[ti * 128:(ti + 1) * 128, :]), 16)
                    sy.wait_ge(s_x[0].h, s_x[0].n)
                    s_pre.inc(sy.dma_start(out=pre_d[ti * 128:(ti + 1) * 128, :],
                                           in_=PRE1[:]), 16)
                SY(_syd)
                prev_sd[ti] = s_pre.n

        barrier_except({s_cc, s_fill, s_scat})
        p1_stack.close()

        if do_moe:
            TBLS = e2(nc.sbuf_tensor("TBLS", [128, 16], F32))
            TIDX = e2(nc.sbuf_tensor("TIDX", [128, 8], U32))
            WSC = e2(nc.sbuf_tensor("WSC", [128, 8], F32))
            XE = e2(nc.sbuf_tensor("XE", [128, 8 * D], BF16))
            XET = e2(nc.sbuf_tensor("XET", [128, 8 * CAP], BF16))
            W1B = [e2(nc.sbuf_tensor(f"W1B{i}", [128, 8 * FF], BF16)) for i in range(2)]
            W3B = [e2(nc.sbuf_tensor(f"W3B{i}", [128, 8 * FF], BF16)) for i in range(2)]
            W2B = [e2(nc.sbuf_tensor(f"W2B{i}", [128, 4 * D], BF16)) for i in range(2)]
            H1B = e2(nc.sbuf_tensor("H1B", [128, 4 * CAP], BF16))
            YES = [e2(nc.sbuf_tensor(f"YES{i}", [128, D], BF16)) for i in range(2)]

            p2 = {"pe": {}, "vy": {}, "sdy": {}, "vg2": {}, "vx": {}}
            for ee in range(EPC):
                bw = ee % 2

                def _gpw2(gp, ee=ee, bw=bw):
                    if ee >= 2:
                        gp.wait_ge(s_pe.h, p2["pe"][ee - 2])
                    s_w2[bw].inc(gp.dma_start(
                        out=W1B[bw][:].rearrange("p (a f) -> p a f", a=NFT),
                        in_=w1t_in[ee].rearrange("(a p) f -> p a f", p=128)), 16)
                    s_w2[bw].inc(gp.dma_start(
                        out=W3B[bw][:].rearrange("p (a f) -> p a f", a=NFT),
                        in_=w3t_in[ee].rearrange("(a p) f -> p a f", p=128)), 16)
                    s_w2[bw].inc(gp.dma_start(
                        out=W2B[bw][:].rearrange("p (a h) -> p a h", a=4),
                        in_=w2t_in[ee].rearrange("(a p) h -> p a h", p=128)), 16)
                GP(_gpw2)
                gd_w = s_w2[bw].n

                def _syt(sy, ee=ee):
                    if ee == 0:
                        sy.wait_ge(s_cc.h, cc_tbl)
                    else:
                        sy.wait_ge(s_v.h, p2["vx"][ee - 1])
                    s_tbl.inc(sy.dma_start(
                        out=TBLS[:].rearrange("p (a two) -> p a two", two=2),
                        in_=tbl_mine[ee * CAP:(ee + 1) * CAP, :]
                            .rearrange("(a p) two -> p a two", p=128)), 16)
                SY(_syt)
                ld_t = s_tbl.n

                def _vx(v, ee=ee, ld_t=ld_t):
                    v.wait_ge(s_tbl.h, ld_t)
                    if ee >= 1:
                        v.wait_ge(s_pe.h, p2["pe"][ee - 1])
                    tb2 = TBLS[:].rearrange("p (a two) -> p a two", two=2)
                    nc.vector.tensor_copy(TIDX[:], tb2[:, :, 0])
                    nc.vector.tensor_copy(WSC[:], tb2[:, :, 1])
                    s_v.inc(nc.vector.memset(XE[:], 0.0))
                V(_vx)
                v_xe = s_v.n
                p2["vx"][ee] = v_xe

                def _gpg(gp, ee=ee, v_xe=v_xe):
                    gp.wait_ge(s_v.h, v_xe)
                    if ee == 0:
                        gp.wait_ge(s_cc.h, cc_xn)
                    for a in range(8):
                        s_xeg.inc(nc.gpsimd.indirect_dma_start(
                            out=XE[:, a * D:(a + 1) * D], out_offset=None,
                            in_=xn_all,
                            in_offset=IndirectOffsetOnAxis(TIDX[:, a:a + 1], 0),
                            bounds_check=REGS["bxn"], oob_is_err=False), 16)
                GP(_gpg)
                gd_g = s_xeg.n

                for a in range(8):
                    def _tx(t, ee=ee, a=a, gd_g=gd_g):
                        if a == 0:
                            t.wait_ge(s_xeg.h, gd_g)
                        if (ee, a - 1) in p2.get("ev", {}):
                            t.wait_ge(s_v.h, p2["ev"][(ee, a - 1)])
                        for j in range(NFT):
                            _li = nc.tensor.transpose(
                                PBT[:, j * 128:(j + 1) * 128],
                                XE[:, a * D + j * 128: a * D + (j + 1) * 128], IDENT[:])
                        s_pe.inc(_li)
                    if ee == 0 and a == 0:
                        p2["ev"] = {}
                    T(_tx)
                    pe_x = s_pe.n

                    def _vxt(v, ee=ee, a=a, pe_x=pe_x):
                        v.wait_ge(s_pe.h, pe_x)
                        for j in range(NFT):
                            _li = nc.vector.tensor_copy(
                                XET[:, j * CAP + a * 128: j * CAP + (a + 1) * 128],
                                PBT[:, j * 128:(j + 1) * 128])
                        s_v.inc(_li)
                    V(_vxt)
                    p2["ev"][(ee, a)] = s_v.n
                v_xet = s_v.n

                for fi in range(4):
                    for nk in range(2):
                        def _tg2(t, ee=ee, fi=fi, nk=nk, bw=bw, v_xet=v_xet,
                                 gd_w=gd_w):
                            t.wait_ge(s_v.h, v_xet)
                            if fi == 0 and nk == 0:
                                t.wait_ge(s_w2[bw].h, gd_w)
                            if ee == 0 and fi == 0 and nk == 0:
                                t.wait_ge(s_w2[0].h, gd_w)
                            if ee == 1 and fi == 0 and nk == 0:
                                t.wait_ge(s_w2[1].h, s_w2[1].n)
                            prev = p2["vg2"].get((ee, fi, nk - 1),
                                                 p2["vg2"].get((ee, fi - 1, 1),
                                                               p2["vg2"].get((ee - 1, 3, 1), 0)))
                            if prev:
                                t.wait_ge(s_v.h, prev)
                            for j in range(NFT):
                                _li = nc.tensor.matmul(
                                    PB[0][:, 0:512],
                                    W1B[bw][:, j * FF + fi * 128: j * FF + (fi + 1) * 128],
                                    XET[:, j * CAP + nk * 512: j * CAP + (nk + 1) * 512],
                                    start=(j == 0), stop=(j == NFT - 1))
                            for j in range(NFT):
                                _li = nc.tensor.matmul(
                                    PB[1][:, 0:512],
                                    W3B[bw][:, j * FF + fi * 128: j * FF + (fi + 1) * 128],
                                    XET[:, j * CAP + nk * 512: j * CAP + (nk + 1) * 512],
                                    start=(j == 0), stop=(j == NFT - 1))
                            s_pe.inc(_li)
                        T(_tg2)
                        pe_g2 = s_pe.n

                        def _scg2(sc, pe_g2=pe_g2):
                            sc.wait_ge(s_pe.h, pe_g2)
                            nc.scalar.activation(SE1[:], PB[0][:, 0:512], AF.Exp,
                                                 bias=0.0, scale=-1.0)
                            _li = nc.scalar.copy(SEU[:], PB[1][:, 0:512])
                            s_a.inc(_li)
                        SC(_scg2)
                        a_g2 = s_a.n

                        def _vg2(v, ee=ee, fi=fi, nk=nk, a_g2=a_g2):
                            v.wait_ge(s_a.h, a_g2)
                            nc.vector.tensor_scalar(SE1[:], SE1[:], 1.0, None, ALU.add)
                            v.drain()
                            nc.vector.reciprocal(SE1[:], SE1[:])
                            v.drain()
                            nc.vector.scalar_tensor_tensor(SE2[:], PB[0][:, 0:512], 1.0,
                                                           SE1[:], ALU.mult, ALU.mult)
                            v.drain()
                            s_v.inc(nc.vector.tensor_tensor(
                                H1B[:, fi * CAP + nk * 512: fi * CAP + (nk + 1) * 512],
                                SE2[:], SEU[:], ALU.mult))
                        V(_vg2)
                        p2["vg2"][(ee, fi, nk)] = s_v.n
                v_h1 = s_v.n

                for a in range(8):
                    def _ty(t, ee=ee, a=a, bw=bw, v_h1=v_h1):
                        t.wait_ge(s_v.h, v_h1)
                        if (ee, a - 1) in p2["vy"]:
                            t.wait_ge(s_v.h, p2["vy"][(ee, a - 1)])
                        elif ee >= 1:
                            t.wait_ge(s_v.h, p2["vy"][(ee - 1, 7)])
                        for hc in range(2):
                            for fi in range(4):
                                _li = nc.tensor.matmul(
                                    PB[2 + hc][:, 0:512],
                                    H1B[:, fi * CAP + a * 128: fi * CAP + (a + 1) * 128],
                                    W2B[bw][:, fi * D + hc * 512: fi * D + (hc + 1) * 512],
                                    start=(fi == 0), stop=(fi == 3))
                        s_pe.inc(_li)
                    T(_ty)
                    pe_y = s_pe.n

                    def _vy(v, ee=ee, a=a, pe_y=pe_y):
                        v.wait_ge(s_pe.h, pe_y)
                        v.wait_ge(s_yew[a % 2].h, s_yew[a % 2].n)
                        ybuf = YES[a % 2]
                        nc.vector.tensor_scalar(ybuf[:, 0:512], PB[2][:, 0:512],
                                                WSC[:, a:a + 1], None, ALU.mult)
                        s_v.inc(nc.vector.tensor_scalar(ybuf[:, 512:1024],
                                                        PB[3][:, 0:512],
                                                        WSC[:, a:a + 1], None, ALU.mult))
                    V(_vy)
                    p2["vy"][(ee, a)] = s_v.n

                    def _syy(sy, ee=ee, a=a):
                        sy.wait_ge(s_v.h, p2["vy"][(ee, a)])
                        s_yew[a % 2].inc(sy.dma_start(
                            out=ye_mine[ee * CAP + a * 128: ee * CAP + (a + 1) * 128, :],
                            in_=YES[a % 2][:]), 16)
                    SY(_syy)
                    p2["sdy"][(ee, a)] = s_yew[a % 2].n
                p2["pe"][ee] = s_pe.n

            def _gpy2(gp):
                gp.wait_ge(s_yew[0].h, s_yew[0].n)
                gp.wait_ge(s_yew[1].h, s_yew[1].n)
                s_cc.inc(nc.gpsimd.collective_compute(
                    "AllGather", ALU.bypass, replica_groups=RG,
                    ins=[ye_mine.opt()], outs=[ye_all[0:ZROW, :].opt()]))
            GP(_gpy2)
        cc_ye = s_cc.n

        barrier_except({s_cc, s_fill, s_scat, s_pre})
        p2_stack.close()
        p12_stack.close()

        if do_moe:
            PREM = [e3(nc.sbuf_tensor(f"PREM{i}", [128, D], F32)) for i in range(2)]

            def emit_pre_moe(gp, tis):
                for ti in tis:
                    pm = PREM[ti % 2]
                    sl = s_prem[ti % 2]
                    gp.wait_ge(sl.h, sl.n)
                    nc.gpsimd.memset(pm[:], 0.0)
                    for kk in range(KTOP):
                        sl.inc(nc.gpsimd.indirect_dma_start(
                            out=pm[:], out_offset=None,
                            in_=ye_all,
                            in_offset=IndirectOffsetOnAxis(
                                SLU32[:, ti * 8 + kk: ti * 8 + kk + 1], 0),
                            compute_op=ALU.add, oob_is_err=True), 16)
                    sl.inc(gp.dma_start(out=pre_d[ti * 128:(ti + 1) * 128, :],
                                        in_=pm[:], accum_op=ALU.add), 16)
        else:
            def emit_pre_moe(gp, tis):
                pass

        if do_attn:
            XHB = e3(nc.sbuf_tensor("XHB", [128, NFT * S], BF16))
            WQK = [e3(nc.sbuf_tensor(f"WQK{i}", [128, 8 * 384], BF16)) for i in range(2)]
            WKA = e3(nc.sbuf_tensor("WKA", [128, 8 * 128], BF16))
            WKB = e3(nc.sbuf_tensor("WKB", [128, 8 * 128], BF16))
            QTS = e3(nc.sbuf_tensor("QTS", [128, S], BF16))
            KTS = e3(nc.sbuf_tensor("KTS", [128, S], BF16))
            VTS = e3(nc.sbuf_tensor("VTS", [128, S], BF16))
            VTP = e3(nc.sbuf_tensor("VTP", [128, (S // 128) * 129], BF16))
            PTS = e3(nc.sbuf_tensor("PTS", [128, S], BF16))
            PH3 = e3(nc.sbuf_tensor("PH3", [3, S], BF16))
            _SSB0 = e3(nc.sbuf_tensor("SSB", [128, S], F32))
            SSB = [_SSB0, _SSB0]
            EBUF = [e3(nc.sbuf_tensor(f"EBUF{i}", [128, S], BF16)) for i in range(2)]
            ETS = e3(nc.sbuf_tensor("ETS", [128, 1024], BF16))
            CMX = e3(nc.sbuf_tensor("CMX", [128, 24], F32))
            BCOL = e3(nc.sbuf_tensor("BCOL", [128, 8], F32))
            OSB = e3(nc.sbuf_tensor("OSB", [128, 128], BF16))
            GOT = e3(nc.sbuf_tensor("GOT", [128, 128], BF16))
            PSC = e3(nc.sbuf_tensor("PSC", [128, 512], F32))
            PSD = e3(nc.sbuf_tensor("PSD", [128, 512], F32))

            def _syh(sy):
                wait_all(sy, [s_v, s_a, s_pe, s_g])
                sy.wait_ge(s_xsp.h, s_xsp.n)
                for j in range(NFT):
                    s_xhb.inc(sy.dma_start(out=XHB[:, j * S:(j + 1) * S],
                                           in_=xht_sp[:, j * S:(j + 1) * S]), 16)
            SY(_syh)
            ld_xhb = s_xhb.n

            A = {"wb": {}, "ph3": {}, "qkv": {}, "pts": {}, "vt": {}, "sebank": [],
                 "ebhist": [], "tehist": [], "pv": {}, "ov": {}, "godma": [],
                 "exp": [], "osb": [], "pemoe": 0}
            pre_sched = {h: [] for h in range(NH)}
            if do_moe:
                tl = list(range(NT))
                for i3, ti in enumerate(tl):
                    pre_sched[9 + (i3 % (NH - 9))].append(ti)

            for h in range(NH):
                hb = h % 2

                def _syw(gp, h=h, hb=hb):
                    if h >= 2:
                        gp.wait_ge(s_wb.h, A["wb"][h - 2])
                    s_wq[hb].inc(gp.dma_start(
                        out=WQK[hb][:].rearrange("p (a d) -> p a d", a=NFT),
                        in_=wqkv[h].rearrange("(a p) d -> p a d", p=128)), 16)
                GP(_syw)
                sd_w = s_wq[hb].n

                def _gph(gp, h=h):
                    gp.wait_ge(s_v.h, v_pos3)
                    if do_moe and pre_sched[h] and not A["pemoe"]:
                        A["pemoe"] = 1
                        gp.wait_ge(s_cc.h, cc_ye)
                        gp.wait_ge(s_pre.h, s_pre.n)
                        gp.wait_ge(s_fill.h, s_fill.n)
                    if h >= 1:
                        gp.wait_ge(s_pe.h, A["ph3"][h - 1])
                    s_ph3.inc(gp.dma_start(out=PH3[:], in_=POS3B[h::32, :][0:3, :]), 16)
                    emit_pre_moe(gp, pre_sched[h])
                GP(_gph)
                gd_ph3 = s_ph3.n

                def _tb3(t, h=h, hb=hb, sd_w=sd_w):
                    t.wait_ge(s_wq[hb].h, sd_w)
                    if h == 0:
                        t.wait_ge(s_xhb.h, ld_xhb)
                    for r in range(3):
                        for j in range(NFT):
                            _li = nc.tensor.matmul(
                                PBM[:, r:r + 1],
                                WQK[hb][:, j * 384 + r * 128: j * 384 + (r + 1) * 128],
                                LNBB[:, j:j + 1],
                                start=(j == 0), stop=(j == NFT - 1))
                    s_pe.inc(_li)
                T(_tb3)
                pe_b = s_pe.n

                def _vw(v, h=h, hb=hb, pe_b=pe_b):
                    v.wait_ge(s_pe.h, pe_b)
                    v.wait_ge(s_a.h, s_a.n)
                    nc.vector.tensor_scalar(BCOL[:, 0:1], PBM[:, 0:1],
                                            BQKV[:, h * 3:h * 3 + 1],
                                            HC[:, 16 + h:17 + h], ALU.add, ALU.mult)
                    nc.vector.tensor_scalar(BCOL[:, 1:2], PBM[:, 1:2],
                                            BQKV[:, h * 3 + 1:h * 3 + 2], None, ALU.add)
                    nc.vector.tensor_scalar(BCOL[:, 2:3], PBM[:, 2:3],
                                            BQKV[:, h * 3 + 2:h * 3 + 3], None, ALU.add)
                    for j in range(NFT):
                        q0 = WQK[hb][:, j * 384:j * 384 + 128]
                        k0 = WQK[hb][:, j * 384 + 128:j * 384 + 256]
                        v0 = WQK[hb][:, j * 384 + 256:j * 384 + 384]
                        nc.vector.tensor_scalar(q0, q0, LNW[:, j:j + 1],
                                                HC[:, 16 + h:17 + h], ALU.mult, ALU.mult)
                        nc.vector.tensor_scalar(WKA[:, j * 128:(j + 1) * 128], k0,
                                                LNW[:, j:j + 1], HC[:, 32 + h:33 + h],
                                                ALU.mult, ALU.mult)
                        nc.vector.tensor_scalar(WKB[:, j * 128:(j + 1) * 128], k0,
                                                LNW[:, j:j + 1], HC[:, h:h + 1],
                                                ALU.mult, ALU.mult)
                        _li = nc.vector.tensor_scalar(v0, v0, LNW[:, j:j + 1], None, ALU.mult)
                    s_v.inc(_li)
                V(_vw)
                v_w = s_v.n

                for tc in range(8):
                    def _tq(t, h=h, hb=hb, tc=tc, v_w=v_w):
                        t.wait_ge(s_v.h, v_w)
                        if (h, tc - 1) in A["qkv"]:
                            t.wait_ge(s_a.h, A["qkv"][(h, tc - 1)])
                        for j in range(NFT):
                            _li = nc.tensor.matmul(PB[0][:, 0:512],
                                             WQK[hb][:, j * 384:j * 384 + 128],
                                             XHB[:, j * S + tc * 512: j * S + (tc + 1) * 512],
                                             start=(j == 0), stop=(j == NFT - 1))
                        for j in range(NFT):
                            _li = nc.tensor.matmul(PB[1][:, 0:512],
                                             WKA[:, j * 128:(j + 1) * 128],
                                             XHB[:, j * S + tc * 512: j * S + (tc + 1) * 512],
                                             start=(j == 0), stop=False)
                        for j in range(NFT):
                            if tc > 0:
                                _li = nc.tensor.matmul(
                                    PB[1][:, 0:512], WKB[:, j * 128:(j + 1) * 128],
                                    XHB[:, j * S + tc * 512 - 1: j * S + (tc + 1) * 512 - 1],
                                    start=False, stop=(j == NFT - 1))
                            else:
                                _li = nc.tensor.matmul(
                                    PB[1][:, 1:512], WKB[:, j * 128:(j + 1) * 128],
                                    XHB[:, j * S: j * S + 511],
                                    start=False, stop=(j == NFT - 1))
                        for j in range(NFT):
                            _li = nc.tensor.matmul(PB[2][:, 0:512],
                                             WQK[hb][:, j * 384 + 256: j * 384 + 384],
                                             XHB[:, j * S + tc * 512: j * S + (tc + 1) * 512],
                                             start=(j == 0), stop=(j == NFT - 1))
                        s_pe.inc(_li)
                    T(_tq)
                    pe_q = s_pe.n

                    def _sce(sc, h=h, tc=tc, pe_q=pe_q):
                        sc.wait_ge(s_pe.h, pe_q)
                        nc.scalar.activation(QTS[:, tc * 512:(tc + 1) * 512],
                                             PB[0][:, 0:512], AF.Identity,
                                             bias=BCOL[:, 0:1])
                        nc.scalar.activation(KTS[:, tc * 512:(tc + 1) * 512],
                                             PB[1][:, 0:512], AF.Identity,
                                             bias=BCOL[:, 1:2])
                        _li = nc.scalar.activation(VTS[:, tc * 512:(tc + 1) * 512],
                                                   PB[2][:, 0:512], AF.Identity,
                                                   bias=BCOL[:, 2:3])
                        s_a.inc(_li)
                    SC(_sce)
                    A["qkv"][(h, tc)] = s_a.n
                a_qkv = s_a.n

                def _vk(v, h=h, a_qkv=a_qkv):
                    v.wait_ge(s_a.h, a_qkv)
                    nc.vector.tensor_scalar(PSC[:, 0:1], BCOL[:, 1:2],
                                            HC[:, h:h + 1], None, ALU.mult)
                    v.drain()
                    s_v.inc(nc.vector.tensor_tensor(KTS[:, 0:1], KTS[:, 0:1],
                                                    PSC[:, 0:1], ALU.subtract))
                V(_vk)
                v_kfix = s_v.n

                if cfg.get("dbg_attn") and h == 0:
                    def _dbga(sy, v_kfix=v_kfix):
                        sy.wait_ge(s_v.h, v_kfix)
                        for i in range(4):
                            s_st.inc(sy.dma_start(
                                out=dbg_att[i * 128:(i + 1) * 128, :],
                                in_=QTS[:, i * 1024:(i + 1) * 1024]), 16)
                            s_st.inc(sy.dma_start(
                                out=dbg_att[(4 + i) * 128:(5 + i) * 128, :],
                                in_=KTS[:, i * 1024:(i + 1) * 1024]), 16)
                            s_st.inc(sy.dma_start(
                                out=dbg_att[(8 + i) * 128:(9 + i) * 128, :],
                                in_=VTS[:, i * 1024:(i + 1) * 1024]), 16)
                            s_st.inc(sy.dma_start(
                                out=dbg_att[(16 + i) * 128:(16 + i) * 128 + 80, :],
                                in_=POS3B[:, i * 1024:(i + 1) * 1024]), 16)
                    SY(_dbga)

                WPB = WKA

                def _syp(gp, h=h, v_kfix=v_kfix):
                    gp.wait_ge(s_v.h, v_kfix)
                    gp.wait_ge(s_pe.h, s_pe.n)
                    s_wp.inc(gp.dma_start(
                        out=WPB[:].rearrange("p (a d) -> p a d", a=NFT),
                        in_=wp_in[h].rearrange("(a p) d -> p a d", p=128)), 16)
                GP(_syp)
                sd_wp = s_wp.n

                def _tb5(t, h=h, sd_wp=sd_wp):
                    t.wait_ge(s_wp.h, sd_wp)
                    for j in range(NFT):
                        _li = nc.tensor.matmul(PBM[:, 4:5], WPB[:, j * 128:(j + 1) * 128],
                                         LNBB[:, j:j + 1],
                                         start=(j == 0), stop=(j == NFT - 1))
                    s_pe.inc(_li)
                T(_tb5)
                pe_bp = s_pe.n

                def _vwp(v, h=h, pe_bp=pe_bp):
                    v.wait_ge(s_pe.h, pe_bp)
                    nc.vector.tensor_scalar(BCOL[:, 3:4], PBM[:, 4:5],
                                            BPS[:, h:h + 1], None, ALU.add)
                    for j in range(NFT):
                        w0 = WPB[:, j * 128:(j + 1) * 128]
                        _li = nc.vector.tensor_scalar(w0, w0, LNW[:, j:j + 1], None, ALU.mult)
                    s_v.inc(_li)
                V(_vwp)
                v_wp = s_v.n

                for tc in range(8):
                    def _tp2(t, h=h, tc=tc, v_wp=v_wp):
                        t.wait_ge(s_v.h, v_wp)
                        if (h, tc - 1) in A["pts"]:
                            t.wait_ge(s_v.h, A["pts"][(h, tc - 1)])
                        for j in range(NFT):
                            _li = nc.tensor.matmul(PB[3][:, 0:512],
                                             WPB[:, j * 128:(j + 1) * 128],
                                             XHB[:, j * S + tc * 512: j * S + (tc + 1) * 512],
                                             start=(j == 0), stop=(j == NFT - 1))
                        s_pe.inc(_li)
                    T(_tp2)
                    pe_p = s_pe.n

                    def _scp(sc, h=h, tc=tc, pe_p=pe_p):
                        sc.wait_ge(s_pe.h, pe_p)
                        nc.scalar.activation(PSC[:], PB[3][:, 0:512], AF.Identity,
                                             bias=BCOL[:, 3:4])
                        sc.drain()
                        _li = nc.scalar.activation(PSD[:], PSC[:], AF.Exp,
                                                   bias=0.0, scale=-1.0)
                        s_a.inc(_li)
                    SC(_scp)
                    a_p = s_a.n

                    def _vp3(v, h=h, tc=tc, a_p=a_p):
                        v.wait_ge(s_a.h, a_p)
                        nc.vector.tensor_scalar(PSD[:], PSD[:], 1.0, None, ALU.add)
                        v.drain()
                        nc.vector.reciprocal(PSD[:], PSD[:])
                        v.drain()
                        s_v.inc(nc.vector.tensor_tensor(
                            PTS[:, tc * 512:(tc + 1) * 512], PSC[:], PSD[:], ALU.mult))
                    V(_vp3)
                    A["pts"][(h, tc)] = s_v.n

                if cfg.get("dbg_attn") and h == 0:
                    def _dbgp(sy, pts7=A["pts"][(0, 7)]):
                        sy.wait_ge(s_v.h, pts7)
                        for i in range(4):
                            s_st.inc(sy.dma_start(
                                out=dbg_att[(12 + i) * 128:(13 + i) * 128, :],
                                in_=PTS[:, i * 1024:(i + 1) * 1024]), 16)
                    SY(_dbgp)

                def _vv(v, h=h, a_qkv=a_qkv):
                    v.wait_ge(s_a.h, a_qkv)
                    s_v.inc(nc.vector.memset(VTP[:], 1.0))
                V(_vv)
                v_vtp0 = s_v.n

                for grp in range(4):
                    def _tv(t, h=h, grp=grp, v_vtp0=v_vtp0):
                        t.wait_ge(s_v.h, v_vtp0)
                        if h >= 1 and grp == 0:
                            t.wait_ge(s_go.h, A["go_prev"])
                        if (h, grp - 1) in A["vt"]:
                            t.wait_ge(s_v.h, A["vt"][(h, grp - 1)])
                        for j in range(8):
                            kb = grp * 8 + j
                            _li = nc.tensor.transpose(
                                PBT2[:, j * 128:(j + 1) * 128],
                                VTS[:, kb * 128:(kb + 1) * 128], IDENT[:])
                        s_pe.inc(_li)
                    if h == 0 and grp == 0:
                        A["go_prev"] = 0
                    T(_tv)
                    pe_v = s_pe.n

                    def _vtv(v, h=h, grp=grp, pe_v=pe_v):
                        v.wait_ge(s_pe.h, pe_v)
                        for j in range(8):
                            kb = grp * 8 + j
                            nc.vector.tensor_copy(VTP[:, kb * 129: kb * 129 + 128],
                                                  PBT2[:, j * 128:(j + 1) * 128])
                        s_v.inc(v.drain())
                    V(_vtv)
                    A["vt"][(h, grp)] = s_v.n
                v_vtp = s_v.n

                ebq = {}

                def emit_sprime(qt, h=h, v_vtp=v_vtp, gd_ph3=gd_ph3):
                    L = (qt + 1) * 128
                    nch = _ceil(L, 512)
                    ebuf = EBUF[qt % 2]
                    ssb = SSB[qt % 2]
                    base = (qt % 2) * 8

                    for cc in range(nch):
                        cw = min(512, L - cc * 512)
                        gidx = len(A["sebank"])
                        bank = PB[gidx % 4]

                        def _ts(t, qt=qt, cc=cc, cw=cw, bank=bank, gidx=gidx,
                                L=L):
                            if qt == 0 and cc == 0:
                                t.wait_ge(s_ph3.h, gd_ph3)
                                t.wait_ge(s_v.h, v_vtp)
                                t.wait_ge(s_v.h, A["pts"][(h, 7)])
                            if gidx >= 4:
                                t.wait_ge(s_se.h, A["sebank"][gidx - 4])
                            _li = nc.tensor.matmul(bank[:, 0:cw], ONES3B[:, 0:128],
                                             PH3[:, cc * 512: cc * 512 + cw],
                                             start=True, stop=False)
                            if cc == (L - 1) // 512:
                                _li = nc.tensor.matmul(bank[:, cw - 128:cw],
                                                 IDENT[:], UTNEG[:],
                                                 start=False, stop=False)
                            s_sp.inc(nc.tensor.matmul(
                                bank[:, 0:cw], QTS[:, qt * 128:(qt + 1) * 128],
                                KTS[:, cc * 512: cc * 512 + cw],
                                start=False, stop=True))
                        T(_ts)
                        sp_here = s_sp.n

                        def _vs(v, qt=qt, cc=cc, cw=cw, bank=bank,
                                sp_here=sp_here, ssb=ssb, base=base):
                            v.wait_ge(s_sp.h, sp_here)
                            if cc == 0 and A["exp"]:
                                v.wait_ge(s_eb.h, A["exp"][-1])
                            s_se.inc(nc.vector.tensor_scalar(
                                ssb[:, cc * 512: cc * 512 + cw], bank[:, 0:cw],
                                0.0, None, ALU.add, ALU.max,
                                accum_out=CMX[:, base + cc:base + cc + 1]))
                        V(_vs)
                        A["sebank"].append(s_se.n)
                    se_all = s_se.n

                    def _vm(v, qt=qt, nch=nch, se_all=se_all, base=base):
                        v.wait_ge(s_se.h, se_all)
                        s_v.inc(nc.vector.reduce_max(
                            CMX[:, 16 + qt % 2:17 + qt % 2],
                            CMX[:, base:base + nch], axis=AX.X, negate=True))
                    V(_vm)
                    v_mx = s_v.n

                    if cfg.get("dbg_attn") and h == 0 and qt == NT - 1:
                        def _dbgs(sy, v_mx=v_mx, ssb=ssb):
                            sy.wait_ge(s_v.h, v_mx)
                            for i in range(4):
                                s_st.inc(sy.dma_start(
                                    out=dbg_f32[i * 128:(i + 1) * 128, :],
                                    in_=ssb[:, i * 1024:(i + 1) * 1024]), 16)
                            s_st.inc(sy.dma_start(
                                out=dbg_f32[4 * 128:5 * 128, 0:18],
                                in_=CMX[:, 0:18]), 16)
                        SY(_dbgs)

                    def _se2(sc, qt=qt, L=L, ebuf=ebuf, v_mx=v_mx, ssb=ssb):
                        sc.wait_ge(s_v.h, v_mx)
                        if A["ebhist"]:
                            sc.wait_ge(s_pv.h, A["ebhist"][-1])
                        _li = nc.scalar.activation(ebuf[:, 0:L], ssb[:, 0:L],
                                                   AF.Exp,
                                                   bias=CMX[:, 16 + qt % 2:17 + qt % 2])
                        s_eb.inc(_li)
                    SC(_se2)
                    ebq[qt] = s_eb.n
                    A["exp"].append(s_eb.n)

                def emit_attend(qt, h=h):
                    L = (qt + 1) * 128
                    ebuf = EBUF[qt % 2]
                    eb_here = ebq[qt]
                    nkb = L // 128
                    for grp in range(_ceil(nkb, 8)):
                        nblk = min(8, nkb - grp * 8)

                        def _tt2(t, qt=qt, grp=grp, nblk=nblk, ebuf=ebuf,
                                 eb_here=eb_here):
                            t.wait_ge(s_eb.h, eb_here)
                            if A["tehist"]:
                                t.wait_ge(s_te.h, A["tehist"][-1])
                            for j in range(nblk):
                                kb = grp * 8 + j
                                _li = nc.tensor.transpose(
                                    PBT[:, j * 128:(j + 1) * 128],
                                    ebuf[:, kb * 128:(kb + 1) * 128], IDENT[:])
                            s_tp.inc(_li)
                        T(_tt2)
                        tp_here = s_tp.n

                        def _ve(v, nblk=nblk, tp_here=tp_here):
                            v.wait_ge(s_tp.h, tp_here)
                            s_te.inc(nc.vector.tensor_copy(
                                ETS[:, 0:nblk * 128], PBT[:, 0:nblk * 128]))
                        V(_ve)
                        te_here = s_te.n
                        A["tehist"].append(te_here)

                        def _tpv(t, qt=qt, grp=grp, nblk=nblk, te_here=te_here,
                                 nkb=nkb):
                            t.wait_ge(s_te.h, te_here)
                            if grp == 0 and A["osb"]:
                                t.wait_ge(s_ov.h, A["osb"][-1])
                            for j in range(nblk):
                                kb = grp * 8 + j
                                _li = nc.tensor.matmul(
                                    PBO[:, 0:129], ETS[:, j * 128:(j + 1) * 128],
                                    VTP[:, kb * 129:(kb + 1) * 129],
                                    start=(kb == 0), stop=(kb == nkb - 1))
                            s_pv.inc(_li)
                        T(_tpv)
                    pv_here = s_pv.n
                    A["ebhist"].append(pv_here)

                    def _vo(v, qt=qt, pv_here=pv_here):
                        v.wait_ge(s_pv.h, pv_here)
                        if A["osb"]:
                            v.wait_ge(s_pe.h, A["osb_pe"])
                        nc.vector.reciprocal(CMX[:, 18:19], PBO[:, 128:129])
                        v.drain()
                        _li = nc.vector.tensor_scalar(OSB[:], PBO[:, 0:128],
                                                      CMX[:, 18:19], None, ALU.mult)
                        s_ov.inc(_li)
                    V(_vo)
                    A["osb"].append(s_ov.n)
                    v_osb = s_ov.n

                    def _to(t, qt=qt, v_osb=v_osb):
                        t.wait_ge(s_ov.h, v_osb)
                        _li = nc.tensor.transpose(PBT2[:, 896:1024], OSB[:], IDENT[:])
                        s_pe.inc(_li)
                    T(_to)
                    A["osb_pe"] = s_pe.n
                    pe_ot = s_pe.n

                    def _vg3(v, qt=qt, pe_ot=pe_ot):
                        v.wait_ge(s_pe.h, pe_ot)
                        if A["godma"]:
                            v.wait_ge(s_got.h, A["godma"][-1])
                        _li = nc.vector.tensor_tensor(
                            GOT[:], PBT2[:, 896:1024],
                            PTS[:, qt * 128:(qt + 1) * 128], ALU.mult)
                        s_go.inc(_li)
                    V(_vg3)
                    go_here = s_go.n

                    def _syg2(sy, qt=qt, go_here=go_here):
                        sy.wait_ge(s_go.h, go_here)
                        s_got.inc(sy.dma_start(
                            out=gots[qt, :, h * 128:(h + 1) * 128], in_=GOT[:]), 16)
                    SY(_syg2)
                    A["godma"].append(s_got.n)

                for qq in range(NT + 1):
                    if qq < NT:
                        emit_sprime(qq)
                    if qq >= 1:
                        emit_attend(qq - 1)

                A["go_prev"] = s_go.n
                A["ph3"][h] = s_pe.n

                def _vwb(v, h=h):
                    last_pv = A["ebhist"][-1]
                    v.wait_ge(s_pv.h, last_pv)
                    s_wb.inc(nc.vector.engine_nop())
                V(_vwb)
                A["wb"][h] = s_wb.n
        else:
            def _gpall(gp):
                wait_all(gp)
                emit_pre_moe(gp, list(range(NT)))
            GP(_gpall)
        gd_pre = (s_prem[0].n, s_prem[1].n)

        barrier()
        p3_stack.close()

        WOB = e4(nc.sbuf_tensor("WOB", [128, NH * D], BF16))
        GHB = [e4(nc.sbuf_tensor(f"GHB{i}", [128, NH * 128], BF16)) for i in range(2)]
        ARB = e4(nc.sbuf_tensor("ARB", [128, D], F32))
        PREL = [e4(nc.sbuf_tensor(f"PREL{i}", [128, D], F32)) for i in range(2)]
        OUTT = e4(nc.sbuf_tensor("OUTT", [128, D], F32))
        ST4 = e4(nc.sbuf_tensor("ST4", [128, 8], F32))

        def _gpo(gp):
            s_wo.inc(gp.dma_start(
                out=WOB[:].rearrange("p (hh d) -> p hh d", hh=NH), in_=wout_in.rearrange("(hh p) d -> p hh d", p=128)), 16)
        GP(_gpo)
        gd_wo = s_wo.n

        p4 = {"v": {}, "st": {}}
        for ti in range(NT):
            tb = ti % 2

            def _syo(sy, ti=ti, tb=tb):
                if ti == 0:
                    if do_moe:
                        sy.wait_ge(s_prem[0].h, gd_pre[0])
                        sy.wait_ge(s_prem[1].h, gd_pre[1])
                    sy.wait_ge(s_pre.h, s_pre.n)
                    sy.wait_ge(s_got.h, s_got.n)
                if ti >= 2:
                    sy.wait_ge(s_v.h, p4["v"][ti - 2])
                if do_attn:
                    s_p4l[tb].inc(sy.dma_start(out=GHB[tb][:], in_=gots[ti]), 16)
                s_p4l[tb].inc(sy.dma_start(out=PREL[tb][:],
                                           in_=pre_d[ti * 128:(ti + 1) * 128, :]), 16)
            SY(_syo)
            ld_o = s_p4l[tb].n

            def _t4(t, ti=ti, tb=tb, ld_o=ld_o):
                t.wait_ge(s_p4l[tb].h, ld_o)
                if ti == 0:
                    t.wait_ge(s_wo.h, gd_wo)
                if ti >= 1:
                    t.wait_ge(s_a.h, p4["st"][ti - 1])
                if do_attn:
                    for hh in range(NH):
                        for fc in range(2):
                            _li = nc.tensor.matmul(
                                PB[fc][:, 0:512],
                                GHB[tb][:, hh * 128:(hh + 1) * 128],
                                WOB[:, hh * D + fc * 512: hh * D + (fc + 1) * 512],
                                start=(hh == 0), stop=(hh == NH - 1))
                    s_pe.inc(_li)
                else:
                    s_pe.inc(t.nop())
            T(_t4)
            pe_o = s_pe.n

            def _sc5(sc, ti=ti, pe_o=pe_o):
                sc.wait_ge(s_pe.h, pe_o)
                if do_attn:
                    nc.scalar.copy(ARB[:, 0:512], PB[0][:, 0:512])
                    _li = nc.scalar.copy(ARB[:, 512:1024], PB[1][:, 0:512])
                    s_a.inc(_li)
                else:
                    _li = nc.scalar.copy(ARB[:], ZTILE[:])
                    s_a.inc(_li)
            SC(_sc5)
            p4["st"][ti] = s_a.n

            def _v8(v, ti=ti):
                v.wait_ge(s_a.h, s_a.n)
                nc.vector.tensor_tensor(OUTT[:], ARB[:], ARB[:], ALU.mult)
                nc.vector.reduce_sum(ST4[:, 0:1], ARB[:], axis=AX.X)
                v.drain()
                nc.vector.reduce_sum(ST4[:, 1:2], OUTT[:], axis=AX.X)
                nc.vector.tensor_scalar_mul(ST4[:, 0:1], ST4[:, 0:1], 1.0 / D)
                v.drain()
                nc.vector.tensor_scalar_mul(ST4[:, 1:2], ST4[:, 1:2], 1.0 / D)
                nc.vector.tensor_tensor(ST4[:, 2:3], ST4[:, 0:1], ST4[:, 0:1], ALU.mult)
                v.drain()
                s_v.inc(nc.vector.tensor_tensor(ST4[:, 1:2], ST4[:, 1:2], ST4[:, 2:3],
                                                ALU.subtract))
            V(_v8)

            def _sc6(sc, ti=ti):
                sc.wait_ge(s_v.h, s_v.n)
                nc.scalar.activation(ST4[:, 2:3], ST4[:, 1:2], AF.Ln, bias=EPSC[:, 0:1])
                sc.drain()
                _li = nc.scalar.activation(ST4[:, 2:3], ST4[:, 2:3], AF.Exp,
                                           bias=0.0, scale=-0.5)
                s_a.inc(_li)
            SC(_sc6)

            def _v9(v, ti=ti):
                v.wait_ge(s_a.h, s_a.n)
                nc.vector.tensor_tensor(ST4[:, 3:4], ST4[:, 0:1], ST4[:, 2:3], ALU.mult)
                v.drain()
                s_v.inc(nc.vector.tensor_scalar_mul(ST4[:, 3:4], ST4[:, 3:4], -1.0))
            V(_v9)

            def _sc7(sc, ti=ti):
                sc.wait_ge(s_v.h, s_v.n)
                _li = nc.scalar.activation(OUTT[:], ARB[:], AF.Identity,
                                           bias=ST4[:, 3:4], scale=ST4[:, 2:3])
                s_a.inc(_li)
            SC(_sc7)
            a_n = s_a.n

            def _v10(v, ti=ti, tb=tb, a_n=a_n, ld_o=ld_o):
                v.wait_ge(s_a.h, a_n)
                v.wait_ge(s_p4l[tb].h, ld_o)
                nc.vector.tensor_tensor(OUTT[:], OUTT[:], OLWR[:], ALU.mult)
                nc.vector.tensor_tensor(OUTT[:], OUTT[:], OLBR[:], ALU.add)
                s_v.inc(nc.vector.tensor_tensor(OUTT[:], OUTT[:], PREL[tb][:], ALU.add))
            V(_v10)
            p4["v"][ti] = s_v.n
            v_out = s_v.n

            def _syo2(sy, ti=ti, v_out=v_out):
                sy.wait_ge(s_v.h, v_out)
                s_st.inc(sy.dma_start(out=out[ti * 128:(ti + 1) * 128, :],
                                      in_=OUTT[:]), 16)
            SY(_syo2)

        def _syz(sy):
            sy.wait_ge(s_st.h, s_st.n)
        SY(_syz)

        p4_stack.close()

    return nc


_NC_CACHE = {}


def _get_nc(key, cfg):
    if key not in _NC_CACHE:
        _NC_CACHE[key] = build_nc(cfg)
    return _NC_CACHE[key]


def make_in_maps(inputs):
    A = np.ascontiguousarray
    W_in = np.asarray(inputs["L0_attn_W_in"], np.float32)
    W_out = np.asarray(inputs["L0_attn_W_out"], np.float32)
    x = np.asarray(inputs["L0_attn_x"], np.float32)
    wqkv = np.empty((NH, D, 384), np.float32)
    for h in range(NH):
        for r in range(3):
            wqkv[h, :, r * 128:(r + 1) * 128] = W_in[r * DEXP + h * DH:
                                                     r * DEXP + (h + 1) * DH, :].T
    wp = np.empty((NH, D, DH), np.float32)
    for h in range(NH):
        wp[h] = W_in[3 * DEXP + h * DH: 3 * DEXP + (h + 1) * DH, :].T
    wy = A(W_in[4 * DEXP:4 * DEXP + NH, :].T)
    b_in = np.asarray(inputs["L0_attn_b_in"], np.float32)
    bqkv = np.empty((128, 48), np.float32)
    for h in range(NH):
        for r in range(3):
            bqkv[:, h * 3 + r] = b_in[r * DEXP + h * DH: r * DEXP + (h + 1) * DH]
    bp = np.empty((128, NH), np.float32)
    for h in range(NH):
        bp[:, h] = b_in[3 * DEXP + h * DH: 3 * DEXP + (h + 1) * DH]
    by = A(b_in[4 * DEXP:4 * DEXP + NH].reshape(NH, 1))
    lnw8 = A(np.asarray(inputs["L0_attn_in_ln_w"], np.float32).reshape(8, 128).T)
    lnb8 = A(np.asarray(inputs["L0_attn_in_ln_b"], np.float32).reshape(8, 128).T)
    olnw = A(np.asarray(inputs["L0_attn_out_ln_w"], np.float32).reshape(1, D))
    olnb = A(np.asarray(inputs["L0_attn_out_ln_b"], np.float32).reshape(1, D))
    smear = A(np.asarray(inputs["L0_attn_smear_factor"], np.float32).reshape(1, NH))
    lsc = A(np.asarray(inputs["L0_attn_log_scale"], np.float32).reshape(1, NH))
    wout = A(W_out.T)
    gwt = A(np.asarray(inputs["L0_ffn_gate_weight"], np.float32).T)
    ebias = A(np.asarray(inputs["L0_ffn_e_bias"], np.float32).reshape(1, E))
    sgt = A(np.asarray(inputs["L0_ffn_shared_gate"], np.float32).T)
    sut = A(np.asarray(inputs["L0_ffn_shared_up"], np.float32).T)
    sdt = A(np.asarray(inputs["L0_ffn_shared_down"], np.float32).T)
    w1 = np.asarray(inputs["L0_ffn_w1"], np.float32)
    w2 = np.asarray(inputs["L0_ffn_w2"], np.float32)
    w3 = np.asarray(inputs["L0_ffn_w3"], np.float32)

    in_maps = []
    for c in range(NCORE):
        rmask = np.zeros((8, 1), np.float32)
        rmask[:c] = 1.0
        cbase = np.array([[c * S, c]], np.float32)
        in_maps.append({
            "x": A(x[c]),
            "wqkv": wqkv, "wp": wp, "wy": wy, "bqkv": bqkv, "bp": bp, "by": by,
            "wout": wout, "lnw8": lnw8, "lnb8": lnb8, "olnw": olnw, "olnb": olnb,
            "smear": smear, "lsc": lsc, "ebias": ebias, "gwt": gwt,
            "sgt": sgt, "sut": sut, "sdt": sdt,
            "w1t": A(w1[c * EPC:(c + 1) * EPC].transpose(0, 2, 1)),
            "w3t": A(w3[c * EPC:(c + 1) * EPC].transpose(0, 2, 1)),
            "w2t": A(w2[c * EPC:(c + 1) * EPC].transpose(0, 2, 1)),
            "rmask": rmask, "cbase": cbase,
        })
    return in_maps


def kernel(**inputs):
    cfg = {"attn": True, "moe": True}
    nc = _get_nc(("full",), cfg)
    in_maps = make_in_maps(inputs)
    res = run_bass_kernel_spmd(nc, in_maps, core_ids=list(range(NCORE)))
    return np.stack([res.results[c]["out"] for c in range(NCORE)], axis=0)


# version 07
# speedup vs baseline: 1.0019x; 1.0019x over previous
import numpy as np
import ml_dtypes
from contextlib import ExitStack

import concourse.bass as bass
import concourse.mybir as mybir
from concourse.bass_utils import run_bass_kernel_spmd
from concourse.bass import IndirectOffsetOnAxis

F32 = mybir.dt.float32
BF16 = mybir.dt.bfloat16
U32 = mybir.dt.uint32
AF = mybir.ActivationFunctionType
ALU = mybir.AluOpType
AX = mybir.AxisListType

NCORE = 8
S = 4096
D = 1024
NH = 16
DH = 128
DEXP = 2048
E = 64
EPC = 8
KTOP = 8
CAP = 1024
FF = 512
MOE_SCALE = 2.5
LN_EPS = 1e-5
NT = S // 128
NFT = D // 128
ZROW = E * CAP
RG = [list(range(NCORE))]
RSQ = float(1.0 / np.sqrt(128.0))


class Sem:
    def __init__(self, ctx, nc, name):
        self.h = ctx.enter_context(nc.semaphore(name))
        self.n = 0

    def inc(self, instr, k=1):
        instr.then_inc(self.h, k)
        self.n += k
        return instr


def _ceil(a, b):
    return (a + b - 1) // b


def build_nc(cfg):
    do_attn = cfg.get("attn", True)
    do_moe = cfg.get("moe", True)
    skip_ln = cfg.get("skip_ln", 0)
    nc = bass.Bass("TRN2", target_bir_lowering=False,
                   detect_race_conditions=not cfg.get("no_race", False))
    dtn = nc.dram_tensor

    x_in = dtn("x", [S, D], F32, kind="ExternalInput").ap()
    wqkv = dtn("wqkv", [NH, D, 384], F32, kind="ExternalInput").ap()
    wp_in = dtn("wp", [NH, D, DH], F32, kind="ExternalInput").ap()
    wy_in = dtn("wy", [D, NH], F32, kind="ExternalInput").ap()
    bqkv_in = dtn("bqkv", [128, 48], F32, kind="ExternalInput").ap()
    bp_in = dtn("bp", [128, NH], F32, kind="ExternalInput").ap()
    by_in = dtn("by", [NH, 1], F32, kind="ExternalInput").ap()
    wout_in = dtn("wout", [DEXP, D], F32, kind="ExternalInput").ap()
    lnw8_in = dtn("lnw8", [128, 8], F32, kind="ExternalInput").ap()
    lnb8_in = dtn("lnb8", [128, 8], F32, kind="ExternalInput").ap()
    olnw_in = dtn("olnw", [1, D], F32, kind="ExternalInput").ap()
    olnb_in = dtn("olnb", [1, D], F32, kind="ExternalInput").ap()
    smear_in = dtn("smear", [1, NH], F32, kind="ExternalInput").ap()
    lsc_in = dtn("lsc", [1, NH], F32, kind="ExternalInput").ap()
    ebias_in = dtn("ebias", [1, E], F32, kind="ExternalInput").ap()
    gwt_in = dtn("gwt", [D, E], F32, kind="ExternalInput").ap()
    sgt_in = dtn("sgt", [D, D], F32, kind="ExternalInput").ap()
    sut_in = dtn("sut", [D, D], F32, kind="ExternalInput").ap()
    sdt_in = dtn("sdt", [D, D], F32, kind="ExternalInput").ap()
    w1t_in = dtn("w1t", [EPC, D, FF], F32, kind="ExternalInput").ap()
    w3t_in = dtn("w3t", [EPC, D, FF], F32, kind="ExternalInput").ap()
    w2t_in = dtn("w2t", [EPC, FF, D], F32, kind="ExternalInput").ap()
    rmask_in = dtn("rmask", [8, 1], F32, kind="ExternalInput").ap()
    cbase_in = dtn("cbase", [1, 2], F32, kind="ExternalInput").ap()
    out = dtn("out", [S, D], F32, kind="ExternalOutput").ap()

    xn_rows = dtn("xn_rows", [S, D], BF16).ap()
    dbg_stat = dtn("dbg_stat", [NT * 128, 16], F32).ap()
    xht_sp = dtn("xht_sp", [128, NFT * S], BF16).ap()
    pre_d = dtn("pre_d", [S, D], F32).ap()
    gots = dtn("gots", [NT, 128, NH * DH], BF16).ap()
    dbg_att = dtn("dbg_att", [NT * 128, 1024], BF16).ap()
    dbg_f32 = dtn("dbg_f32", [NT * 128, 1024], F32).ap()
    if do_moe:
        xn_all = dtn("xn_all", [NCORE * S, D], BF16).ap()
        counts_in_d = dtn("counts_in_d", [1, E], F32).ap()
        counts_all = dtn("counts_all", [NCORE, E], F32).ap()
        tbl = dtn("tbl", [ZROW + 128, 2], F32).ap()
        tbl_mine = dtn("tbl_mine", [EPC * CAP, 2], F32).ap()
        ye_mine = dtn("ye_mine", [EPC * CAP, D], BF16).ap()
        ye_all = dtn("ye_all", [ZROW + 128, D], BF16).ap()

    with ExitStack() as top:
        et = top.enter_context

        s_ld = Sem(top, nc, "s_ld")
        s_sd = Sem(top, nc, "s_sd")
        s_gd = Sem(top, nc, "s_gd")
        s_st = Sem(top, nc, "s_st")
        s_pe = Sem(top, nc, "s_pe")
        s_v = Sem(top, nc, "s_v")
        s_a = Sem(top, nc, "s_a")
        s_g = Sem(top, nc, "s_g")
        s_cc = Sem(top, nc, "s_cc")
        s_sp = Sem(top, nc, "s_sp")
        s_se = Sem(top, nc, "s_se")
        s_eb = Sem(top, nc, "s_eb")
        s_ez = Sem(top, nc, "s_ez")
        s_tp = Sem(top, nc, "s_tp")
        s_te = Sem(top, nc, "s_te")
        s_pv = Sem(top, nc, "s_pv")
        s_ov = Sem(top, nc, "s_ov")
        s_go = Sem(top, nc, "s_go")
        s_wb = Sem(top, nc, "s_wb")
        s_c = Sem(top, nc, "s_c")
        s_x = [Sem(top, nc, f"s_x{i}") for i in range(2)]
        s_xnr = Sem(top, nc, "s_xnr")
        s_xsp = Sem(top, nc, "s_xsp")
        s_gw = Sem(top, nc, "s_gw")
        s_wy = Sem(top, nc, "s_wy")
        s_xr = [Sem(top, nc, f"s_xr{i}") for i in range(2)]
        s_cnt = Sem(top, nc, "s_cnt")
        s_call = Sem(top, nc, "s_call")
        s_fill = Sem(top, nc, "s_fill")
        s_scat = Sem(top, nc, "s_scat")
        s_sg = Sem(top, nc, "s_sg")
        s_sd2 = Sem(top, nc, "s_sd2")
        s_xnb = Sem(top, nc, "s_xnb")
        s_xre = [Sem(top, nc, f"s_xre{i}") for i in range(2)]
        s_pre = Sem(top, nc, "s_pre")
        s_tbl = Sem(top, nc, "s_tbl")
        s_w2 = [Sem(top, nc, f"s_w2{i}") for i in range(2)]
        s_xeg = Sem(top, nc, "s_xeg")
        s_yew = [Sem(top, nc, f"s_yew{i}") for i in range(2)]
        s_prem = [Sem(top, nc, f"s_prem{i}") for i in range(2)]
        s_xhb = Sem(top, nc, "s_xhb")
        s_wq = [Sem(top, nc, f"s_wq{i}") for i in range(2)]
        s_wp = Sem(top, nc, "s_wp")
        s_ph3 = Sem(top, nc, "s_ph3")
        s_got = Sem(top, nc, "s_got")
        s_p4l = [Sem(top, nc, f"s_p4l{i}") for i in range(2)]
        s_wo = Sem(top, nc, "s_wo")
        ALLS = [s_ld, s_sd, s_gd, s_st, s_pe, s_v, s_a, s_g, s_cc,
                s_c, s_x[0], s_x[1], s_xnr, s_xsp, s_gw, s_wy, s_xr[0], s_xr[1],
                s_cnt, s_call, s_fill, s_scat, s_sg, s_sd2, s_xnb,
                s_xre[0], s_xre[1], s_pre, s_tbl, s_w2[0], s_w2[1], s_xeg,
                s_yew[0], s_yew[1], s_prem[0], s_prem[1], s_xhb,
                s_wq[0], s_wq[1], s_wp, s_ph3, s_got, s_p4l[0], s_p4l[1], s_wo]

        route_stack = ExitStack()
        p0_stack = ExitStack()
        y_stack = ExitStack()
        p1_stack = ExitStack()
        p12_stack = ExitStack()
        p2_stack = ExitStack()
        p3_stack = ExitStack()
        p4_stack = ExitStack()
        er = route_stack.enter_context
        e0 = p0_stack.enter_context
        ey = y_stack.enter_context
        e1 = p1_stack.enter_context
        e12 = p12_stack.enter_context
        e2 = p2_stack.enter_context
        e3 = p3_stack.enter_context
        e4 = p4_stack.enter_context
        top.callback(route_stack.close)
        top.callback(p0_stack.close)
        top.callback(y_stack.close)
        top.callback(p1_stack.close)
        top.callback(p12_stack.close)
        top.callback(p2_stack.close)
        top.callback(p3_stack.close)
        top.callback(p4_stack.close)

        PB = [et(nc.psum_tensor(f"PB{i}", [128, 512], F32)) for i in range(4)]
        PBT = et(nc.psum_tensor("PBT", [128, 1024], BF16))
        PBT2 = et(nc.psum_tensor("PBT2", [128, 1024], BF16))
        PBO = et(nc.psum_tensor("PBO", [128, 512], F32))
        PBM = et(nc.psum_tensor("PBM", [128, 512], F32))

        IDENT = et(nc.sbuf_tensor("IDENT", [128, 128], BF16))
        IDENTF = et(nc.sbuf_tensor("IDENTF", [128, 128], F32))
        ONESROW = et(nc.sbuf_tensor("ONESROW", [1, 128], F32))
        ONESCOLB = et(nc.sbuf_tensor("ONESCOLB", [128, 1], BF16))
        ONES3B = et(nc.sbuf_tensor("ONES3B", [3, 128], BF16))
        IOTAP = et(nc.sbuf_tensor("IOTAP", [128, 1], F32))
        IOTA64 = et(nc.sbuf_tensor("IOTA64", [128, 64], F32))
        EXPOFF = et(nc.sbuf_tensor("EXPOFF", [128, 64], F32))
        DUMPC = et(nc.sbuf_tensor("DUMPC", [128, 1], F32))
        UT128 = et(nc.sbuf_tensor("UT128", [128, 128], BF16))
        UTNEG = et(nc.sbuf_tensor("UTNEG", [128, 128], BF16))
        HC = et(nc.sbuf_tensor("HC", [128, 64], F32))
        EBREP = et(nc.sbuf_tensor("EBREP", [128, 64], F32))
        OLWR = et(nc.sbuf_tensor("OLWR", [128, D], F32))
        OLBR = et(nc.sbuf_tensor("OLBR", [128, D], F32))
        LNW = et(nc.sbuf_tensor("LNW", [128, 8], F32))
        LNB = et(nc.sbuf_tensor("LNB", [128, 8], F32))
        LNBB = et(nc.sbuf_tensor("LNBB", [128, 8], BF16))
        BQKV = et(nc.sbuf_tensor("BQKV", [128, 48], F32))
        BPS = et(nc.sbuf_tensor("BPS", [128, NH], F32))
        BYS = et(nc.sbuf_tensor("BYS", [NH, 1], F32))
        SMS = et(nc.sbuf_tensor("SMS", [1, 64], F32))
        RMS = et(nc.sbuf_tensor("RMS", [8, 1], F32))
        CBS = et(nc.sbuf_tensor("CBS", [1, 2], F32))
        POS3B = et(nc.sbuf_tensor("POS3B", [80, S], BF16))
        SLOTS = et(nc.sbuf_tensor("SLOTS", [128, NT * KTOP], F32))
        SLU32 = et(nc.sbuf_tensor("SLU32", [128, NT * KTOP], U32))
        BASER = et(nc.sbuf_tensor("BASER", [128, E], F32))
        ZTILE = et(nc.sbuf_tensor("ZTILE", [128, 1024], F32))
        FILLT = et(nc.sbuf_tensor("FILLT", [128, 128], F32))
        EPSC = et(nc.sbuf_tensor("EPSC", [128, 1], F32))
        GWT = er(nc.sbuf_tensor("GWT", [128, NFT * E], F32))
        LOGIT = er(nc.sbuf_tensor("LOGIT", [128, NT * E], F32))
        WNALL = er(nc.sbuf_tensor("WNALL", [128, NT * KTOP], F32))
        E8ALL = er(nc.sbuf_tensor("E8ALL", [128, NT * KTOP], F32))
        PREFB = er(nc.sbuf_tensor("PREFB", [128, NT * E], F32))
        SFC = er(nc.sbuf_tensor("SFC", [128, 64], F32))
        G8S = er(nc.sbuf_tensor("G8S", [128, 64], F32))
        GSUM = er(nc.sbuf_tensor("GSUM", [128, 8], F32))
        T4B = er(nc.sbuf_tensor("T4B", [128, 8], F32))
        GIDX = er(nc.sbuf_tensor("GIDX", [128, 8], U32))
        GIDF = er(nc.sbuf_tensor("GIDF", [128, 8], F32))
        GMB = er(nc.sbuf_tensor("GMB", [128, 8], F32))
        TMPR = er(nc.sbuf_tensor("TMPR", [128, 64], F32))
        V8B = er(nc.sbuf_tensor("V8B", [128, 8], F32))
        E8U = er(nc.sbuf_tensor("E8U", [128, 8], U32))
        OHK = er(nc.sbuf_tensor("OHK", [128, 64], F32))
        W8B = er(nc.sbuf_tensor("W8B", [128, 8], F32))
        S01 = er(nc.sbuf_tensor("S01", [128, 64], F32))
        S01B = er(nc.sbuf_tensor("S01B", [128, 64], BF16))
        RC8 = er(nc.sbuf_tensor("RC8", [128, 1], F32))
        CARRY = er(nc.sbuf_tensor("CARRY", [1, 64], F32))
        CARRB = er(nc.sbuf_tensor("CARRB", [1, 64], BF16))
        CNTROW = er(nc.sbuf_tensor("CNTROW", [1, 64], F32))
        CALL = er(nc.sbuf_tensor("CALL", [8, 64], F32))
        CALLB = er(nc.sbuf_tensor("CALLB", [8, 64], BF16))
        RMSB = er(nc.sbuf_tensor("RMSB", [8, 1], BF16))
        SCV = er(nc.sbuf_tensor("SCV", [128, NT * 16], F32))
        FLAT = er(nc.sbuf_tensor("FLAT", [128, 64], F32))
        VBM = er(nc.sbuf_tensor("VBM", [128, 64], F32))
        TOKC = er(nc.sbuf_tensor("TOKC", [128, 1], F32))

        blk = et(nc.Block())

        def T(f):
            blk.tensor(f)

        def V(f):
            blk.vector(f)

        def SC(f):
            blk.scalar(f)

        def SY(f):
            blk.sync(f)

        def GP(f):
            blk.gpsimd(f)

        def wait_all(eng, sems=None):
            for s in (sems or ALLS):
                if s.n:
                    eng.wait_ge(s.h, s.n)

        def barrier():
            for em in (T, V, SC, SY, GP):
                def _f(eng):
                    wait_all(eng)
                em(_f)

        def barrier_except(skip):
            sems = [s for s in ALLS if s not in skip]
            for em in (T, V, SC, SY, GP):
                def _f(eng, sems=sems):
                    wait_all(eng, sems)
                em(_f)

        def _sy(sy):
            for dst, src in [(LNW[:], lnw8_in), (LNB[:], lnb8_in),
                             (BQKV[:], bqkv_in), (BPS[:], bp_in), (BYS[:], by_in),
                             (SMS[0:1, 0:NH], smear_in),
                             (SMS[0:1, 16:32], lsc_in),
                             (SMS[0:1, 32:48], smear_in),
                             (EBREP[0:1, 0:E], ebias_in),
                             (RMS[:], rmask_in), (CBS[:], cbase_in),
                             (OLWR[0:1, :], olnw_in), (OLBR[0:1, :], olnb_in)]:
                s_c.inc(sy.dma_start(out=dst, in_=src), 16)
        SY(_sy)
        ld_const = s_c.n

        REGS = {}

        def _gp(gp):
            REGS["btbl"] = gp.alloc_register("btbl")
            gp.reg_mov(REGS["btbl"], ZROW + 127)
            REGS["bxn"] = gp.alloc_register("bxn")
            gp.reg_mov(REGS["bxn"], NCORE * S - 1)
            nc.gpsimd.memset(IDENT[:], 1.0)
            nc.gpsimd.affine_select(out=IDENT[:], in_=IDENT[:], compare_op=ALU.is_equal,
                                    fill=0.0, base=0, pattern=[[-1, 128]],
                                    channel_multiplier=1)
            nc.gpsimd.memset(IDENTF[:], 1.0)
            nc.gpsimd.affine_select(out=IDENTF[:], in_=IDENTF[:], compare_op=ALU.is_equal,
                                    fill=0.0, base=0, pattern=[[-1, 128]],
                                    channel_multiplier=1)
            nc.gpsimd.memset(ONESROW[:], 1.0)
            nc.gpsimd.memset(ONESCOLB[:], 1.0)
            nc.gpsimd.memset(ONES3B[:], 1.0)
            nc.gpsimd.iota(IOTAP[:], pattern=[[0, 1]], base=0, channel_multiplier=1, allow_small_or_imprecise_dtypes=True)
            nc.gpsimd.iota(IOTA64[:], pattern=[[1, 64]], base=0, channel_multiplier=0, allow_small_or_imprecise_dtypes=True)
            nc.gpsimd.iota(EXPOFF[:], pattern=[[1024, 64]], base=0, channel_multiplier=0, allow_small_or_imprecise_dtypes=True)
            nc.gpsimd.iota(DUMPC[:], pattern=[[0, 1]], base=ZROW, channel_multiplier=1, allow_small_or_imprecise_dtypes=True)
            nc.gpsimd.memset(UT128[:], 1.0)
            nc.gpsimd.affine_select(
                out=UT128[:], in_=UT128[:], compare_op=ALU.is_gt, fill=0.0,
                base=0, pattern=[[1, 128]], channel_multiplier=-1)
            nc.gpsimd.memset(UTNEG[:], -1.0e10)
            nc.gpsimd.affine_select(
                out=UTNEG[:], in_=UTNEG[:], compare_op=ALU.is_gt, fill=0.0,
                base=0, pattern=[[1, 128]], channel_multiplier=-1)
            nc.gpsimd.memset(ZTILE[:], 0.0)
            nc.gpsimd.memset(SMS[0:1, 48:64], 0.0)
            nc.gpsimd.memset(EPSC[:], LN_EPS)
            s_g.inc(nc.gpsimd.memset(FILLT[:], 1.0e9))
        GP(_gp)

        def _sc(sc):
            sc.wait_ge(s_c.h, ld_const)
            nc.scalar.activation(SMS[0:1, 32:48], SMS[0:1, 32:48], AF.Sigmoid,
                                 bias=0.0, scale=-1.0)
            nc.scalar.activation(SMS[0:1, 0:16], SMS[0:1, 0:16], AF.Sigmoid)
            nc.scalar.activation(SMS[0:1, 16:32], SMS[0:1, 16:32], AF.Exp,
                                 bias=0.0, scale=-2.0)
            s_a.inc(sc.drain())
        SC(_sc)

        def _v(v):
            v.wait_ge(s_a.h, s_a.n)
            s_v.inc(nc.vector.tensor_scalar_mul(SMS[0:1, 16:32], SMS[0:1, 16:32], RSQ))
        V(_v)

        def _t2(t):
            wait_all(t, [s_v, s_g, s_c])
            _li = nc.tensor.matmul(PBM[:, 0:64], ONESROW[:], SMS[0:1, 0:64],
                             start=True, stop=True)
            _li = nc.tensor.matmul(PBM[:, 64:128], ONESROW[:], EBREP[0:1, 0:E],
                             start=True, stop=True)
            _li = nc.tensor.matmul(PBM[:, 128:130], ONESROW[:], CBS[0:1, :],
                             start=True, stop=True)
            _li = nc.tensor.matmul(PB[0][:, 0:512], ONESROW[:], OLWR[0:1, 0:512],
                             start=True, stop=True)
            _li = nc.tensor.matmul(PB[1][:, 0:512], ONESROW[:], OLWR[0:1, 512:1024],
                             start=True, stop=True)
            _li = nc.tensor.matmul(PB[2][:, 0:512], ONESROW[:], OLBR[0:1, 0:512],
                             start=True, stop=True)
            _li = nc.tensor.matmul(PB[3][:, 0:512], ONESROW[:], OLBR[0:1, 512:1024],
                             start=True, stop=True)
            s_pe.inc(_li)
        T(_t2)
        pe_const = s_pe.n

        def _v2(v):
            v.wait_ge(s_pe.h, pe_const)
            nc.vector.tensor_copy(HC[:, 0:50], PBM[:, 0:50])
            nc.vector.tensor_copy(HC[:, 48:50], PBM[:, 128:130])
            nc.vector.tensor_copy(LNBB[:], LNB[:])
            nc.vector.tensor_copy(EBREP[:], PBM[:, 64:128])
            nc.vector.tensor_copy(OLWR[:, 0:512], PB[0][:, 0:512])
            nc.vector.tensor_copy(OLWR[:, 512:1024], PB[1][:, 0:512])
            nc.vector.tensor_copy(OLBR[:, 0:512], PB[2][:, 0:512])
            s_v.inc(nc.vector.tensor_copy(OLBR[:, 512:1024], PB[3][:, 0:512]))
        V(_v2)

        barrier()

        WYB = ey(nc.sbuf_tensor("WYB", [128, NFT * NH], BF16))
        YT = ey(nc.sbuf_tensor("YT", [NH, S], F32))
        MIDF = ey(nc.sbuf_tensor("MIDF", [NH, S], F32))
        ONES16 = ey(nc.sbuf_tensor("ONES16", [NH, S], F32))

        XT = [e0(nc.sbuf_tensor(f"XT{i}", [128, D], F32)) for i in range(2)]
        XN = e0(nc.sbuf_tensor("XN", [128, D], F32))
        XH = e0(nc.sbuf_tensor("XH", [128, D], F32))
        XNB = e0(nc.sbuf_tensor("XNB", [128, D], BF16))
        XHTB = e0(nc.sbuf_tensor("XHTB", [128, NFT * 128], BF16))
        XNTF = e0(nc.sbuf_tensor("XNTF", [128, NFT * 128], F32))
        STAT = e0(nc.sbuf_tensor("STAT", [128, 16], F32))

        def _syg(sy):
            s_gw.inc(sy.dma_start(
                out=GWT[:].rearrange("p (a e) -> p a e", a=NFT), in_=gwt_in.rearrange("(a p) e -> p a e", p=128)), 16)
        SY(_syg)

        def _gpy0(gp):
            gp.wait_ge(s_c.h, ld_const)
            s_wy.inc(gp.dma_start(out=WYB[:].rearrange("p (a h) -> p a h", a=NFT),
                                  in_=wy_in.rearrange("(a p) h -> p a h", p=128)), 16)
        GP(_gpy0)
        gd_wy = s_wy.n

        g = {"v_done": {}, "pe_tr": {}, "ev": {}, "sd_sp": {}, "ya": {}, "sg": {}}
        for ti in range(NT):
            xt = XT[ti % 2]

            def _sy(sy, ti=ti, xt=xt):
                if ti >= 2:
                    sy.wait_ge(s_v.h, g["v_done"][ti - 2])
                s_x[ti % 2].inc(sy.dma_start(out=xt[:], in_=x_in[ti * 128:(ti + 1) * 128, :]), 16)
            SY(_sy)
            ld_here = s_x[ti % 2].n

            def _v(v, ti=ti, xt=xt, ld_here=ld_here):
                v.wait_ge(s_x[ti % 2].h, ld_here)
                if ti >= 1:
                    v.wait_ge(s_pe.h, g["pe_tr"][ti - 1])
                nc.vector.tensor_tensor(XN[:], xt[:], xt[:], ALU.mult)
                nc.vector.reduce_sum(STAT[:, 0:1], xt[:], axis=AX.X)
                v.drain()
                nc.vector.reduce_sum(STAT[:, 1:2], XN[:], axis=AX.X)
                nc.vector.tensor_scalar_mul(STAT[:, 0:1], STAT[:, 0:1], 1.0 / D)
                v.drain()
                nc.vector.tensor_scalar_mul(STAT[:, 1:2], STAT[:, 1:2], 1.0 / D)
                nc.vector.tensor_tensor(STAT[:, 2:3], STAT[:, 0:1], STAT[:, 0:1], ALU.mult)
                v.drain()
                s_v.inc(nc.vector.tensor_tensor(STAT[:, 1:2], STAT[:, 1:2],
                                                STAT[:, 2:3], ALU.subtract))
            V(_v)

            def _sc(sc, ti=ti):
                sc.wait_ge(s_v.h, s_v.n)
                nc.scalar.activation(STAT[:, 2:3], STAT[:, 1:2], AF.Ln, bias=EPSC[:, 0:1])
                sc.drain()
                _li = nc.scalar.activation(STAT[:, 2:3], STAT[:, 2:3], AF.Exp,
                                           bias=0.0, scale=-0.5)
                s_a.inc(_li)
            SC(_sc)

            def _v2(v, ti=ti):
                v.wait_ge(s_a.h, s_a.n)
                nc.vector.tensor_tensor(STAT[:, 3:4], STAT[:, 0:1], STAT[:, 2:3], ALU.mult)
                v.drain()
                s_v.inc(nc.vector.tensor_scalar_mul(STAT[:, 3:4], STAT[:, 3:4], -1.0))
            V(_v2)

            def _sc2(sc, ti=ti, xt=xt):
                sc.wait_ge(s_v.h, s_v.n)
                if skip_ln == 1:
                    _li = nc.scalar.copy(XN[:], xt[:])
                else:
                    _li = nc.scalar.activation(XN[:], xt[:], AF.Identity,
                                               bias=STAT[:, 3:4], scale=STAT[:, 2:3])
                s_a.inc(_li)
            SC(_sc2)

            def _v3(v, ti=ti):
                v.wait_ge(s_a.h, s_a.n)
                nc.vector.tensor_tensor(XH[:], XN[:], XN[:], ALU.mult)
                nc.vector.reduce_sum(STAT[:, 4:5], XN[:], axis=AX.X)
                v.drain()
                nc.vector.reduce_sum(STAT[:, 5:6], XH[:], axis=AX.X)
                nc.vector.tensor_scalar_mul(STAT[:, 4:5], STAT[:, 4:5], 1.0 / D)
                v.drain()
                nc.vector.tensor_scalar_mul(STAT[:, 5:6], STAT[:, 5:6], 1.0 / D)
                nc.vector.tensor_tensor(STAT[:, 6:7], STAT[:, 4:5], STAT[:, 4:5], ALU.mult)
                v.drain()
                s_v.inc(nc.vector.tensor_tensor(STAT[:, 5:6], STAT[:, 5:6],
                                                STAT[:, 6:7], ALU.subtract))
            V(_v3)

            def _sc3(sc, ti=ti):
                sc.wait_ge(s_v.h, s_v.n)
                nc.scalar.activation(STAT[:, 6:7], STAT[:, 5:6], AF.Ln, bias=EPSC[:, 0:1])
                sc.drain()
                _li = nc.scalar.activation(STAT[:, 6:7], STAT[:, 6:7], AF.Exp,
                                           bias=0.0, scale=-0.5)
                s_a.inc(_li)
            SC(_sc3)

            def _v4(v, ti=ti):
                v.wait_ge(s_a.h, s_a.n)
                nc.vector.tensor_tensor(STAT[:, 7:8], STAT[:, 4:5], STAT[:, 6:7], ALU.mult)
                v.drain()
                s_v.inc(nc.vector.tensor_scalar_mul(STAT[:, 7:8], STAT[:, 7:8], -1.0))
            V(_v4)

            def _sc4(sc, ti=ti):
                sc.wait_ge(s_v.h, s_v.n)
                _li = nc.scalar.activation(XH[:], XN[:], AF.Identity,
                                           bias=STAT[:, 7:8], scale=STAT[:, 6:7])
                s_a.inc(_li)
            SC(_sc4)
            a_ln = s_a.n

            def _v5(v, ti=ti, a_ln=a_ln):
                v.wait_ge(s_a.h, a_ln)
                s_v.inc(nc.vector.tensor_copy(XNB[:], XN[:]))
            V(_v5)
            v_xnb = s_v.n

            if do_moe:
                def _sy2(sy, ti=ti, v_xnb=v_xnb):
                    sy.wait_ge(s_v.h, v_xnb)
                    s_xnr.inc(sy.dma_start(out=xn_rows[ti * 128:(ti + 1) * 128, :],
                                           in_=XNB[:]), 16)
                    if cfg.get("dbg_stat"):
                        s_xnr.inc(sy.dma_start(
                            out=dbg_stat[ti * 128:(ti + 1) * 128, :],
                            in_=STAT[:]), 16)
                SY(_sy2)

            for half in range(2):
                def _t(t, ti=ti, half=half, a_ln=a_ln):
                    t.wait_ge(s_a.h, a_ln)
                    key = (ti, half - 1) if half else (ti - 1, 1)
                    if key in g["ev"]:
                        t.wait_ge(s_v.h, g["ev"][key])
                    for j in range(4):
                        jj = half * 4 + j
                        _li = nc.tensor.transpose(PB[2][:, j * 128:(j + 1) * 128],
                                            XN[:, jj * 128:(jj + 1) * 128], IDENTF[:])
                        _li = nc.tensor.transpose(PB[3][:, j * 128:(j + 1) * 128],
                                            XH[:, jj * 128:(jj + 1) * 128], IDENTF[:])
                    s_pe.inc(_li)
                T(_t)
                pe_here = s_pe.n

                def _v6(v, ti=ti, half=half, pe_here=pe_here):
                    v.wait_ge(s_pe.h, pe_here)
                    if (ti - 1, 1) in g["sd_sp"]:
                        v.wait_ge(s_xsp.h, g["sd_sp"][(ti - 1, 1)])
                    nc.vector.tensor_copy(XNTF[:, half * 512:(half + 1) * 512],
                                          PB[2][:, 0:512])
                    s_v.inc(nc.vector.tensor_copy(XHTB[:, half * 512:(half + 1) * 512],
                                                  PB[3][:, 0:512]))
                V(_v6)
                g["ev"][(ti, half)] = s_v.n
            g["pe_tr"][ti] = s_pe.n

            def _sy3(sy, ti=ti):
                sy.wait_ge(s_v.h, g["ev"][(ti, 1)])
                for j in range(NFT):
                    s_xsp.inc(sy.dma_start(
                        out=xht_sp[:, j * S + ti * 128: j * S + (ti + 1) * 128],
                        in_=XHTB[:, j * 128:(j + 1) * 128]), 16)
            SY(_sy3)
            g["sd_sp"][(ti, 1)] = s_xsp.n

            def _t3(t, ti=ti):
                t.wait_ge(s_v.h, g["ev"][(ti, 1)])
                if ti == 0:
                    t.wait_ge(s_gw.h, s_gw.n)
                if do_moe:
                    if ti >= 1:
                        t.wait_ge(s_a.h, g["sg"][ti - 1])
                    for j in range(NFT):
                        _li = nc.tensor.matmul(PBO[:, 0:64],
                                         XNTF[:, j * 128:(j + 1) * 128],
                                         GWT[:, j * E:(j + 1) * E],
                                         start=(j == 0), stop=(j == NFT - 1))
                    s_pe.inc(_li)
                else:
                    s_pe.inc(t.nop())
            T(_t3)
            pe_lg = s_pe.n

            if do_moe:
                def _scl(sc, ti=ti, pe_lg=pe_lg):
                    sc.wait_ge(s_pe.h, pe_lg)
                    _li = nc.scalar.activation(LOGIT[:, ti * E:(ti + 1) * E],
                                               PBO[:, 0:64], AF.Sigmoid)
                    s_a.inc(_li)
                SC(_scl)
                g["sg"][ti] = s_a.n
                a_sg = s_a.n

                if ti == 0:
                    def _vr0(v):
                        nc.vector.memset(CARRY[:], 0.0)
                        v.drain()
                        s_v.inc(nc.vector.tensor_copy(CARRB[:], CARRY[:]))
                    V(_vr0)

                def _vr(v, ti=ti, a_sg=a_sg):
                    v.wait_ge(s_a.h, a_sg)
                    sc0 = LOGIT[:, ti * E:(ti + 1) * E]
                    nc.vector.tensor_tensor(SFC[:], sc0, EBREP[:], ALU.add)
                    v.drain()
                    for gg in range(8):
                        nc.vector.max(out=G8S[:, gg * 8:(gg + 1) * 8],
                                      in_=SFC[:, gg * 8:(gg + 1) * 8])
                    v.drain()
                    a0 = G8S[:].rearrange("p (g e) -> p g e", g=8)
                    nc.vector.tensor_tensor(GSUM[:], a0[:, :, 0], a0[:, :, 1], ALU.add)
                    v.drain()
                    s_v.inc(nc.vector.max(out=T4B[:], in_=GSUM[:]))
                    v.wait_ge(s_v.h, s_v.n)
                    nc.vector.max_index(out=GIDX[:], in_max=T4B[:], in_values=GSUM[:])
                    v.drain()
                    nc.vector.tensor_copy(GIDF[:], GIDX[:])
                    nc.vector.memset(GMB[:], 0.0)
                    v.drain()
                    for kk in range(4):
                        nc.vector.tensor_scalar(OHK[:, 0:8], IOTA64[:, 0:8],
                                                GIDF[:, kk:kk + 1], None, ALU.is_equal)
                        v.drain()
                        nc.vector.tensor_tensor(GMB[:], GMB[:], OHK[:, 0:8], ALU.add)
                        v.drain()
                    for gg in range(8):
                        nc.vector.tensor_scalar(TMPR[:, gg * 8:(gg + 1) * 8],
                                                SFC[:, gg * 8:(gg + 1) * 8],
                                                GMB[:, gg:gg + 1], None, ALU.mult)
                    v.drain()
                    s_v.inc(nc.vector.max(out=V8B[:], in_=TMPR[:]))
                    v.wait_ge(s_v.h, s_v.n)
                    nc.vector.max_index(out=E8U[:], in_max=V8B[:], in_values=TMPR[:])
                    v.drain()
                    nc.vector.tensor_copy(E8ALL[:, ti * 8:(ti + 1) * 8], E8U[:])
                    nc.vector.memset(S01[:], 0.0)
                    v.drain()
                    for kk in range(KTOP):
                        nc.vector.tensor_scalar(OHK[:], IOTA64[:],
                                                E8ALL[:, ti * 8 + kk: ti * 8 + kk + 1],
                                                None, ALU.is_equal)
                        v.drain()
                        nc.vector.tensor_tensor(S01[:], S01[:], OHK[:], ALU.add)
                        nc.vector.tensor_tensor(TMPR[:], sc0, OHK[:], ALU.mult)
                        v.drain()
                        nc.vector.reduce_sum(W8B[:, kk:kk + 1], TMPR[:], axis=AX.X)
                        v.drain()
                    nc.vector.reduce_sum(RC8[:], W8B[:], axis=AX.X)
                    v.drain()
                    nc.vector.tensor_scalar(RC8[:], RC8[:], 1e-20, None, ALU.add)
                    v.drain()
                    nc.vector.reciprocal(RC8[:], RC8[:])
                    v.drain()
                    nc.vector.tensor_scalar(WNALL[:, ti * 8:(ti + 1) * 8], W8B[:],
                                            RC8[:, 0:1], MOE_SCALE, ALU.mult, ALU.mult)
                    s_v.inc(nc.vector.tensor_copy(S01B[:], S01[:]))
                V(_vr)
                v_s01 = s_v.n

                def _tr(t, ti=ti, v_s01=v_s01):
                    t.wait_ge(s_v.h, v_s01)
                    _li = nc.tensor.matmul(PB[0][:, 0:64], UT128[:], S01B[:],
                                     start=True, stop=False)
                    _li = nc.tensor.matmul(PB[0][:, 0:64], ONES3B[0:1, :], CARRB[0:1, :],
                                     start=False, stop=True)
                    _li = nc.tensor.matmul(PB[1][0:1, 0:64], ONESCOLB[:], S01B[:],
                                     start=True, stop=True)
                    s_pe.inc(_li)
                T(_tr)
                pe_pref = s_pe.n

                def _vpr(v, ti=ti, pe_pref=pe_pref):
                    v.wait_ge(s_pe.h, pe_pref)
                    nc.vector.tensor_copy(PREFB[:, ti * E:(ti + 1) * E],
                                          PB[0][:, 0:64])
                    nc.vector.tensor_copy(CNTROW[:], PB[1][0:1, 0:64])
                    v.drain()
                    nc.vector.tensor_tensor(CARRY[:], CARRY[:], CNTROW[:], ALU.add)
                    v.drain()
                    s_v.inc(nc.vector.tensor_copy(CARRB[:], CARRY[:]))
                V(_vpr)
                g["v_done"][ti] = s_v.n
            else:
                def _v7(v, ti=ti, pe_lg=pe_lg):
                    v.wait_ge(s_pe.h, pe_lg)
                    s_v.inc(nc.vector.engine_nop())
                V(_v7)
                g["v_done"][ti] = s_v.n

            def _t3y(t, ti=ti):
                if ti == 0:
                    t.wait_ge(s_wy.h, gd_wy)
                else:
                    t.wait_ge(s_a.h, g["ya"][ti - 1])
                for j in range(NFT):
                    _li = nc.tensor.matmul(PBM[0:NH, 0:128],
                                     WYB[:, j * NH:(j + 1) * NH],
                                     XHTB[:, j * 128:(j + 1) * 128],
                                     start=(j == 0), stop=(j == NFT - 1))
                s_pe.inc(_li)
            T(_t3y)
            pe_y0 = s_pe.n

            def _scy2(sc, ti=ti, pe_y0=pe_y0):
                sc.wait_ge(s_pe.h, pe_y0)
                _li = nc.scalar.activation(YT[0:NH, ti * 128:(ti + 1) * 128],
                                           PBM[0:NH, 0:128], AF.Sigmoid,
                                           bias=BYS[0:NH, 0:1])
                s_a.inc(_li)
            SC(_scy2)
            g["ya"][ti] = s_a.n

        barrier()

        if do_moe:
            def _gp2(gp):
                gp.wait_ge(s_xnr.h, s_xnr.n)
                s_cc.inc(nc.gpsimd.collective_compute(
                    "AllGather", ALU.bypass, replica_groups=RG,
                    ins=[xn_rows.opt()], outs=[xn_all.opt()]))
            GP(_gp2)
        cc_xn = s_cc.n

        def _gpy(gp):
            nc.gpsimd.memset(ONES16[:], 1.0)
            s_g.inc(nc.gpsimd.memset(POS3B[:], 0.0))
        GP(_gpy)
        g_ones = s_g.n

        a_sig = g["ya"][NT - 1]

        def _vp(v):
            v.wait_ge(s_a.h, a_sig)
            v.wait_ge(s_g.h, g_ones)
            nc.vector.tensor_tensor_scan(out=YT[:], data0=ONES16[:], data1=YT[:],
                                         initial=0.0, op0=ALU.mult, op1=ALU.add)
            v.drain()
            nc.vector.tensor_copy(POS3B[0:16, :], YT[:])
            v.drain()
            nc.vector.tensor_copy(MIDF[:], POS3B[0:16, :])
            v.drain()
            nc.vector.tensor_tensor(MIDF[:], YT[:], MIDF[:], ALU.subtract)
            v.drain()
            nc.vector.tensor_copy(POS3B[32:48, :], MIDF[:])
            v.drain()
            nc.vector.tensor_copy(ONES16[:], POS3B[32:48, :])
            v.drain()
            nc.vector.tensor_tensor(MIDF[:], MIDF[:], ONES16[:], ALU.subtract)
            v.drain()
            s_v.inc(nc.vector.tensor_copy(POS3B[64:80, :], MIDF[:]))
        V(_vp)
        v_pos3 = s_v.n

        barrier_except({s_cc})
        p0_stack.close()
        y_stack.close()

        if do_moe:
            v_route = g["v_done"][NT - 1]

            def _syc(sy):
                sy.wait_ge(s_v.h, v_route)
                s_cnt.inc(sy.dma_start(out=counts_in_d, in_=CARRY[:]), 16)
            SY(_syc)
            ld_cnt = s_cnt.n

            def _gpc(gp):
                gp.wait_ge(s_cnt.h, ld_cnt)
                s_cc.inc(nc.gpsimd.collective_compute(
                    "AllGather", ALU.bypass, replica_groups=RG,
                    ins=[counts_in_d.opt()], outs=[counts_all.opt()]))
                for i in range(8):
                    s_fill.inc(gp.dma_start(
                        out=tbl[i * 8192:(i + 1) * 8192, :]
                            .rearrange("(p a) two -> p (a two)", p=128),
                        in_=FILLT[0:128, 0:128]), 16)
            GP(_gpc)
            cc_cnt = s_cc.n
            gd_fill = s_fill.n

            def _syc2(sy):
                sy.wait_ge(s_cc.h, cc_cnt)
                s_call.inc(sy.dma_start(out=CALL[:], in_=counts_all), 16)
            SY(_syc2)
            ld_call = s_call.n

            def _vb(v):
                v.wait_ge(s_call.h, ld_call)
                nc.vector.tensor_copy(CALLB[:], CALL[:])
                s_v.inc(nc.vector.tensor_copy(RMSB[:], RMS[:]))
            V(_vb)
            v_call = s_v.n

            def _tb(t):
                t.wait_ge(s_v.h, v_call)
                _li = nc.tensor.matmul(PBM[0:1, 64:128], RMSB[0:8, 0:1], CALLB[:],
                                 start=True, stop=True)
                s_pe.inc(_li)
            T(_tb)
            pe_base = s_pe.n

            def _vb2(v):
                v.wait_ge(s_pe.h, pe_base)
                s_v.inc(nc.vector.tensor_copy(CNTROW[:], PBM[0:1, 64:128]))
            V(_vb2)
            v_b2 = s_v.n

            def _tb2(t):
                t.wait_ge(s_v.h, v_b2)
                _li = nc.tensor.matmul(PBM[:, 128:192], ONESROW[:], CNTROW[0:1, :],
                                 start=True, stop=True)
                s_pe.inc(_li)
            T(_tb2)
            pe_base2 = s_pe.n

            def _vb3(v):
                v.wait_ge(s_pe.h, pe_base2)
                s_v.inc(nc.vector.tensor_copy(BASER[:], PBM[:, 128:192]))
            V(_vb3)

            for ti in range(NT):
                def _vt(v, ti=ti):
                    pr = PREFB[:, ti * E:(ti + 1) * E]
                    nc.vector.tensor_tensor(FLAT[:], pr, BASER[:], ALU.add)
                    v.drain()
                    nc.vector.tensor_scalar(VBM[:], FLAT[:], float(CAP), None, ALU.is_ge)
                    nc.vector.tensor_tensor(FLAT[:], FLAT[:], EXPOFF[:], ALU.add)
                    v.drain()
                    nc.vector.tensor_scalar(TMPR[:], VBM[:], -1.0, 1.0, ALU.mult, ALU.add)
                    v.drain()
                    nc.vector.tensor_tensor(FLAT[:], FLAT[:], TMPR[:], ALU.mult)
                    v.drain()
                    nc.vector.tensor_scalar(TMPR[:], VBM[:], DUMPC[:, 0:1], None, ALU.mult)
                    v.drain()
                    nc.vector.tensor_tensor(FLAT[:], FLAT[:], TMPR[:], ALU.add)
                    v.drain()
                    for kk in range(KTOP):
                        nc.vector.tensor_scalar(OHK[:], IOTA64[:],
                                                E8ALL[:, ti * 8 + kk: ti * 8 + kk + 1],
                                                None, ALU.is_equal)
                        v.drain()
                        nc.vector.tensor_tensor(TMPR[:], FLAT[:], OHK[:], ALU.mult)
                        v.drain()
                        nc.vector.reduce_sum(SLOTS[:, ti * 8 + kk: ti * 8 + kk + 1],
                                             TMPR[:], axis=AX.X)
                        v.drain()
                    nc.vector.tensor_scalar(TOKC[:], IOTAP[:], HC[:, 48:49],
                                            float(ti * 128), ALU.add, ALU.add)
                    v.drain()
                    scv = SCV[:].rearrange("p (t k two) -> p t k two", t=NT, two=2)
                    nc.vector.tensor_copy(scv[:, ti, :, 0],
                                          TOKC[:, 0:1].to_broadcast([128, 8]))
                    s_v.inc(nc.vector.tensor_copy(scv[:, ti, :, 1],
                                                  WNALL[:, ti * 8:(ti + 1) * 8]))
                V(_vt)
            v_tbl = s_v.n

            def _vu(v):
                v.wait_ge(s_v.h, v_tbl)
                s_v.inc(nc.vector.tensor_copy(SLU32[:], SLOTS[:]))
            V(_vu)
            v_slu = s_v.n

            def _gps(gp):
                gp.wait_ge(s_v.h, v_tbl)
                gp.wait_ge(s_fill.h, gd_fill)
                for ti in range(NT):
                    for kk in range(KTOP):
                        col = ti * 8 + kk
                        s_scat.inc(nc.gpsimd.indirect_dma_start(
                            out=tbl,
                            out_offset=IndirectOffsetOnAxis(
                                SLU32[:, col:col + 1], 0),
                            in_=SCV[:, col * 2:col * 2 + 2], in_offset=None,
                            bounds_check=REGS["btbl"], oob_is_err=False), 16)
            GP(_gps)
            gd_scat = s_scat.n

            def _gpr(gp):
                gp.wait_ge(s_scat.h, gd_scat)
                s_cc.inc(nc.gpsimd.collective_compute(
                    "ReduceScatter", ALU.min, replica_groups=RG,
                    ins=[tbl[0:ZROW, :].opt()], outs=[tbl_mine.opt()]))
                s_fill.inc(gp.dma_start(out=ye_all[ZROW:ZROW + 128, :],
                                         in_=ZTILE[:]), 16)
            GP(_gpr)
        cc_tbl = s_cc.n

        barrier_except({s_cc, s_fill})
        route_stack.close()

        SE1 = e12(nc.sbuf_tensor("SE1", [128, 512], F32))
        SE2 = e12(nc.sbuf_tensor("SE2", [128, 512], F32))
        SEU = e12(nc.sbuf_tensor("SEU", [128, 512], BF16))
        if do_moe:
            XNBL = e1(nc.sbuf_tensor("XNBL", [128, D], BF16))
            SGB = e1(nc.sbuf_tensor("SGB", [128, 8 * D], BF16))
            SUBW = e1(nc.sbuf_tensor("SUBW", [128, 8 * D], BF16))
            GSB = e1(nc.sbuf_tensor("GSB", [128, 8 * S], BF16))
            XC512 = [e1(nc.sbuf_tensor(f"XC512_{i}", [128, NFT * 512], BF16))
                     for i in range(2)]

            def _gpw(gp):
                wait_all(gp, [s_v, s_a, s_pe])
                s_sg.inc(gp.dma_start(
                    out=SGB[:].rearrange("p (a c) -> p a c", a=NFT), in_=sgt_in.rearrange("(a p) c -> p a c", p=128)), 16)
                s_sg.inc(gp.dma_start(
                    out=SUBW[:].rearrange("p (a c) -> p a c", a=NFT), in_=sut_in.rearrange("(a p) c -> p a c", p=128)), 16)
            GP(_gpw)
            gd_sg = s_sg.n

            p1 = {"ev": {}, "vg": {}, "pe": {}}
            for tc in range(8):
                buf = XC512[tc % 2]
                for sub in range(4):
                    ti = tc * 4 + sub

                    def _syl(sy, ti=ti, tc=tc, sub=sub):
                        if tc >= 2 and sub == 0:
                            sy.wait_ge(s_pe.h, p1["pe"][tc - 2])
                        if ti >= 1:
                            sy.wait_ge(s_v.h, p1["ev"][ti - 1])
                        s_xnb.inc(sy.dma_start(
                            out=XNBL[:], in_=xn_rows[ti * 128:(ti + 1) * 128, :]), 16)
                    SY(_syl)
                    ld_p1 = s_xnb.n

                    def _tt(t, ti=ti, ld_p1=ld_p1):
                        t.wait_ge(s_xnb.h, ld_p1)
                        if ti >= 1:
                            t.wait_ge(s_v.h, p1["ev"][ti - 1])
                        for j in range(NFT):
                            _li = nc.tensor.transpose(PBT[:, j * 128:(j + 1) * 128],
                                                XNBL[:, j * 128:(j + 1) * 128],
                                                IDENT[:])
                        s_pe.inc(_li)
                    T(_tt)
                    pe_tt = s_pe.n

                    def _vtt(v, ti=ti, sub=sub, buf=buf, pe_tt=pe_tt):
                        v.wait_ge(s_pe.h, pe_tt)
                        for j in range(NFT):
                            _li = nc.vector.tensor_copy(
                                buf[:, j * 512 + sub * 128: j * 512 + sub * 128 + 128],
                                PBT[:, j * 128:(j + 1) * 128])
                        s_v.inc(_li)
                    V(_vtt)
                    p1["ev"][ti] = s_v.n
                v_chunk = s_v.n

                for i in range(8):
                    def _tg(t, tc=tc, i=i, buf=buf, v_chunk=v_chunk):
                        t.wait_ge(s_v.h, v_chunk)
                        if tc == 0 and i == 0:
                            t.wait_ge(s_sg.h, gd_sg)
                        if (tc, i - 1) in p1["vg"]:
                            t.wait_ge(s_v.h, p1["vg"][(tc, i - 1)])
                        elif tc >= 1:
                            t.wait_ge(s_v.h, p1["vg"][(tc - 1, 7)])
                        for j in range(NFT):
                            _li = nc.tensor.matmul(PB[0][:, 0:512],
                                             SGB[:, j * D + i * 128: j * D + (i + 1) * 128],
                                             buf[:, j * 512:(j + 1) * 512],
                                             start=(j == 0), stop=(j == NFT - 1))
                        for j in range(NFT):
                            _li = nc.tensor.matmul(PB[1][:, 0:512],
                                             SUBW[:, j * D + i * 128: j * D + (i + 1) * 128],
                                             buf[:, j * 512:(j + 1) * 512],
                                             start=(j == 0), stop=(j == NFT - 1))
                        s_pe.inc(_li)
                    T(_tg)
                    pe_g = s_pe.n

                    def _scg(sc, pe_g=pe_g):
                        sc.wait_ge(s_pe.h, pe_g)
                        nc.scalar.activation(SE1[:], PB[0][:, 0:512], AF.Exp,
                                             bias=0.0, scale=-1.0)
                        _li = nc.scalar.copy(SEU[:], PB[1][:, 0:512])
                        s_a.inc(_li)
                    SC(_scg)
                    a_g = s_a.n

                    def _vg(v, tc=tc, i=i, a_g=a_g):
                        v.wait_ge(s_a.h, a_g)
                        nc.vector.tensor_scalar(SE1[:], SE1[:], 1.0, None, ALU.add)
                        v.drain()
                        nc.vector.reciprocal(SE1[:], SE1[:])
                        v.drain()
                        nc.vector.scalar_tensor_tensor(SE2[:], PB[0][:, 0:512], 1.0,
                                                       SE1[:], ALU.mult, ALU.mult)
                        v.drain()
                        s_v.inc(nc.vector.tensor_tensor(
                            GSB[:, i * S + tc * 512: i * S + (tc + 1) * 512],
                            SE2[:], SEU[:], ALU.mult))
                    V(_vg)
                    p1["vg"][(tc, i)] = s_v.n
                p1["pe"][tc] = s_pe.n

            barrier_except({s_cc, s_fill, s_scat})

            SDB = e1(nc.sbuf_tensor("SDB", [128, 8 * D], BF16))
            PRE1 = e1(nc.sbuf_tensor("PRE1", [128, D], F32))
            XRE = [e1(nc.sbuf_tensor(f"XRE{i}", [128, D], F32)) for i in range(2)]

            def _gpd(gp):
                s_sd2.inc(gp.dma_start(
                    out=SDB[:].rearrange("p (a h) -> p a h", a=NFT), in_=sdt_in.rearrange("(a p) h -> p a h", p=128)), 16)
            GP(_gpd)
            gd_sd2 = s_sd2.n

            p1b = {"v": {}, "sd": {}}
            for ti in range(NT):
                def _syd(sy, ti=ti):
                    if ti >= 2:
                        sy.wait_ge(s_v.h, p1b["v"][ti - 2])
                    s_xre[ti % 2].inc(sy.dma_start(out=XRE[ti % 2][:],
                                       in_=x_in[ti * 128:(ti + 1) * 128, :]), 16)
                SY(_syd)
                ld_x2 = s_xre[ti % 2].n

                def _td(t, ti=ti, ld_x2=ld_x2):
                    if ti == 0:
                        t.wait_ge(s_sd2.h, gd_sd2)
                    if ti >= 1:
                        t.wait_ge(s_v.h, p1b["v"][ti - 1])
                    for hc in range(2):
                        for i in range(8):
                            _li = nc.tensor.matmul(
                                PB[2 + hc][:, 0:512],
                                GSB[:, i * S + ti * 128: i * S + (ti + 1) * 128],
                                SDB[:, i * D + hc * 512: i * D + (hc + 1) * 512],
                                start=(i == 0), stop=(i == 7))
                    s_pe.inc(_li)
                T(_td)
                pe_d2 = s_pe.n

                def _vd(v, ti=ti, pe_d2=pe_d2, ld_x2=ld_x2):
                    v.wait_ge(s_pe.h, pe_d2)
                    v.wait_ge(s_xre[ti % 2].h, ld_x2)
                    if ti >= 1:
                        v.wait_ge(s_pre.h, p1b["sd"][ti - 1])
                    nc.vector.tensor_tensor(PRE1[:, 0:512], PB[2][:, 0:512],
                                            XRE[ti % 2][:, 0:512], ALU.add)
                    s_v.inc(nc.vector.tensor_tensor(PRE1[:, 512:1024], PB[3][:, 0:512],
                                                    XRE[ti % 2][:, 512:1024], ALU.add))
                V(_vd)
                p1b["v"][ti] = s_v.n

                def _syd2(sy, ti=ti):
                    sy.wait_ge(s_v.h, p1b["v"][ti])
                    s_pre.inc(sy.dma_start(out=pre_d[ti * 128:(ti + 1) * 128, :],
                                           in_=PRE1[:]), 16)
                SY(_syd2)
                p1b["sd"][ti] = s_pre.n
        else:
            PRE1 = e1(nc.sbuf_tensor("PRE1", [128, D], F32))
            prev_sd = {}
            for ti in range(NT):
                def _syd(sy, ti=ti):
                    if ti >= 1:
                        sy.wait_ge(s_pre.h, prev_sd[ti - 1])
                    s_x[0].inc(sy.dma_start(out=PRE1[:],
                                            in_=x_in[ti * 128:(ti + 1) * 128, :]), 16)
                    sy.wait_ge(s_x[0].h, s_x[0].n)
                    s_pre.inc(sy.dma_start(out=pre_d[ti * 128:(ti + 1) * 128, :],
                                           in_=PRE1[:]), 16)
                SY(_syd)
                prev_sd[ti] = s_pre.n

        barrier_except({s_cc, s_fill, s_scat})
        p1_stack.close()

        if do_moe:
            TBLS = e2(nc.sbuf_tensor("TBLS", [128, 16], F32))
            TIDX = e2(nc.sbuf_tensor("TIDX", [128, 8], U32))
            WSC = e2(nc.sbuf_tensor("WSC", [128, 8], F32))
            XE = e2(nc.sbuf_tensor("XE", [128, 8 * D], BF16))
            XET = e2(nc.sbuf_tensor("XET", [128, 8 * CAP], BF16))
            W1B = [e2(nc.sbuf_tensor(f"W1B{i}", [128, 8 * FF], BF16)) for i in range(2)]
            W3B = [e2(nc.sbuf_tensor(f"W3B{i}", [128, 8 * FF], BF16)) for i in range(2)]
            W2B = [e2(nc.sbuf_tensor(f"W2B{i}", [128, 4 * D], BF16)) for i in range(2)]
            H1B = e2(nc.sbuf_tensor("H1B", [128, 4 * CAP], BF16))
            YES = [e2(nc.sbuf_tensor(f"YES{i}", [128, D], BF16)) for i in range(2)]

            p2 = {"pe": {}, "vy": {}, "sdy": {}, "vg2": {}, "vx": {}}
            for ee in range(EPC):
                bw = ee % 2

                def _gpw2(gp, ee=ee, bw=bw):
                    if ee >= 2:
                        gp.wait_ge(s_pe.h, p2["pe"][ee - 2])
                    s_w2[bw].inc(gp.dma_start(
                        out=W1B[bw][:].rearrange("p (a f) -> p a f", a=NFT),
                        in_=w1t_in[ee].rearrange("(a p) f -> p a f", p=128)), 16)
                    s_w2[bw].inc(gp.dma_start(
                        out=W3B[bw][:].rearrange("p (a f) -> p a f", a=NFT),
                        in_=w3t_in[ee].rearrange("(a p) f -> p a f", p=128)), 16)
                    s_w2[bw].inc(gp.dma_start(
                        out=W2B[bw][:].rearrange("p (a h) -> p a h", a=4),
                        in_=w2t_in[ee].rearrange("(a p) h -> p a h", p=128)), 16)
                GP(_gpw2)
                gd_w = s_w2[bw].n

                def _syt(sy, ee=ee):
                    if ee == 0:
                        sy.wait_ge(s_cc.h, cc_tbl)
                    else:
                        sy.wait_ge(s_v.h, p2["vx"][ee - 1])
                    s_tbl.inc(sy.dma_start(
                        out=TBLS[:].rearrange("p (a two) -> p a two", two=2),
                        in_=tbl_mine[ee * CAP:(ee + 1) * CAP, :]
                            .rearrange("(a p) two -> p a two", p=128)), 16)
                SY(_syt)
                ld_t = s_tbl.n

                def _vx(v, ee=ee, ld_t=ld_t):
                    v.wait_ge(s_tbl.h, ld_t)
                    if ee >= 1:
                        v.wait_ge(s_pe.h, p2["pe"][ee - 1])
                    tb2 = TBLS[:].rearrange("p (a two) -> p a two", two=2)
                    nc.vector.tensor_copy(TIDX[:], tb2[:, :, 0])
                    nc.vector.tensor_copy(WSC[:], tb2[:, :, 1])
                    s_v.inc(nc.vector.memset(XE[:], 0.0))
                V(_vx)
                v_xe = s_v.n
                p2["vx"][ee] = v_xe

                def _gpg(gp, ee=ee, v_xe=v_xe):
                    gp.wait_ge(s_v.h, v_xe)
                    if ee == 0:
                        gp.wait_ge(s_cc.h, cc_xn)
                    for a in range(8):
                        s_xeg.inc(nc.gpsimd.indirect_dma_start(
                            out=XE[:, a * D:(a + 1) * D], out_offset=None,
                            in_=xn_all,
                            in_offset=IndirectOffsetOnAxis(TIDX[:, a:a + 1], 0),
                            bounds_check=REGS["bxn"], oob_is_err=False), 16)
                GP(_gpg)
                gd_g = s_xeg.n

                for a in range(8):
                    def _tx(t, ee=ee, a=a, gd_g=gd_g):
                        if a == 0:
                            t.wait_ge(s_xeg.h, gd_g)
                        if (ee, a - 1) in p2.get("ev", {}):
                            t.wait_ge(s_v.h, p2["ev"][(ee, a - 1)])
                        for j in range(NFT):
                            _li = nc.tensor.transpose(
                                PBT[:, j * 128:(j + 1) * 128],
                                XE[:, a * D + j * 128: a * D + (j + 1) * 128], IDENT[:])
                        s_pe.inc(_li)
                    if ee == 0 and a == 0:
                        p2["ev"] = {}
                    T(_tx)
                    pe_x = s_pe.n

                    def _vxt(v, ee=ee, a=a, pe_x=pe_x):
                        v.wait_ge(s_pe.h, pe_x)
                        for j in range(NFT):
                            _li = nc.vector.tensor_copy(
                                XET[:, j * CAP + a * 128: j * CAP + (a + 1) * 128],
                                PBT[:, j * 128:(j + 1) * 128])
                        s_v.inc(_li)
                    V(_vxt)
                    p2["ev"][(ee, a)] = s_v.n
                v_xet = s_v.n

                for fi in range(4):
                    for nk in range(2):
                        def _tg2(t, ee=ee, fi=fi, nk=nk, bw=bw, v_xet=v_xet,
                                 gd_w=gd_w):
                            t.wait_ge(s_v.h, v_xet)
                            if fi == 0 and nk == 0:
                                t.wait_ge(s_w2[bw].h, gd_w)
                            if ee == 0 and fi == 0 and nk == 0:
                                t.wait_ge(s_w2[0].h, gd_w)
                            if ee == 1 and fi == 0 and nk == 0:
                                t.wait_ge(s_w2[1].h, s_w2[1].n)
                            prev = p2["vg2"].get((ee, fi, nk - 1),
                                                 p2["vg2"].get((ee, fi - 1, 1),
                                                               p2["vg2"].get((ee - 1, 3, 1), 0)))
                            if prev:
                                t.wait_ge(s_v.h, prev)
                            for j in range(NFT):
                                _li = nc.tensor.matmul(
                                    PB[0][:, 0:512],
                                    W1B[bw][:, j * FF + fi * 128: j * FF + (fi + 1) * 128],
                                    XET[:, j * CAP + nk * 512: j * CAP + (nk + 1) * 512],
                                    start=(j == 0), stop=(j == NFT - 1))
                            for j in range(NFT):
                                _li = nc.tensor.matmul(
                                    PB[1][:, 0:512],
                                    W3B[bw][:, j * FF + fi * 128: j * FF + (fi + 1) * 128],
                                    XET[:, j * CAP + nk * 512: j * CAP + (nk + 1) * 512],
                                    start=(j == 0), stop=(j == NFT - 1))
                            s_pe.inc(_li)
                        T(_tg2)
                        pe_g2 = s_pe.n

                        def _scg2(sc, pe_g2=pe_g2):
                            sc.wait_ge(s_pe.h, pe_g2)
                            nc.scalar.activation(SE1[:], PB[0][:, 0:512], AF.Exp,
                                                 bias=0.0, scale=-1.0)
                            _li = nc.scalar.copy(SEU[:], PB[1][:, 0:512])
                            s_a.inc(_li)
                        SC(_scg2)
                        a_g2 = s_a.n

                        def _vg2(v, ee=ee, fi=fi, nk=nk, a_g2=a_g2):
                            v.wait_ge(s_a.h, a_g2)
                            nc.vector.tensor_scalar(SE1[:], SE1[:], 1.0, None, ALU.add)
                            v.drain()
                            nc.vector.reciprocal(SE1[:], SE1[:])
                            v.drain()
                            nc.vector.scalar_tensor_tensor(SE2[:], PB[0][:, 0:512], 1.0,
                                                           SE1[:], ALU.mult, ALU.mult)
                            v.drain()
                            s_v.inc(nc.vector.tensor_tensor(
                                H1B[:, fi * CAP + nk * 512: fi * CAP + (nk + 1) * 512],
                                SE2[:], SEU[:], ALU.mult))
                        V(_vg2)
                        p2["vg2"][(ee, fi, nk)] = s_v.n
                v_h1 = s_v.n

                for a in range(8):
                    def _ty(t, ee=ee, a=a, bw=bw, v_h1=v_h1):
                        t.wait_ge(s_v.h, v_h1)
                        if (ee, a - 1) in p2["vy"]:
                            t.wait_ge(s_v.h, p2["vy"][(ee, a - 1)])
                        elif ee >= 1:
                            t.wait_ge(s_v.h, p2["vy"][(ee - 1, 7)])
                        for hc in range(2):
                            for fi in range(4):
                                _li = nc.tensor.matmul(
                                    PB[2 + hc][:, 0:512],
                                    H1B[:, fi * CAP + a * 128: fi * CAP + (a + 1) * 128],
                                    W2B[bw][:, fi * D + hc * 512: fi * D + (hc + 1) * 512],
                                    start=(fi == 0), stop=(fi == 3))
                        s_pe.inc(_li)
                    T(_ty)
                    pe_y = s_pe.n

                    def _vy(v, ee=ee, a=a, pe_y=pe_y):
                        v.wait_ge(s_pe.h, pe_y)
                        v.wait_ge(s_yew[a % 2].h, s_yew[a % 2].n)
                        ybuf = YES[a % 2]
                        nc.vector.tensor_scalar(ybuf[:, 0:512], PB[2][:, 0:512],
                                                WSC[:, a:a + 1], None, ALU.mult)
                        s_v.inc(nc.vector.tensor_scalar(ybuf[:, 512:1024],
                                                        PB[3][:, 0:512],
                                                        WSC[:, a:a + 1], None, ALU.mult))
                    V(_vy)
                    p2["vy"][(ee, a)] = s_v.n

                    def _syy(sy, ee=ee, a=a):
                        sy.wait_ge(s_v.h, p2["vy"][(ee, a)])
                        s_yew[a % 2].inc(sy.dma_start(
                            out=ye_mine[ee * CAP + a * 128: ee * CAP + (a + 1) * 128, :],
                            in_=YES[a % 2][:]), 16)
                    SY(_syy)
                    p2["sdy"][(ee, a)] = s_yew[a % 2].n
                p2["pe"][ee] = s_pe.n

            def _gpy2(gp):
                gp.wait_ge(s_yew[0].h, s_yew[0].n)
                gp.wait_ge(s_yew[1].h, s_yew[1].n)
                s_cc.inc(nc.gpsimd.collective_compute(
                    "AllGather", ALU.bypass, replica_groups=RG,
                    ins=[ye_mine.opt()], outs=[ye_all[0:ZROW, :].opt()]))
            GP(_gpy2)
        cc_ye = s_cc.n

        barrier_except({s_cc, s_fill, s_scat, s_pre})
        p2_stack.close()
        p12_stack.close()

        if do_moe:
            PREM = [e3(nc.sbuf_tensor(f"PREM{i}", [128, D], F32)) for i in range(2)]

            def emit_pre_moe(gp, tis):
                for ti in tis:
                    pm = PREM[ti % 2]
                    sl = s_prem[ti % 2]
                    gp.wait_ge(sl.h, sl.n)
                    nc.gpsimd.memset(pm[:], 0.0)
                    for kk in range(KTOP):
                        sl.inc(nc.gpsimd.indirect_dma_start(
                            out=pm[:], out_offset=None,
                            in_=ye_all,
                            in_offset=IndirectOffsetOnAxis(
                                SLU32[:, ti * 8 + kk: ti * 8 + kk + 1], 0),
                            compute_op=ALU.add, oob_is_err=True), 16)
                    sl.inc(gp.dma_start(out=pre_d[ti * 128:(ti + 1) * 128, :],
                                        in_=pm[:], accum_op=ALU.add), 16)
        else:
            def emit_pre_moe(gp, tis):
                pass

        if do_attn:
            XHB = e3(nc.sbuf_tensor("XHB", [128, NFT * S], BF16))
            WQK = [e3(nc.sbuf_tensor(f"WQK{i}", [128, 8 * 384], BF16)) for i in range(2)]
            WKA = e3(nc.sbuf_tensor("WKA", [128, 8 * 128], BF16))
            WKB = e3(nc.sbuf_tensor("WKB", [128, 8 * 128], BF16))
            QTS = e3(nc.sbuf_tensor("QTS", [128, S], BF16))
            KTS = e3(nc.sbuf_tensor("KTS", [128, S], BF16))
            VTS = e3(nc.sbuf_tensor("VTS", [128, S], BF16))
            VTP = e3(nc.sbuf_tensor("VTP", [128, (S // 128) * 129], BF16))
            PTS = e3(nc.sbuf_tensor("PTS", [128, S], BF16))
            PH3 = e3(nc.sbuf_tensor("PH3", [3, S], BF16))
            _SSB0 = e3(nc.sbuf_tensor("SSB", [128, S], F32))
            SSB = [_SSB0, _SSB0]
            EBUF = [e3(nc.sbuf_tensor(f"EBUF{i}", [128, S], BF16)) for i in range(2)]
            ETS = e3(nc.sbuf_tensor("ETS", [128, 1024], BF16))
            CMX = e3(nc.sbuf_tensor("CMX", [128, 24], F32))
            BCOL = e3(nc.sbuf_tensor("BCOL", [128, 8], F32))
            OSB = e3(nc.sbuf_tensor("OSB", [128, 128], BF16))
            GOT = e3(nc.sbuf_tensor("GOT", [128, 128], BF16))
            PSC = e3(nc.sbuf_tensor("PSC", [128, 512], F32))
            PSD = e3(nc.sbuf_tensor("PSD", [128, 512], F32))

            def _syh(sy):
                wait_all(sy, [s_v, s_a, s_pe, s_g])
                sy.wait_ge(s_xsp.h, s_xsp.n)
                for j in range(NFT):
                    s_xhb.inc(sy.dma_start(out=XHB[:, j * S:(j + 1) * S],
                                           in_=xht_sp[:, j * S:(j + 1) * S]), 16)
            SY(_syh)
            ld_xhb = s_xhb.n

            A = {"wb": {}, "ph3": {}, "qkv": {}, "pts": {}, "vt": {}, "sebank": [],
                 "ebhist": [], "tehist": [], "pv": {}, "ov": {}, "godma": [],
                 "exp": [], "osb": [], "pemoe": 0}
            pre_sched = {h: [] for h in range(NH)}
            if do_moe:
                tl = list(range(NT))
                for i3, ti in enumerate(tl):
                    pre_sched[9 + (i3 % (NH - 9))].append(ti)

            for h in range(NH):
                hb = h % 2

                def _syw(gp, h=h, hb=hb):
                    if h >= 2:
                        gp.wait_ge(s_wb.h, A["wb"][h - 2])
                    s_wq[hb].inc(gp.dma_start(
                        out=WQK[hb][:].rearrange("p (a d) -> p a d", a=NFT),
                        in_=wqkv[h].rearrange("(a p) d -> p a d", p=128)), 16)
                GP(_syw)
                sd_w = s_wq[hb].n

                def _gph(gp, h=h):
                    gp.wait_ge(s_v.h, v_pos3)
                    if do_moe and pre_sched[h] and not A["pemoe"]:
                        A["pemoe"] = 1
                        gp.wait_ge(s_cc.h, cc_ye)
                        gp.wait_ge(s_pre.h, s_pre.n)
                        gp.wait_ge(s_fill.h, s_fill.n)
                    if h >= 1:
                        gp.wait_ge(s_pe.h, A["ph3"][h - 1])
                    s_ph3.inc(gp.dma_start(out=PH3[:], in_=POS3B[h::32, :][0:3, :]), 16)
                    emit_pre_moe(gp, pre_sched[h])
                GP(_gph)
                gd_ph3 = s_ph3.n

                def _tb3(t, h=h, hb=hb, sd_w=sd_w):
                    t.wait_ge(s_wq[hb].h, sd_w)
                    if h == 0:
                        t.wait_ge(s_xhb.h, ld_xhb)
                    for r in range(3):
                        for j in range(NFT):
                            _li = nc.tensor.matmul(
                                PBM[:, r:r + 1],
                                WQK[hb][:, j * 384 + r * 128: j * 384 + (r + 1) * 128],
                                LNBB[:, j:j + 1],
                                start=(j == 0), stop=(j == NFT - 1))
                    s_pe.inc(_li)
                T(_tb3)
                pe_b = s_pe.n

                def _vw(v, h=h, hb=hb, pe_b=pe_b):
                    v.wait_ge(s_pe.h, pe_b)
                    v.wait_ge(s_a.h, s_a.n)
                    nc.vector.tensor_scalar(BCOL[:, 0:1], PBM[:, 0:1],
                                            BQKV[:, h * 3:h * 3 + 1],
                                            HC[:, 16 + h:17 + h], ALU.add, ALU.mult)
                    nc.vector.tensor_scalar(BCOL[:, 1:2], PBM[:, 1:2],
                                            BQKV[:, h * 3 + 1:h * 3 + 2], None, ALU.add)
                    nc.vector.tensor_scalar(BCOL[:, 2:3], PBM[:, 2:3],
                                            BQKV[:, h * 3 + 2:h * 3 + 3], None, ALU.add)
                    for j in range(NFT):
                        q0 = WQK[hb][:, j * 384:j * 384 + 128]
                        k0 = WQK[hb][:, j * 384 + 128:j * 384 + 256]
                        v0 = WQK[hb][:, j * 384 + 256:j * 384 + 384]
                        nc.vector.tensor_scalar(q0, q0, LNW[:, j:j + 1],
                                                HC[:, 16 + h:17 + h], ALU.mult, ALU.mult)
                        nc.vector.tensor_scalar(WKA[:, j * 128:(j + 1) * 128], k0,
                                                LNW[:, j:j + 1], HC[:, 32 + h:33 + h],
                                                ALU.mult, ALU.mult)
                        nc.vector.tensor_scalar(WKB[:, j * 128:(j + 1) * 128], k0,
                                                LNW[:, j:j + 1], HC[:, h:h + 1],
                                                ALU.mult, ALU.mult)
                        _li = nc.vector.tensor_scalar(v0, v0, LNW[:, j:j + 1], None, ALU.mult)
                    s_v.inc(_li)
                V(_vw)
                v_w = s_v.n

                for tc in range(8):
                    def _tq(t, h=h, hb=hb, tc=tc, v_w=v_w):
                        t.wait_ge(s_v.h, v_w)
                        if (h, tc - 1) in A["qkv"]:
                            t.wait_ge(s_a.h, A["qkv"][(h, tc - 1)])
                        for j in range(NFT):
                            _li = nc.tensor.matmul(PB[0][:, 0:512],
                                             WQK[hb][:, j * 384:j * 384 + 128],
                                             XHB[:, j * S + tc * 512: j * S + (tc + 1) * 512],
                                             start=(j == 0), stop=(j == NFT - 1))
                        for j in range(NFT):
                            _li = nc.tensor.matmul(PB[1][:, 0:512],
                                             WKA[:, j * 128:(j + 1) * 128],
                                             XHB[:, j * S + tc * 512: j * S + (tc + 1) * 512],
                                             start=(j == 0), stop=False)
                        for j in range(NFT):
                            if tc > 0:
                                _li = nc.tensor.matmul(
                                    PB[1][:, 0:512], WKB[:, j * 128:(j + 1) * 128],
                                    XHB[:, j * S + tc * 512 - 1: j * S + (tc + 1) * 512 - 1],
                                    start=False, stop=(j == NFT - 1))
                            else:
                                _li = nc.tensor.matmul(
                                    PB[1][:, 1:512], WKB[:, j * 128:(j + 1) * 128],
                                    XHB[:, j * S: j * S + 511],
                                    start=False, stop=(j == NFT - 1))
                        for j in range(NFT):
                            _li = nc.tensor.matmul(PB[2][:, 0:512],
                                             WQK[hb][:, j * 384 + 256: j * 384 + 384],
                                             XHB[:, j * S + tc * 512: j * S + (tc + 1) * 512],
                                             start=(j == 0), stop=(j == NFT - 1))
                        s_pe.inc(_li)
                    T(_tq)
                    pe_q = s_pe.n

                    def _sce(sc, h=h, tc=tc, pe_q=pe_q):
                        sc.wait_ge(s_pe.h, pe_q)
                        nc.scalar.activation(QTS[:, tc * 512:(tc + 1) * 512],
                                             PB[0][:, 0:512], AF.Identity,
                                             bias=BCOL[:, 0:1])
                        nc.scalar.activation(KTS[:, tc * 512:(tc + 1) * 512],
                                             PB[1][:, 0:512], AF.Identity,
                                             bias=BCOL[:, 1:2])
                        _li = nc.scalar.activation(VTS[:, tc * 512:(tc + 1) * 512],
                                                   PB[2][:, 0:512], AF.Identity,
                                                   bias=BCOL[:, 2:3])
                        s_a.inc(_li)
                    SC(_sce)
                    A["qkv"][(h, tc)] = s_a.n
                a_qkv = s_a.n

                def _vk(v, h=h, a_qkv=a_qkv):
                    v.wait_ge(s_a.h, a_qkv)
                    nc.vector.tensor_scalar(PSC[:, 0:1], BCOL[:, 1:2],
                                            HC[:, h:h + 1], None, ALU.mult)
                    v.drain()
                    s_v.inc(nc.vector.tensor_tensor(KTS[:, 0:1], KTS[:, 0:1],
                                                    PSC[:, 0:1], ALU.subtract))
                V(_vk)
                v_kfix = s_v.n

                if cfg.get("dbg_attn") and h == 0:
                    def _dbga(sy, v_kfix=v_kfix):
                        sy.wait_ge(s_v.h, v_kfix)
                        for i in range(4):
                            s_st.inc(sy.dma_start(
                                out=dbg_att[i * 128:(i + 1) * 128, :],
                                in_=QTS[:, i * 1024:(i + 1) * 1024]), 16)
                            s_st.inc(sy.dma_start(
                                out=dbg_att[(4 + i) * 128:(5 + i) * 128, :],
                                in_=KTS[:, i * 1024:(i + 1) * 1024]), 16)
                            s_st.inc(sy.dma_start(
                                out=dbg_att[(8 + i) * 128:(9 + i) * 128, :],
                                in_=VTS[:, i * 1024:(i + 1) * 1024]), 16)
                            s_st.inc(sy.dma_start(
                                out=dbg_att[(16 + i) * 128:(16 + i) * 128 + 80, :],
                                in_=POS3B[:, i * 1024:(i + 1) * 1024]), 16)
                    SY(_dbga)

                WPB = WKA

                def _syp(gp, h=h, v_kfix=v_kfix):
                    gp.wait_ge(s_v.h, v_kfix)
                    gp.wait_ge(s_pe.h, s_pe.n)
                    s_wp.inc(gp.dma_start(
                        out=WPB[:].rearrange("p (a d) -> p a d", a=NFT),
                        in_=wp_in[h].rearrange("(a p) d -> p a d", p=128)), 16)
                GP(_syp)
                sd_wp = s_wp.n

                def _tb5(t, h=h, sd_wp=sd_wp):
                    t.wait_ge(s_wp.h, sd_wp)
                    for j in range(NFT):
                        _li = nc.tensor.matmul(PBM[:, 4:5], WPB[:, j * 128:(j + 1) * 128],
                                         LNBB[:, j:j + 1],
                                         start=(j == 0), stop=(j == NFT - 1))
                    s_pe.inc(_li)
                T(_tb5)
                pe_bp = s_pe.n

                def _vwp(v, h=h, pe_bp=pe_bp):
                    v.wait_ge(s_pe.h, pe_bp)
                    nc.vector.tensor_scalar(BCOL[:, 3:4], PBM[:, 4:5],
                                            BPS[:, h:h + 1], None, ALU.add)
                    v.drain()
                    nc.vector.tensor_scalar_mul(BCOL[:, 4:5], BCOL[:, 3:4], -1.0)
                    for j in range(NFT):
                        w0 = WPB[:, j * 128:(j + 1) * 128]
                        _li = nc.vector.tensor_scalar(w0, w0, LNW[:, j:j + 1], None, ALU.mult)
                    s_v.inc(_li)
                V(_vwp)
                v_wp = s_v.n

                for tc in range(8):
                    def _tp2(t, h=h, tc=tc, v_wp=v_wp):
                        t.wait_ge(s_v.h, v_wp)
                        if (h, tc - 1) in A["pts"]:
                            t.wait_ge(s_v.h, A["pts"][(h, tc - 1)])
                        for j in range(NFT):
                            _li = nc.tensor.matmul(PB[3][:, 0:512],
                                             WPB[:, j * 128:(j + 1) * 128],
                                             XHB[:, j * S + tc * 512: j * S + (tc + 1) * 512],
                                             start=(j == 0), stop=(j == NFT - 1))
                        s_pe.inc(_li)
                    T(_tp2)
                    pe_p = s_pe.n

                    def _scp(sc, h=h, tc=tc, pe_p=pe_p):
                        sc.wait_ge(s_pe.h, pe_p)
                        _li = nc.scalar.activation(PSD[:], PB[3][:, 0:512], AF.Exp,
                                                   bias=BCOL[:, 4:5], scale=-1.0)
                        s_a.inc(_li)
                    SC(_scp)
                    a_p = s_a.n

                    def _vp3(v, h=h, tc=tc, a_p=a_p):
                        v.wait_ge(s_a.h, a_p)
                        nc.vector.tensor_scalar(PSC[:], PB[3][:, 0:512],
                                                BCOL[:, 3:4], None, ALU.add)
                        nc.vector.tensor_scalar(PSD[:], PSD[:], 1.0, None, ALU.add)
                        v.drain()
                        nc.vector.reciprocal(PSD[:], PSD[:])
                        v.drain()
                        s_v.inc(nc.vector.tensor_tensor(
                            PTS[:, tc * 512:(tc + 1) * 512], PSC[:], PSD[:], ALU.mult))
                    V(_vp3)
                    A["pts"][(h, tc)] = s_v.n

                if cfg.get("dbg_attn") and h == 0:
                    def _dbgp(sy, pts7=A["pts"][(0, 7)]):
                        sy.wait_ge(s_v.h, pts7)
                        for i in range(4):
                            s_st.inc(sy.dma_start(
                                out=dbg_att[(12 + i) * 128:(13 + i) * 128, :],
                                in_=PTS[:, i * 1024:(i + 1) * 1024]), 16)
                    SY(_dbgp)

                def _vv(v, h=h, a_qkv=a_qkv):
                    v.wait_ge(s_a.h, a_qkv)
                    s_v.inc(nc.vector.memset(VTP[:], 1.0))
                V(_vv)
                v_vtp0 = s_v.n

                for grp in range(4):
                    def _tv(t, h=h, grp=grp, v_vtp0=v_vtp0):
                        t.wait_ge(s_v.h, v_vtp0)
                        if h >= 1 and grp == 0:
                            t.wait_ge(s_go.h, A["go_prev"])
                        if (h, grp - 1) in A["vt"]:
                            t.wait_ge(s_v.h, A["vt"][(h, grp - 1)])
                        for j in range(8):
                            kb = grp * 8 + j
                            _li = nc.tensor.transpose(
                                PBT2[:, j * 128:(j + 1) * 128],
                                VTS[:, kb * 128:(kb + 1) * 128], IDENT[:])
                        s_pe.inc(_li)
                    if h == 0 and grp == 0:
                        A["go_prev"] = 0
                    T(_tv)
                    pe_v = s_pe.n

                    def _vtv(v, h=h, grp=grp, pe_v=pe_v):
                        v.wait_ge(s_pe.h, pe_v)
                        for j in range(8):
                            kb = grp * 8 + j
                            nc.vector.tensor_copy(VTP[:, kb * 129: kb * 129 + 128],
                                                  PBT2[:, j * 128:(j + 1) * 128])
                        s_v.inc(v.drain())
                    V(_vtv)
                    A["vt"][(h, grp)] = s_v.n
                v_vtp = s_v.n

                ebq = {}

                def emit_sprime(qt, h=h, v_vtp=v_vtp, gd_ph3=gd_ph3):
                    L = (qt + 1) * 128
                    nch = _ceil(L, 512)
                    ebuf = EBUF[qt % 2]
                    ssb = SSB[qt % 2]
                    base = (qt % 2) * 8

                    for cc in range(nch):
                        cw = min(512, L - cc * 512)
                        gidx = len(A["sebank"])
                        bank = PB[gidx % 4]

                        def _ts(t, qt=qt, cc=cc, cw=cw, bank=bank, gidx=gidx,
                                L=L):
                            if qt == 0 and cc == 0:
                                t.wait_ge(s_ph3.h, gd_ph3)
                                t.wait_ge(s_v.h, v_vtp)
                                t.wait_ge(s_v.h, A["pts"][(h, 7)])
                            if gidx >= 4:
                                t.wait_ge(s_se.h, A["sebank"][gidx - 4])
                            _li = nc.tensor.matmul(bank[:, 0:cw], ONES3B[:, 0:128],
                                             PH3[:, cc * 512: cc * 512 + cw],
                                             start=True, stop=False)
                            if cc == (L - 1) // 512:
                                _li = nc.tensor.matmul(bank[:, cw - 128:cw],
                                                 IDENT[:], UTNEG[:],
                                                 start=False, stop=False)
                            s_sp.inc(nc.tensor.matmul(
                                bank[:, 0:cw], QTS[:, qt * 128:(qt + 1) * 128],
                                KTS[:, cc * 512: cc * 512 + cw],
                                start=False, stop=True))
                        T(_ts)
                        sp_here = s_sp.n

                        def _vs(v, qt=qt, cc=cc, cw=cw, bank=bank,
                                sp_here=sp_here, ssb=ssb, base=base):
                            v.wait_ge(s_sp.h, sp_here)
                            if cc == 0 and A["exp"]:
                                v.wait_ge(s_eb.h, A["exp"][-1])
                            s_se.inc(nc.vector.tensor_scalar(
                                ssb[:, cc * 512: cc * 512 + cw], bank[:, 0:cw],
                                0.0, None, ALU.add, ALU.max,
                                accum_out=CMX[:, base + cc:base + cc + 1]))
                        V(_vs)
                        A["sebank"].append(s_se.n)
                    se_all = s_se.n

                    def _vm(v, qt=qt, nch=nch, se_all=se_all, base=base):
                        v.wait_ge(s_se.h, se_all)
                        s_v.inc(nc.vector.reduce_max(
                            CMX[:, 16 + qt % 2:17 + qt % 2],
                            CMX[:, base:base + nch], axis=AX.X, negate=True))
                    V(_vm)
                    v_mx = s_v.n

                    if cfg.get("dbg_attn") and h == 0 and qt == NT - 1:
                        def _dbgs(sy, v_mx=v_mx, ssb=ssb):
                            sy.wait_ge(s_v.h, v_mx)
                            for i in range(4):
                                s_st.inc(sy.dma_start(
                                    out=dbg_f32[i * 128:(i + 1) * 128, :],
                                    in_=ssb[:, i * 1024:(i + 1) * 1024]), 16)
                            s_st.inc(sy.dma_start(
                                out=dbg_f32[4 * 128:5 * 128, 0:18],
                                in_=CMX[:, 0:18]), 16)
                        SY(_dbgs)

                    def _se2(sc, qt=qt, L=L, ebuf=ebuf, v_mx=v_mx, ssb=ssb):
                        sc.wait_ge(s_v.h, v_mx)
                        if A["ebhist"]:
                            sc.wait_ge(s_pv.h, A["ebhist"][-1])
                        _li = nc.scalar.activation(ebuf[:, 0:L], ssb[:, 0:L],
                                                   AF.Exp,
                                                   bias=CMX[:, 16 + qt % 2:17 + qt % 2])
                        s_eb.inc(_li)
                    SC(_se2)
                    ebq[qt] = s_eb.n
                    A["exp"].append(s_eb.n)

                def emit_attend(qt, h=h):
                    L = (qt + 1) * 128
                    ebuf = EBUF[qt % 2]
                    eb_here = ebq[qt]
                    nkb = L // 128
                    for grp in range(_ceil(nkb, 8)):
                        nblk = min(8, nkb - grp * 8)

                        def _tt2(t, qt=qt, grp=grp, nblk=nblk, ebuf=ebuf,
                                 eb_here=eb_here):
                            t.wait_ge(s_eb.h, eb_here)
                            if A["tehist"]:
                                t.wait_ge(s_te.h, A["tehist"][-1])
                            for j in range(nblk):
                                kb = grp * 8 + j
                                _li = nc.tensor.transpose(
                                    PBT[:, j * 128:(j + 1) * 128],
                                    ebuf[:, kb * 128:(kb + 1) * 128], IDENT[:])
                            s_tp.inc(_li)
                        T(_tt2)
                        tp_here = s_tp.n

                        def _ve(v, nblk=nblk, tp_here=tp_here):
                            v.wait_ge(s_tp.h, tp_here)
                            s_te.inc(nc.vector.tensor_copy(
                                ETS[:, 0:nblk * 128], PBT[:, 0:nblk * 128]))
                        V(_ve)
                        te_here = s_te.n
                        A["tehist"].append(te_here)

                        def _tpv(t, qt=qt, grp=grp, nblk=nblk, te_here=te_here,
                                 nkb=nkb):
                            t.wait_ge(s_te.h, te_here)
                            if grp == 0 and A["osb"]:
                                t.wait_ge(s_ov.h, A["osb"][-1])
                            for j in range(nblk):
                                kb = grp * 8 + j
                                _li = nc.tensor.matmul(
                                    PBO[:, 0:129], ETS[:, j * 128:(j + 1) * 128],
                                    VTP[:, kb * 129:(kb + 1) * 129],
                                    start=(kb == 0), stop=(kb == nkb - 1))
                            s_pv.inc(_li)
                        T(_tpv)
                    pv_here = s_pv.n
                    A["ebhist"].append(pv_here)

                    def _vo(v, qt=qt, pv_here=pv_here):
                        v.wait_ge(s_pv.h, pv_here)
                        if A["osb"]:
                            v.wait_ge(s_pe.h, A["osb_pe"])
                        nc.vector.reciprocal(CMX[:, 18:19], PBO[:, 128:129])
                        v.drain()
                        _li = nc.vector.tensor_scalar(OSB[:], PBO[:, 0:128],
                                                      CMX[:, 18:19], None, ALU.mult)
                        s_ov.inc(_li)
                    V(_vo)
                    A["osb"].append(s_ov.n)
                    v_osb = s_ov.n

                    def _to(t, qt=qt, v_osb=v_osb):
                        t.wait_ge(s_ov.h, v_osb)
                        _li = nc.tensor.transpose(PBT2[:, 896:1024], OSB[:], IDENT[:])
                        s_pe.inc(_li)
                    T(_to)
                    A["osb_pe"] = s_pe.n
                    pe_ot = s_pe.n

                    def _vg3(v, qt=qt, pe_ot=pe_ot):
                        v.wait_ge(s_pe.h, pe_ot)
                        if A["godma"]:
                            v.wait_ge(s_got.h, A["godma"][-1])
                        _li = nc.vector.tensor_tensor(
                            GOT[:], PBT2[:, 896:1024],
                            PTS[:, qt * 128:(qt + 1) * 128], ALU.mult)
                        s_go.inc(_li)
                    V(_vg3)
                    go_here = s_go.n

                    def _syg2(sy, qt=qt, go_here=go_here):
                        sy.wait_ge(s_go.h, go_here)
                        s_got.inc(sy.dma_start(
                            out=gots[qt, :, h * 128:(h + 1) * 128], in_=GOT[:]), 16)
                    SY(_syg2)
                    A["godma"].append(s_got.n)

                for qq in range(NT + 1):
                    if qq < NT:
                        emit_sprime(qq)
                    if qq >= 1:
                        emit_attend(qq - 1)

                A["go_prev"] = s_go.n
                A["ph3"][h] = s_pe.n

                def _vwb(v, h=h):
                    last_pv = A["ebhist"][-1]
                    v.wait_ge(s_pv.h, last_pv)
                    s_wb.inc(nc.vector.engine_nop())
                V(_vwb)
                A["wb"][h] = s_wb.n
        else:
            def _gpall(gp):
                wait_all(gp)
                emit_pre_moe(gp, list(range(NT)))
            GP(_gpall)
        gd_pre = (s_prem[0].n, s_prem[1].n)

        barrier()
        p3_stack.close()

        WOB = e4(nc.sbuf_tensor("WOB", [128, NH * D], BF16))
        GHB = [e4(nc.sbuf_tensor(f"GHB{i}", [128, NH * 128], BF16)) for i in range(2)]
        ARB = e4(nc.sbuf_tensor("ARB", [128, D], F32))
        PREL = [e4(nc.sbuf_tensor(f"PREL{i}", [128, D], F32)) for i in range(2)]
        OUTT = e4(nc.sbuf_tensor("OUTT", [128, D], F32))
        ST4 = e4(nc.sbuf_tensor("ST4", [128, 8], F32))

        def _gpo(gp):
            s_wo.inc(gp.dma_start(
                out=WOB[:].rearrange("p (hh d) -> p hh d", hh=NH), in_=wout_in.rearrange("(hh p) d -> p hh d", p=128)), 16)
        GP(_gpo)
        gd_wo = s_wo.n

        p4 = {"v": {}, "st": {}}
        for ti in range(NT):
            tb = ti % 2

            def _syo(sy, ti=ti, tb=tb):
                if ti == 0:
                    if do_moe:
                        sy.wait_ge(s_prem[0].h, gd_pre[0])
                        sy.wait_ge(s_prem[1].h, gd_pre[1])
                    sy.wait_ge(s_pre.h, s_pre.n)
                    sy.wait_ge(s_got.h, s_got.n)
                if ti >= 2:
                    sy.wait_ge(s_v.h, p4["v"][ti - 2])
                if do_attn:
                    s_p4l[tb].inc(sy.dma_start(out=GHB[tb][:], in_=gots[ti]), 16)
                s_p4l[tb].inc(sy.dma_start(out=PREL[tb][:],
                                           in_=pre_d[ti * 128:(ti + 1) * 128, :]), 16)
            SY(_syo)
            ld_o = s_p4l[tb].n

            def _t4(t, ti=ti, tb=tb, ld_o=ld_o):
                t.wait_ge(s_p4l[tb].h, ld_o)
                if ti == 0:
                    t.wait_ge(s_wo.h, gd_wo)
                if ti >= 1:
                    t.wait_ge(s_a.h, p4["st"][ti - 1])
                if do_attn:
                    for hh in range(NH):
                        for fc in range(2):
                            _li = nc.tensor.matmul(
                                PB[fc][:, 0:512],
                                GHB[tb][:, hh * 128:(hh + 1) * 128],
                                WOB[:, hh * D + fc * 512: hh * D + (fc + 1) * 512],
                                start=(hh == 0), stop=(hh == NH - 1))
                    s_pe.inc(_li)
                else:
                    s_pe.inc(t.nop())
            T(_t4)
            pe_o = s_pe.n

            def _sc5(sc, ti=ti, pe_o=pe_o):
                sc.wait_ge(s_pe.h, pe_o)
                if do_attn:
                    nc.scalar.copy(ARB[:, 0:512], PB[0][:, 0:512])
                    _li = nc.scalar.copy(ARB[:, 512:1024], PB[1][:, 0:512])
                    s_a.inc(_li)
                else:
                    _li = nc.scalar.copy(ARB[:], ZTILE[:])
                    s_a.inc(_li)
            SC(_sc5)
            p4["st"][ti] = s_a.n

            def _v8(v, ti=ti):
                v.wait_ge(s_a.h, s_a.n)
                nc.vector.tensor_tensor(OUTT[:], ARB[:], ARB[:], ALU.mult)
                nc.vector.reduce_sum(ST4[:, 0:1], ARB[:], axis=AX.X)
                v.drain()
                nc.vector.reduce_sum(ST4[:, 1:2], OUTT[:], axis=AX.X)
                nc.vector.tensor_scalar_mul(ST4[:, 0:1], ST4[:, 0:1], 1.0 / D)
                v.drain()
                nc.vector.tensor_scalar_mul(ST4[:, 1:2], ST4[:, 1:2], 1.0 / D)
                nc.vector.tensor_tensor(ST4[:, 2:3], ST4[:, 0:1], ST4[:, 0:1], ALU.mult)
                v.drain()
                s_v.inc(nc.vector.tensor_tensor(ST4[:, 1:2], ST4[:, 1:2], ST4[:, 2:3],
                                                ALU.subtract))
            V(_v8)

            def _sc6(sc, ti=ti):
                sc.wait_ge(s_v.h, s_v.n)
                nc.scalar.activation(ST4[:, 2:3], ST4[:, 1:2], AF.Ln, bias=EPSC[:, 0:1])
                sc.drain()
                _li = nc.scalar.activation(ST4[:, 2:3], ST4[:, 2:3], AF.Exp,
                                           bias=0.0, scale=-0.5)
                s_a.inc(_li)
            SC(_sc6)

            def _v9(v, ti=ti):
                v.wait_ge(s_a.h, s_a.n)
                nc.vector.tensor_tensor(ST4[:, 3:4], ST4[:, 0:1], ST4[:, 2:3], ALU.mult)
                v.drain()
                s_v.inc(nc.vector.tensor_scalar_mul(ST4[:, 3:4], ST4[:, 3:4], -1.0))
            V(_v9)

            def _sc7(sc, ti=ti):
                sc.wait_ge(s_v.h, s_v.n)
                _li = nc.scalar.activation(OUTT[:], ARB[:], AF.Identity,
                                           bias=ST4[:, 3:4], scale=ST4[:, 2:3])
                s_a.inc(_li)
            SC(_sc7)
            a_n = s_a.n

            def _v10(v, ti=ti, tb=tb, a_n=a_n, ld_o=ld_o):
                v.wait_ge(s_a.h, a_n)
                v.wait_ge(s_p4l[tb].h, ld_o)
                nc.vector.tensor_tensor(OUTT[:], OUTT[:], OLWR[:], ALU.mult)
                nc.vector.tensor_tensor(OUTT[:], OUTT[:], OLBR[:], ALU.add)
                s_v.inc(nc.vector.tensor_tensor(OUTT[:], OUTT[:], PREL[tb][:], ALU.add))
            V(_v10)
            p4["v"][ti] = s_v.n
            v_out = s_v.n

            def _syo2(sy, ti=ti, v_out=v_out):
                sy.wait_ge(s_v.h, v_out)
                s_st.inc(sy.dma_start(out=out[ti * 128:(ti + 1) * 128, :],
                                      in_=OUTT[:]), 16)
            SY(_syo2)

        def _syz(sy):
            sy.wait_ge(s_st.h, s_st.n)
        SY(_syz)

        p4_stack.close()

    return nc


_NC_CACHE = {}


def _get_nc(key, cfg):
    if key not in _NC_CACHE:
        _NC_CACHE[key] = build_nc(cfg)
    return _NC_CACHE[key]


def make_in_maps(inputs):
    A = np.ascontiguousarray
    W_in = np.asarray(inputs["L0_attn_W_in"], np.float32)
    W_out = np.asarray(inputs["L0_attn_W_out"], np.float32)
    x = np.asarray(inputs["L0_attn_x"], np.float32)
    wqkv = np.empty((NH, D, 384), np.float32)
    for h in range(NH):
        for r in range(3):
            wqkv[h, :, r * 128:(r + 1) * 128] = W_in[r * DEXP + h * DH:
                                                     r * DEXP + (h + 1) * DH, :].T
    wp = np.empty((NH, D, DH), np.float32)
    for h in range(NH):
        wp[h] = W_in[3 * DEXP + h * DH: 3 * DEXP + (h + 1) * DH, :].T
    wy = A(W_in[4 * DEXP:4 * DEXP + NH, :].T)
    b_in = np.asarray(inputs["L0_attn_b_in"], np.float32)
    bqkv = np.empty((128, 48), np.float32)
    for h in range(NH):
        for r in range(3):
            bqkv[:, h * 3 + r] = b_in[r * DEXP + h * DH: r * DEXP + (h + 1) * DH]
    bp = np.empty((128, NH), np.float32)
    for h in range(NH):
        bp[:, h] = b_in[3 * DEXP + h * DH: 3 * DEXP + (h + 1) * DH]
    by = A(b_in[4 * DEXP:4 * DEXP + NH].reshape(NH, 1))
    lnw8 = A(np.asarray(inputs["L0_attn_in_ln_w"], np.float32).reshape(8, 128).T)
    lnb8 = A(np.asarray(inputs["L0_attn_in_ln_b"], np.float32).reshape(8, 128).T)
    olnw = A(np.asarray(inputs["L0_attn_out_ln_w"], np.float32).reshape(1, D))
    olnb = A(np.asarray(inputs["L0_attn_out_ln_b"], np.float32).reshape(1, D))
    smear = A(np.asarray(inputs["L0_attn_smear_factor"], np.float32).reshape(1, NH))
    lsc = A(np.asarray(inputs["L0_attn_log_scale"], np.float32).reshape(1, NH))
    wout = A(W_out.T)
    gwt = A(np.asarray(inputs["L0_ffn_gate_weight"], np.float32).T)
    ebias = A(np.asarray(inputs["L0_ffn_e_bias"], np.float32).reshape(1, E))
    sgt = A(np.asarray(inputs["L0_ffn_shared_gate"], np.float32).T)
    sut = A(np.asarray(inputs["L0_ffn_shared_up"], np.float32).T)
    sdt = A(np.asarray(inputs["L0_ffn_shared_down"], np.float32).T)
    w1 = np.asarray(inputs["L0_ffn_w1"], np.float32)
    w2 = np.asarray(inputs["L0_ffn_w2"], np.float32)
    w3 = np.asarray(inputs["L0_ffn_w3"], np.float32)

    in_maps = []
    for c in range(NCORE):
        rmask = np.zeros((8, 1), np.float32)
        rmask[:c] = 1.0
        cbase = np.array([[c * S, c]], np.float32)
        in_maps.append({
            "x": A(x[c]),
            "wqkv": wqkv, "wp": wp, "wy": wy, "bqkv": bqkv, "bp": bp, "by": by,
            "wout": wout, "lnw8": lnw8, "lnb8": lnb8, "olnw": olnw, "olnb": olnb,
            "smear": smear, "lsc": lsc, "ebias": ebias, "gwt": gwt,
            "sgt": sgt, "sut": sut, "sdt": sdt,
            "w1t": A(w1[c * EPC:(c + 1) * EPC].transpose(0, 2, 1)),
            "w3t": A(w3[c * EPC:(c + 1) * EPC].transpose(0, 2, 1)),
            "w2t": A(w2[c * EPC:(c + 1) * EPC].transpose(0, 2, 1)),
            "rmask": rmask, "cbase": cbase,
        })
    return in_maps


def kernel(**inputs):
    cfg = {"attn": True, "moe": True}
    nc = _get_nc(("full",), cfg)
    in_maps = make_in_maps(inputs)
    res = run_bass_kernel_spmd(nc, in_maps, core_ids=list(range(NCORE)))
    return np.stack([res.results[c]["out"] for c in range(NCORE)], axis=0)
